# version 03
# baseline (speedup 1.0000x reference)
import numpy as np
import ml_dtypes
from contextlib import ExitStack
import concourse.bass as bass
import concourse.mybir as mybir
from concourse.bass_utils import run_bass_kernel_spmd

F32 = mybir.dt.float32
BF16 = mybir.dt.bfloat16
AF = mybir.ActivationFunctionType
ALU = mybir.AluOpType

T = 2048
C = 1024
BF = ml_dtypes.bfloat16


class Waves:

    def __init__(self):
        self.items = []

    def wave(self, ceng, tfn, cfn, lag=0):
        self.items.append((tfn, ceng, cfn, lag))


def build_l0front():
    nc = bass.Bass("TRN2", target_bir_lowering=False)
    io = dict(
        xT=nc.dram_tensor("xT", [C, T + 1], F32, kind="ExternalInput"),
        maas=nc.dram_tensor("maas", [128, 7 * 8], F32, kind="ExternalInput"),
        maa_w1=nc.dram_tensor("maa_w1", [C, 160], F32, kind="ExternalInput"),
        maa_w2=nc.dram_tensor("maa_w2", [5, 32, C], F32, kind="ExternalInput"),
        Wk=nc.dram_tensor("Wk", [C, C], F32, kind="ExternalInput"),
        Wv=nc.dram_tensor("Wv", [C, C], F32, kind="ExternalInput"),
        Wr=nc.dram_tensor("Wr", [C, C], F32, kind="ExternalInput"),
        gate_w1=nc.dram_tensor("gate_w1", [C, 64], F32, kind="ExternalInput"),
        gate_w2=nc.dram_tensor("gate_w2", [64, C], F32, kind="ExternalInput"),
        decay_w1=nc.dram_tensor("decay_w1", [C, 64], F32, kind="ExternalInput"),
        decay_w2=nc.dram_tensor("decay_w2", [64, C], F32, kind="ExternalInput"),
        rT_o=nc.dram_tensor("rT_o", [C, T], BF16, kind="ExternalOutput"),
        kT_o=nc.dram_tensor("kT_o", [C, T], BF16, kind="ExternalOutput"),
        v_o=nc.dram_tensor("v_o", [T, C], BF16, kind="ExternalOutput"),
        wrawT_o=nc.dram_tensor("wrawT_o", [C, T], F32, kind="ExternalOutput"),
        g_o=nc.dram_tensor("g_o", [T, C], BF16, kind="ExternalOutput"),
    )
    with ExitStack() as ES:
        stgs = ES.enter_context(nc.semaphore("stgs"))
        build_l0front_body(nc, ES, io, "", stgs, 0)
    return nc


def build_l0front_body(nc, ES, io, px, stg, kst):
    xT_d, maas_d, mw1_d, mw2_d = io["xT"], io["maas"], io["maa_w1"], io["maa_w2"]
    Wk_d, Wv_d, Wr_d = io["Wk"], io["Wv"], io["Wr"]
    gw1_d, gw2_d, dw1_d, dw2_d = (io["gate_w1"], io["gate_w2"],
                                  io["decay_w1"], io["decay_w2"])
    rT_o, kT_o, v_o = io["rT_o"], io["kT_o"], io["v_o"]
    wrawT_o, g_o = io["wrawT_o"], io["g_o"]

    def S(n):
        return ES.enter_context(nc.semaphore(px + n))

    ctx = ExitStack()
    with ctx:
        E = ctx.enter_context
        xTs = E(nc.sbuf_tensor(px + "xTs", [128, 8, T + 1], BF16))
        xxs = E(nc.sbuf_tensor(px + "xxs", [128, 8, T], BF16))
        cur = E(nc.sbuf_tensor(px + "cur", [128, 8, T], BF16))
        loraT = E(nc.sbuf_tensor(px + "loraT", [32, 5, T], BF16))
        dtanh = E(nc.sbuf_tensor(px + "dtanh", [64, T], BF16))
        maas = E(nc.sbuf_tensor(px + "maas_s", [128, 7, 8], F32))
        w1s = E(nc.sbuf_tensor(px + "w1s", [128, 8, 160], BF16))
        w2s = E(nc.sbuf_tensor(px + "w2s", [32, 5, C], BF16))
        Wbig = E(nc.sbuf_tensor(px + "Wbig", [128, 8, C], BF16))
        gw1 = E(nc.sbuf_tensor(px + "gw1s", [128, 8, 64], BF16))
        gw2 = E(nc.sbuf_tensor(px + "gw2s", [64, C], BF16))
        dw1 = E(nc.sbuf_tensor(px + "dw1s", [128, 8, 64], BF16))
        dw2 = E(nc.sbuf_tensor(px + "dw2s", [64, C], BF16))
        stgb_f = E(nc.sbuf_tensor(px + "stgf", [128, 2, T], F32))
        stgb = E(nc.sbuf_tensor(px + "stgb", [128, 2, T], BF16))
        PS = E(nc.psum_tensor(px + "PS", [128, 4096], F32))
        ldg = S("ldg")
        gw = S("gw")
        ld = S("ld")
        vv = S("vv")
        sv = S("sv")
        wt = S("wt")
        wvv = S("wvv")
        wvs = S("wvs")
        sy0 = S("sy0")
        sy1 = S("sy1")
        syp = [sy0, sy1]
        block = E(nc.Block())

        KINDS = [("k", 2), ("v", 3), ("r", 4), ("g", 5), ("w", 1)]
        FIDX = {"w": 0, "k": 1, "v": 2, "r": 3, "g": 4}
        nvst = [0]

        wvs_l = Waves()
        outs = []
        slot_uses = [0, 0]

        def tile(pp):
            return PS[:, 2048 * pp:2048 * (pp + 1)]

        for tc in range(4):
            def tf(t, tc=tc):
                if tc == 0:
                    t.wait_ge(sv, 1)
                    t.wait_ge(ldg, 112)
                for f in range(5):
                    for cc in range(8):
                        ins = t.matmul(PS[0:32, 512 * f:512 * f + 512],
                                       w1s[:, cc, 32 * f:32 * (f + 1)],
                                       cur[:, cc, 512 * tc:512 * (tc + 1)],
                                       start=(cc == 0), stop=(cc == 7))
                return ins

            def cf(s, tc=tc):
                for f in range(5):
                    ins = s.activation(loraT[0:32, f, 512 * tc:512 * (tc + 1)],
                                       PS[0:32, 512 * f:512 * f + 512], AF.Tanh)
                return ins
            wvs_l.wave('s', tf, cf)

        ngw = [0]
        for ni, (kind, ki) in enumerate(KINDS):
            for ct in range(8):
                pp = ct % 2

                def tf(t, ct=ct, pp=pp, f=FIDX[kind]):
                    for tc in range(4):
                        ins = t.matmul(
                            PS[:, 2048 * pp + 512 * tc:2048 * pp + 512 * tc + 512],
                            w2s[0:32, f, 128 * ct:128 * (ct + 1)],
                            loraT[0:32, f, 512 * tc:512 * (tc + 1)],
                            start=True, stop=True)
                    return ins

                def cf(v, ct=ct, pp=pp, ki=ki):
                    v.scalar_tensor_tensor(cur[:, ct, :], tile(pp),
                                           maas[:, ki, ct:ct + 1], xxs[:, ct, :],
                                           ALU.add, ALU.mult).then_inc(vv, 1)
                    nvst[0] += 1
                    v.wait_ge(vv, nvst[0])
                    return v.tensor_tensor(cur[:, ct, :], cur[:, ct, :],
                                           xTs[:, ct, 1:T + 1], ALU.add)
                wvs_l.wave('v', tf, cf)

            if kind in ("k", "v", "r"):
                ngw[0] += 1
                mygw = ngw[0]

            if kind in ("k", "r"):
                for ct in range(8):
                    pp = ct % 2

                    def tf(t, ct=ct, pp=pp, mygw=mygw):
                        if ct == 0:
                            t.wait_ge(gw, 16 * mygw)
                        for cin in range(8):
                            for tc in range(4):
                                ins = t.matmul(
                                    PS[:, 2048 * pp + 512 * tc:2048 * pp + 512 * tc + 512],
                                    Wbig[:, cin, 128 * ct:128 * (ct + 1)],
                                    cur[:, cin, 512 * tc:512 * (tc + 1)],
                                    start=(cin == 0), stop=(cin == 7))
                        return ins

                    sl = sum(1 for o in outs) % 2
                    myuse = slot_uses[sl]
                    slot_uses[sl] += 1

                    def cf(s, pp=pp, sl=sl, myuse=myuse):
                        if myuse >= 1:
                            s.wait_ge(syp[sl], 16 * myuse)
                        return s.activation(stgb[:, sl, :], tile(pp), AF.Copy)
                    wvs_l.wave('s', tf, cf)
                    outs.append((len(wvs_l.items) - 1, kind, ct, sl))
            elif kind == "v":
                for ct in range(8):
                    pp = ct % 2

                    def tf(t, ct=ct, pp=pp, mygw=mygw):
                        if ct == 0:
                            t.wait_ge(gw, 16 * mygw)
                        for u in range(2):
                            tt = 2 * ct + u
                            for cin in range(8):
                                for half in range(2):
                                    ins = t.matmul(
                                        PS[:, 2048 * pp + 1024 * u + 512 * half:
                                           2048 * pp + 1024 * u + 512 * (half + 1)],
                                        cur[:, cin, 128 * tt:128 * (tt + 1)],
                                        Wbig[:, cin, 512 * half:512 * (half + 1)],
                                        start=(cin == 0), stop=(cin == 7))
                        return ins

                    sl = len(outs) % 2
                    myuse = slot_uses[sl]
                    slot_uses[sl] += 1

                    def cf(s, pp=pp, sl=sl, myuse=myuse):
                        if myuse >= 1:
                            s.wait_ge(syp[sl], 16 * myuse)
                        return s.activation(stgb[:, sl, :], tile(pp), AF.Copy)
                    wvs_l.wave('s', tf, cf)
                    outs.append((len(wvs_l.items) - 1, kind, ct, sl))
            else:
                ws1 = gw1 if kind == "g" else dw1
                ws2 = gw2 if kind == "g" else dw2
                for tc in range(4):
                    def tf(t, tc=tc, ws1=ws1):
                        for cin in range(8):
                            ins = t.matmul(PS[0:64, 3584:3584 + 512],
                                           ws1[:, cin, :],
                                           cur[:, cin, 512 * tc:512 * (tc + 1)],
                                           start=(cin == 0), stop=(cin == 7))
                        return ins

                    def cf(s, tc=tc):
                        return s.activation(dtanh[0:64, 512 * tc:512 * (tc + 1)],
                                            PS[0:64, 3584:3584 + 512], AF.Tanh)
                    wvs_l.wave('s', tf, cf)
                if kind == "g":
                    for ct in range(8):
                        pp = ct % 2

                        def tf(t, ct=ct, pp=pp, ws2=ws2):
                            for u in range(2):
                                tt = 2 * ct + u
                                for half in range(2):
                                    ins = t.matmul(
                                        PS[:, 2048 * pp + 1024 * u + 512 * half:
                                           2048 * pp + 1024 * u + 512 * (half + 1)],
                                        dtanh[0:64, 128 * tt:128 * (tt + 1)],
                                        ws2[0:64, 512 * half:512 * (half + 1)],
                                        start=True, stop=True)
                            return ins

                        sl = len(outs) % 2
                        myuse = slot_uses[sl]
                        slot_uses[sl] += 1

                        def cf(s, pp=pp, sl=sl, myuse=myuse):
                            if myuse >= 1:
                                s.wait_ge(syp[sl], 16 * myuse)
                            return s.activation(stgb[:, sl, :], tile(pp), AF.Copy)
                        wvs_l.wave('s', tf, cf)
                        outs.append((len(wvs_l.items) - 1, kind, ct, sl))
                else:
                    for ct in range(8):
                        pp = ct % 2

                        def tf(t, ct=ct, pp=pp, ws2=ws2):
                            for tc in range(4):
                                ins = t.matmul(
                                    PS[:, 2048 * pp + 512 * tc:2048 * pp + 512 * tc + 512],
                                    ws2[0:64, 128 * ct:128 * (ct + 1)],
                                    dtanh[0:64, 512 * tc:512 * (tc + 1)],
                                    start=True, stop=True)
                            return ins

                        sl = len(outs) % 2
                        myuse = slot_uses[sl]
                        slot_uses[sl] += 1

                        def cf(v, ct=ct, pp=pp, sl=sl, myuse=myuse):
                            if myuse >= 1:
                                v.wait_ge(syp[sl], 16 * myuse)
                            return v.tensor_scalar(stgb_f[:, sl, :], tile(pp),
                                                   maas[:, 6, ct:ct + 1], None, ALU.add)
                        wvs_l.wave('v', tf, cf)
                        outs.append((len(wvs_l.items) - 1, kind, ct, sl))

        NW = len(wvs_l.items)

        @block.gpsimd
        def _(g):
            g.wait_ge(stg, kst)
            g.dma_start(out=xTs.ap(), in_=xT_d.ap().rearrange("(cc p) t -> p cc t", p=128)
                        ).then_inc(ldg, 16)
            g.dma_start(out=w1s.ap(), in_=mw1_d.ap().rearrange("(cc p) f -> p cc f", p=128)
                        ).then_inc(ldg, 16)
            g.dma_start(out=w2s.ap(), in_=mw2_d.ap().rearrange("f d c -> d f c")
                        ).then_inc(ldg, 16)
            g.dma_start(out=gw1.ap(), in_=gw1_d.ap().rearrange("(cc p) f -> p cc f", p=128)
                        ).then_inc(ldg, 16)
            g.dma_start(out=gw2.ap(), in_=gw2_d.ap()).then_inc(ldg, 16)
            g.dma_start(out=dw1.ap(), in_=dw1_d.ap().rearrange("(cc p) f -> p cc f", p=128)
                        ).then_inc(ldg, 16)
            g.dma_start(out=dw2.ap(), in_=dw2_d.ap()).then_inc(ldg, 16)
            kproj_end = 4 + 8 + 8
            vproj_end = kproj_end + 8 + 8
            for n, (Wd, need) in enumerate([(Wk_d, 0), (Wv_d, kproj_end),
                                            (Wr_d, vproj_end)]):
                if need:
                    g.wait_ge(wt, need)
                g.dma_start(out=Wbig.ap(), in_=Wd.ap().rearrange("(cc p) c -> p cc c", p=128)
                            ).then_inc(gw, 16)

        @block.sync
        def _(s):
            s.wait_ge(stg, kst)
            s.dma_start(out=maas.ap(), in_=maas_d.ap().rearrange("p (k e) -> p k e", k=7)
                        ).then_inc(ld, 16)
            nvec_done = nsca_done = 0
            done = {}
            for k, (tfn, ceng, cfn, lag) in enumerate(wvs_l.items):
                if ceng == 'v':
                    nvec_done += 1
                else:
                    nsca_done += 1
                done[k] = (nvec_done, nsca_done)
            for (widx, kind, ct, sl) in outs:
                if wvs_l.items[widx][1] == 'v':
                    s.wait_ge(wvv, done[widx][0])
                else:
                    s.wait_ge(wvs, done[widx][1])
                if kind in ("k", "r"):
                    dst = {"k": kT_o, "r": rT_o}[kind]
                    s.dma_start(out=dst.ap()[128 * ct:128 * (ct + 1), :],
                                in_=stgb[:, sl, :]).then_inc(syp[sl], 16)
                elif kind == "v":
                    s.dma_start(out=v_o.ap()[256 * ct:256 * (ct + 1), :]
                                .rearrange("(u p) c -> p u c", p=128),
                                in_=stgb[:, sl, :].rearrange("p (u c) -> p u c", u=2)
                                ).then_inc(syp[sl], 16)
                elif kind == "g":
                    s.dma_start(out=g_o.ap()[256 * ct:256 * (ct + 1), :]
                                .rearrange("(u p) c -> p u c", p=128),
                                in_=stgb[:, sl, :].rearrange("p (u c) -> p u c", u=2)
                                ).then_inc(syp[sl], 16)
                else:
                    s.dma_start(out=wrawT_o.ap()[128 * ct:128 * (ct + 1), :],
                                in_=stgb_f[:, sl, :]).then_inc(syp[sl], 16)
            s.wait_ge(syp[0], 16 * slot_uses[0])
            s.wait_ge(syp[1], 16 * slot_uses[1])
            s.sem_inc(stg, 1)

        orig_items = wvs_l.items

        @block.vector
        def _(v):
            v.wait_ge(stg, kst)
            v.wait_ge(ldg, 112)
            v.wait_ge(ld, 16)
            v.tensor_tensor(xxs.ap(), xTs[:, :, 0:T], xTs[:, :, 1:T + 1],
                            ALU.subtract).then_inc(vv, 1)
            nvst[0] += 1
            v.wait_ge(vv, nvst[0])
            v.tensor_tensor(cur.ap(), xxs.ap(),
                            maas[:, 0, :].rearrange("p (cc one) -> p cc one", one=1)
                            .broadcast_to([128, 8, T]), ALU.mult).then_inc(vv, 1)
            nvst[0] += 1
            v.wait_ge(vv, nvst[0])
            v.tensor_tensor(cur.ap(), cur.ap(), xTs[:, :, 1:T + 1],
                            ALU.add).then_inc(sv, 1)
            for k, (tfn, ceng, cfn, lag) in enumerate(orig_items):
                if ceng != 'v':
                    continue
                v.wait_ge(wt, k + 1)
                cfn(v).then_inc(wvv, 1)

        @block.scalar
        def _(s):
            s.wait_ge(stg, kst)
            s.wait_ge(ldg, 112)
            for k, (tfn, ceng, cfn, lag) in enumerate(orig_items):
                if ceng != 's':
                    continue
                s.wait_ge(wt, k + 1)
                cfn(s).then_inc(wvs, 1)

        @block.tensor
        def _(t):
            t.wait_ge(stg, kst)
            engs = [it[1] for it in orig_items]
            for k, (tfn, ceng, cfn, lag) in enumerate(orig_items):
                lim = max(0, k - lag)
                nvec = sum(1 for e in engs[:lim] if e == 'v')
                nsca = lim - nvec
                if nvec:
                    t.wait_ge(wvv, nvec)
                if nsca:
                    t.wait_ge(wvs, nsca)
                tfn(t).then_inc(wt, 1)


def _cmaj(vec):
    return np.ascontiguousarray(vec.reshape(8, 128).T)


def prep_l0front(inputs):
    x = inputs["L0_attn_x"]
    maas = np.stack([_cmaj(np.asarray(inputs[k], np.float32)) for k in
                     ["L0_attn_maa_x", "L0_attn_maa_w", "L0_attn_maa_k",
                      "L0_attn_maa_v", "L0_attn_maa_r", "L0_attn_maa_g",
                      "L0_attn_time_decay"]], axis=1)
    maas = np.ascontiguousarray(maas.reshape(128, 56), np.float32)
    shared = dict(
        maas=maas,
        maa_w1=np.asarray(inputs["L0_attn_maa_w1"], np.float32),
        maa_w2=np.asarray(inputs["L0_attn_maa_w2"], np.float32),
        Wk=np.asarray(inputs["L0_attn_Wk"], np.float32),
        Wv=np.asarray(inputs["L0_attn_Wv"], np.float32),
        Wr=np.asarray(inputs["L0_attn_Wr"], np.float32),
        gate_w1=np.asarray(inputs["L0_attn_gate_w1"], np.float32),
        gate_w2=np.asarray(inputs["L0_attn_gate_w2"], np.float32),
        decay_w1=np.asarray(inputs["L0_attn_decay_w1"], np.float32),
        decay_w2=np.asarray(inputs["L0_attn_decay_w2"], np.float32),
    )
    maps = []
    for c in range(8):
        b, hf = c // 2, c % 2
        xb = np.asarray(x[b], np.float32)
        sl = xb[2048 * hf - 1: 2048 * (hf + 1)].T if hf else None
        if hf == 0:
            xT = np.concatenate([np.zeros((C, 1), np.float32), xb[0:2048].T], axis=1)
        else:
            xT = xb[2047:4096].T
        m = dict(shared)
        m["xT"] = np.ascontiguousarray(xT)
        maps.append(m)
    return maps




TW = 4096
LCH = 64
NCHW = TW // LCH
NWW = 5


def build_wkv():
    nc = bass.Bass("TRN2", target_bir_lowering=False)
    rT_d = nc.dram_tensor("rT", [512, TW], BF16, kind="ExternalInput")
    kT_d = nc.dram_tensor("kT", [512, TW], BF16, kind="ExternalInput")
    wT_d = nc.dram_tensor("wT", [512, TW], F32, kind="ExternalInput")
    v_d = nc.dram_tensor("v", [TW, 512], BF16, kind="ExternalInput")
    ubc_d = nc.dram_tensor("ubc", [128, 256], F32, kind="ExternalInput")
    mask_d = nc.dram_tensor("maskst", [128, 64], F32, kind="ExternalInput")
    eye_d = nc.dram_tensor("eyem", [128, 64], F32, kind="ExternalInput")
    ident_d = nc.dram_tensor("ident", [128, 128], BF16, kind="ExternalInput")
    y_d = nc.dram_tensor("y", [TW, 512], BF16, kind="ExternalOutput")

    with ExitStack() as ctx:
        E = ctx.enter_context
        rT = E(nc.sbuf_tensor("rT_s", [128, 4 * TW], BF16))
        kT = E(nc.sbuf_tensor("kT_s", [128, 4 * TW], BF16))
        v = E(nc.sbuf_tensor("v_s", [128, 32 * 512], BF16))
        ytok = E(nc.sbuf_tensor("ytok", [128, 32 * 512], BF16))
        ubc = E(nc.sbuf_tensor("ubc_s", [128, 256], BF16))
        maskst = E(nc.sbuf_tensor("mask_s", [128, 64], BF16))
        eyem = E(nc.sbuf_tensor("eye_s", [128, 64], BF16))
        idt = E(nc.sbuf_tensor("id_s", [128, 128], BF16))
        ones_s = E(nc.sbuf_tensor("ones_s", [128, 64], F32))
        wch = E(nc.sbuf_tensor("wch_s", [128, 2, 256], F32))
        ew = E(nc.sbuf_tensor("ew_s", [128, 2, 256], F32))
        acum = E(nc.sbuf_tensor("ac_s", [128, 2, 256], F32))
        pexp = E(nc.sbuf_tensor("pexp_s", [128, 2, 256], BF16))
        pm = E(nc.sbuf_tensor("pm_s", [128, 2, 256], F32))
        pmb = E(nc.sbuf_tensor("pmb_s", [128, 2, 256], BF16))
        rt_t = E(nc.sbuf_tensor("rt_s", [128, 2, 256], BF16))
        kt_t = E(nc.sbuf_tensor("ktl_s", [128, 2, 256], BF16))
        kh_t = E(nc.sbuf_tensor("kh_s", [128, 2, 256], BF16))
        uk_t = E(nc.sbuf_tensor("uk_s", [128, 2, 256], BF16))
        a_sb = E(nc.sbuf_tensor("a_sb", [128, 512], F32))
        ahat = E(nc.sbuf_tensor("ahat_s", [128, 512], BF16))
        ahat2 = E(nc.sbuf_tensor("ahat2_s", [128, 512], BF16))
        khtok = E(nc.sbuf_tensor("khtok_s", [128, 512], BF16))
        sf32 = E(nc.sbuf_tensor("sf32_s", [128, 256], F32))
        sbf = E(nc.sbuf_tensor("sbf_s", [128, 256], BF16))
        ysb = E(nc.sbuf_tensor("ysb_s", [128, 512], BF16))
        PS = E(nc.psum_tensor("PS", [128, 4096], F32))
        ld = E(nc.semaphore("ld"))
        ldg = E(nc.semaphore("ldg"))
        vv = E(nc.semaphore("vv"))
        wld0 = E(nc.semaphore("wld0"))
        wld1 = E(nc.semaphore("wld1"))
        wldp = [wld0, wld1]
        se1 = E(nc.semaphore("se1"))
        sv0 = E(nc.semaphore("sv0"))
        se2 = E(nc.semaphore("se2"))
        v1 = E(nc.semaphore("v1"))
        wt = E(nc.semaphore("wt"))
        wv = E(nc.semaphore("wv"))
        sto = E(nc.semaphore("sto"))
        block = E(nc.Block())
        PSB = PS.ap().bitcast(BF16)

        def opchunk(buf, ch):
            return buf.ap().rearrange("p (cc t) -> p cc t", cc=4)[:, :, ch * LCH:(ch + 1) * LCH]

        def scr3(buf, pr):
            return buf[:, pr, :].rearrange("p (cc t) -> p cc t", cc=4)

        @block.sync
        def _(s):
            wsrc = wT_d.ap().rearrange("(cc p) t -> p cc t", p=128)
            for ch in range(NCHW):
                pr = ch % 2
                if ch >= 2:
                    s.wait_ge(se1, ch - 1)
                s.dma_start(out=wch[:, pr, :].rearrange("p (cc t) -> p cc t", cc=4),
                            in_=wsrc[:, :, ch * LCH:(ch + 1) * LCH]).then_inc(wldp[pr], 16)

        @block.gpsimd
        def _(g):
            g.dma_start(out=rT.ap().rearrange("p (cc t) -> p cc t", cc=4),
                        in_=rT_d.ap().rearrange("(cc p) t -> p cc t", p=128)).then_inc(ldg, 16)
            g.dma_start(out=kT.ap().rearrange("p (cc t) -> p cc t", cc=4),
                        in_=kT_d.ap().rearrange("(cc p) t -> p cc t", p=128)).then_inc(ldg, 16)
            g.dma_start(out=v.ap().rearrange("p (tt c) -> p tt c", tt=32),
                        in_=v_d.ap().rearrange("(tt p) c -> p tt c", p=128)).then_inc(ldg, 16)
            g.dma_start(out=ubc.ap(), in_=ubc_d.ap()).then_inc(ldg, 16)
            g.dma_start(out=maskst.ap(), in_=mask_d.ap()).then_inc(ldg, 16)
            g.dma_start(out=eyem.ap(), in_=eye_d.ap()).then_inc(ldg, 16)
            g.dma_start(out=idt.ap(), in_=ident_d.ap()).then_inc(ldg, 16)
            g.wait_ge(wv, NCHW * NWW)
            g.dma_start(out=y_d.ap().rearrange("(tt p) c -> p tt c", p=128),
                        in_=ytok.ap().rearrange("p (tt c) -> p tt c", tt=32)).then_inc(sto, 16)
            g.wait_ge(sto, 16)

        @block.scalar
        def _(sc):
            sc.wait_ge(ldg, 112)
            for ch in range(NCHW):
                pr = ch % 2
                if ch >= 2:
                    sc.wait_ge(sv0, ch - 1)
                sc.wait_ge(wldp[pr], 16 * (ch // 2 + 1))
                sc.activation(ew[:, pr, :], wch[:, pr, :], AF.Exp).then_inc(se1, 1)
                sc.wait_ge(sv0, ch + 1)
                sc.activation(pexp[:, pr, :], acum[:, pr, :], AF.Exp)
                sc.activation(pm[:, pr, :], acum[:, pr, :], AF.Exp, scale=-1.0).then_inc(se2, 1)
                sc.wait_ge(se2, 2 * ch + 1)
                sc.activation(pmb[:, pr, :], pm[:, pr, :], AF.Copy).then_inc(se2, 1)

        @block.vector
        def _(v_):
            v_.wait_ge(ldg, 112)
            nv = 0
            v_.memset(ones_s[:, :], 1.0)
            v_.memset(sf32[:, :], 0.0)
            v_.memset(sbf[:, :], 0.0).then_inc(vv, 1)
            nv += 1
            v_.wait_ge(vv, nv)
            for ch in range(NCHW):
                pr = ch % 2
                ph = ch % 2
                tt = ch // 2
                pb = 64 * ph
                v_.wait_ge(se1, ch + 1)
                if ch >= 2:
                    v_.wait_ge(wt, NWW * (ch - 1))
                for cc in range(4):
                    ins = v_.tensor_tensor_scan(
                        acum[:, pr, cc * LCH:(cc + 1) * LCH], ew[:, pr, cc * LCH:(cc + 1) * LCH],
                        ones_s[:, :], 0.0, ALU.add, ALU.mult)
                ins.then_inc(sv0, 1)
                v_.wait_ge(se2, 2 * (ch + 1))
                pmb3 = pmb[:, pr, :].rearrange("p (cc t) -> p cc t", cc=4)
                v_.tensor_tensor(scr3(rt_t, pr)[:, :, 1:LCH], opchunk(rT, ch)[:, :, 1:LCH],
                                 pmb3[:, :, 0:LCH - 1], ALU.mult)
                v_.tensor_copy(scr3(rt_t, pr)[:, :, 0:1], opchunk(rT, ch)[:, :, 0:1])
                v_.tensor_tensor(scr3(kt_t, pr), opchunk(kT, ch), scr3(pexp, pr),
                                 ALU.mult).then_inc(vv, 1)
                nv += 1
                v_.wait_ge(vv, nv)
                v_.tensor_tensor(scr3(kh_t, pr), scr3(kt_t, pr),
                                 pmb3[:, :, LCH - 1:LCH].broadcast_to([128, 4, LCH]), ALU.mult)
                v_.tensor_tensor(scr3(uk_t, pr), opchunk(kT, ch),
                                 ubc.ap().rearrange("p (cc t) -> p cc t", cc=4),
                                 ALU.mult).then_inc(v1, 1)
                ps8 = PS[pb:pb + 64, :].rearrange("p (b q) -> p b q", b=8)
                psb8 = PSB[pb:pb + 64, :].rearrange("p (b q) -> p b q", b=8)
                v_.wait_ge(wt, ch * NWW + 1)
                v_.tensor_copy(a_sb[pb:pb + 64, :].rearrange("p (h q) -> p h q", h=8),
                               ps8[:, :, 0:64])
                v_.tensor_copy(khtok[pb:pb + 64, :].rearrange("p (j q) -> p j q", j=4),
                               psb8[:, 0:4, 512:640]).then_inc(wv, 1)
                v_.wait_ge(wt, ch * NWW + 2)
                mb = maskst[pb:pb + 64, :].rearrange("p (one q) -> p one q", one=1) \
                    .broadcast_to([64, 8, 64])
                eb = eyem[pb:pb + 64, :].rearrange("p (one q) -> p one q", one=1) \
                    .broadcast_to([64, 8, 64])
                v_.tensor_tensor(ahat[pb:pb + 64, :].rearrange("p (h q) -> p h q", h=8),
                                 a_sb[pb:pb + 64, :].rearrange("p (h q) -> p h q", h=8),
                                 mb, ALU.mult)
                v_.tensor_tensor(ahat2[pb:pb + 64, :].rearrange("p (h q) -> p h q", h=8),
                                 ps8[:, :, 0:64], eb, ALU.mult).then_inc(vv, 1)
                nv += 1
                v_.wait_ge(vv, nv)
                v_.tensor_tensor(ahat[pb:pb + 64, :], ahat[pb:pb + 64, :],
                                 ahat2[pb:pb + 64, :], ALU.add).then_inc(wv, 1)
                v_.wait_ge(wt, ch * NWW + 3)
                v_.tensor_copy(ysb[pb:pb + 64, :].rearrange("p (h q) -> p h q", h=8),
                               ps8[:, :, 0:64]).then_inc(wv, 1)
                v_.wait_ge(wt, ch * NWW + 4)
                v_.tensor_tensor(ytok[pb:pb + 64, tt * 512:(tt + 1) * 512]
                                 .rearrange("p (h q) -> p h q", h=8),
                                 ps8[:, :, 0:64],
                                 ysb[pb:pb + 64, :].rearrange("p (h q) -> p h q", h=8),
                                 ALU.add).then_inc(wv, 1)
                v_.wait_ge(wt, ch * NWW + 5)
                pmB = pm[:, pr, :].rearrange("p (cc q) -> p cc q", cc=4)
                v_.tensor_tensor(sf32.ap().rearrange("p (j q) -> p j q", j=4),
                                 sf32.ap().rearrange("p (j q) -> p j q", j=4),
                                 pmB[:, :, LCH - 1:LCH].broadcast_to([128, 4, 64]),
                                 ALU.mult).then_inc(vv, 1)
                nv += 1
                v_.wait_ge(vv, nv)
                v_.tensor_tensor(sf32[0:64, :].rearrange("p (j q) -> p j q", j=4),
                                 sf32[0:64, :].rearrange("p (j q) -> p j q", j=4),
                                 PS[0:64, :].rearrange("p (b q) -> p b q", b=8)[:, 0:8:2, 0:64],
                                 ALU.add)
                v_.tensor_tensor(sf32[64:128, :].rearrange("p (j q) -> p j q", j=4),
                                 sf32[64:128, :].rearrange("p (j q) -> p j q", j=4),
                                 PS[64:128, :].rearrange("p (b q) -> p b q", b=8)[:, 1:8:2, 0:64],
                                 ALU.add).then_inc(vv, 1)
                nv += 1
                v_.wait_ge(vv, nv)
                v_.tensor_copy(sbf[:, :], sf32[:, :]).then_inc(wv, 1)

        @block.tensor
        def _(t):
            t.wait_ge(ldg, 112)
            for ch in range(NCHW):
                pr = ch % 2
                ph = ch % 2
                tt = ch // 2
                pb = 64 * ph
                t.wait_ge(v1, ch + 1)
                gk = ch * NWW
                if gk >= 1:
                    t.wait_ge(wv, gk)
                for hl in range(8):
                    j, hp = hl // 2, hl % 2
                    cb = 64 * hp
                    t.matmul(PS[pb:pb + 64, 512 * hl:512 * hl + 64],
                             kt_t[cb:cb + 64, pr, 64 * j:64 * (j + 1)],
                             rt_t[cb:cb + 64, pr, 64 * j:64 * (j + 1)],
                             start=True, stop=True)
                for j in range(4):
                    ins = t.transpose(PSB[pb:pb + 64, 1024 * j + 512:1024 * j + 640],
                                      kh_t[:, pr, 64 * j:64 * (j + 1)], idt[:, :])
                ins.then_inc(wt, 1)
                gk += 1
                t.wait_ge(wv, gk)
                for hl in range(8):
                    j, hp = hl // 2, hl % 2
                    cb = 64 * hp
                    ins = t.matmul(PS[pb:pb + 64, 512 * hl:512 * hl + 64],
                                   uk_t[cb:cb + 64, pr, 64 * j:64 * (j + 1)],
                                   opchunk(rT, ch)[cb:cb + 64, j, :],
                                   start=True, stop=True)
                ins.then_inc(wt, 1)
                gk += 1
                t.wait_ge(wv, gk)
                for hl in range(8):
                    vblk = v[pb:pb + 64, tt * 512 + 64 * hl: tt * 512 + 64 * (hl + 1)]
                    ins = t.matmul(PS[pb:pb + 64, 512 * hl:512 * hl + 64],
                                   ahat[pb:pb + 64, 64 * hl:64 * (hl + 1)], vblk,
                                   start=True, stop=True)
                ins.then_inc(wt, 1)
                gk += 1
                t.wait_ge(wv, gk)
                for hl in range(8):
                    j, hp = hl // 2, hl % 2
                    cb = 64 * hp
                    ins = t.matmul(PS[pb:pb + 64, 512 * hl:512 * hl + 64],
                                   rt_t[cb:cb + 64, pr, 64 * j:64 * (j + 1)],
                                   sbf[cb:cb + 64, 64 * j:64 * (j + 1)],
                                   start=True, stop=True)
                ins.then_inc(wt, 1)
                gk += 1
                t.wait_ge(wv, gk)
                for hl in range(8):
                    j, hp = hl // 2, hl % 2
                    vblk = v[pb:pb + 64, tt * 512 + 64 * hl: tt * 512 + 64 * (hl + 1)]
                    ins = t.matmul(PS[64 * hp:64 * hp + 64, 512 * hl:512 * hl + 64],
                                   khtok[pb:pb + 64, 128 * j + 64 * hp:128 * j + 64 * (hp + 1)],
                                   vblk, start=True, stop=True)
                ins.then_inc(wt, 1)
                gk += 1
    return nc


def prep_wkv(front_outs, u_full):
    maps = []
    maskst = np.zeros((128, 64), np.float32)
    eyem = np.zeros((128, 64), np.float32)
    for p in range(128):
        for tcol in range(64):
            if (p % 64) < tcol:
                maskst[p, tcol] = 1.0
            if (p % 64) == tcol:
                eyem[p, tcol] = 1.0
    ident = np.eye(128, dtype=BF)
    for c in range(8):
        b, hg = c // 2, c % 2
        csl = slice(512 * hg, 512 * (hg + 1))
        rT = np.concatenate([np.asarray(front_outs[2 * b]["rT_o"])[csl],
                             np.asarray(front_outs[2 * b + 1]["rT_o"])[csl]], axis=1)
        kT = np.concatenate([np.asarray(front_outs[2 * b]["kT_o"])[csl],
                             np.asarray(front_outs[2 * b + 1]["kT_o"])[csl]], axis=1)
        wT = np.concatenate([np.asarray(front_outs[2 * b]["wrawT_o"])[csl],
                             np.asarray(front_outs[2 * b + 1]["wrawT_o"])[csl]], axis=1)
        vv_ = np.concatenate([np.asarray(front_outs[2 * b]["v_o"])[:, csl],
                              np.asarray(front_outs[2 * b + 1]["v_o"])[:, csl]], axis=0)
        ubc = np.zeros((128, 256), np.float32)
        u = np.asarray(u_full, np.float32)
        for hl in range(8):
            j, hp = hl // 2, hl % 2
            ubc[64 * hp:64 * hp + 64, 64 * j:64 * (j + 1)] = u[8 * hg + hl][:, None]
        maps.append(dict(rT=np.ascontiguousarray(rT), kT=np.ascontiguousarray(kT),
                         wT=np.ascontiguousarray(wT), v=np.ascontiguousarray(vv_),
                         ubc=ubc, maskst=maskst, eyem=eyem, ident=ident))
    return maps



PAIRS4x2 = [[0, 1], [2, 3], [4, 5], [6, 7]]


def build_wkv2():
    nc = bass.Bass("TRN2", target_bir_lowering=False)
    io = dict(
        rTf=nc.dram_tensor("rTf", [1024, T], BF16, kind="ExternalInput"),
        kTf=nc.dram_tensor("kTf", [1024, T], BF16, kind="ExternalInput"),
        wTf=nc.dram_tensor("wTf", [1024, T], F32, kind="ExternalInput"),
        vf=nc.dram_tensor("vf", [T, 1024], BF16, kind="ExternalInput"),
        ubc2=nc.dram_tensor("ubc2", [128, 512], F32, kind="ExternalInput"),
        maskst=nc.dram_tensor("maskst", [128, 64], F32, kind="ExternalInput"),
        eyem=nc.dram_tensor("eyem", [128, 64], F32, kind="ExternalInput"),
        ident=nc.dram_tensor("ident", [128, 128], BF16, kind="ExternalInput"),
        usin=nc.dram_tensor("usin", [128, 1], F32, kind="ExternalInput"),
        y2=nc.dram_tensor("y2", [T, 1024], BF16, kind="ExternalOutput"),
        sfb=nc.dram_tensor("sfb", [128, 512], F32, kind="Internal"),
        agS=nc.dram_tensor("agS", [256, 512], F32, kind="Internal"),
    )
    with ExitStack() as ES:
        stgs = ES.enter_context(nc.semaphore("stgs"))
        build_wkv2_body(nc, ES, io, "", stgs, 0)
    return nc


def build_wkv2_body(nc, ES, io, px, stg, kst):
    rT_d, kT_d, wT_d, v_d = io["rTf"], io["kTf"], io["wTf"], io["vf"]
    ubc_d, mask_d, eye_d, ident_d = (io["ubc2"], io["maskst"], io["eyem"],
                                     io["ident"])
    usin_d, y_d, sfb_d, agS_d = io["usin"], io["y2"], io["sfb"], io["agS"]
    NCH = 64

    def S(n):
        return ES.enter_context(nc.semaphore(px + n))

    with ExitStack() as ctx:
        E = ctx.enter_context
        rT = E(nc.sbuf_tensor(px + "rT_s", [128, 8 * T], BF16))
        kT = E(nc.sbuf_tensor(px + "kT_s", [128, 8 * T], BF16))
        v = E(nc.sbuf_tensor(px + "v_s", [128, 16 * 1024], BF16))
        ytok = E(nc.sbuf_tensor(px + "ytok", [128, 16 * 1024], BF16))
        Rhat = E(nc.sbuf_tensor(px + "Rhat", [128, 8, T], BF16))
        ubc = E(nc.sbuf_tensor(px + "ubc_s", [128, 2, 256], BF16))
        maskst = E(nc.sbuf_tensor(px + "mask_s", [128, 64], BF16))
        eyem = E(nc.sbuf_tensor(px + "eye_s", [128, 64], BF16))
        idt = E(nc.sbuf_tensor(px + "id_s", [128, 128], BF16))
        ones_s = E(nc.sbuf_tensor(px + "ones_s", [128, 64], F32))
        wch = E(nc.sbuf_tensor(px + "wch_s", [128, 2, 256], F32))
        ew = E(nc.sbuf_tensor(px + "ew_s", [128, 2, 256], F32))
        acum = E(nc.sbuf_tensor(px + "ac_s", [128, 2, 256], F32))
        pexp = E(nc.sbuf_tensor(px + "pexp_s", [128, 2, 256], BF16))
        pm = E(nc.sbuf_tensor(px + "pm_s", [128, 2, 256], F32))
        pmb = E(nc.sbuf_tensor(px + "pmb_s", [128, 2, 256], BF16))
        rt_t = E(nc.sbuf_tensor(px + "rt_s", [128, 2, 256], BF16))
        kt_t = E(nc.sbuf_tensor(px + "ktl_s", [128, 2, 256], BF16))
        kh_t = E(nc.sbuf_tensor(px + "kh_s", [128, 2, 256], BF16))
        uk_t = E(nc.sbuf_tensor(px + "uk_s", [128, 2, 256], BF16))
        a_sb = E(nc.sbuf_tensor(px + "a_sb", [128, 512], F32))
        ahat = E(nc.sbuf_tensor(px + "ahat_s", [128, 512], BF16))
        ahat2 = E(nc.sbuf_tensor(px + "ahat2_s", [128, 512], BF16))
        khtok = E(nc.sbuf_tensor(px + "khtok_s", [128, 512], BF16))
        sf32 = E(nc.sbuf_tensor(px + "sf32_s", [128, 256], F32))
        sbf = E(nc.sbuf_tensor(px + "sbf_s", [128, 256], BF16))
        ysb = E(nc.sbuf_tensor(px + "ysb_s", [128, 512], BF16))
        Poff = E(nc.sbuf_tensor(px + "Poff", [128, 8], F32))
        Sfin = E(nc.sbuf_tensor(px + "Sfin", [128, 8, 64], F32))
        Sin = E(nc.sbuf_tensor(px + "Sin", [128, 8, 64], F32))
        Sinb = E(nc.sbuf_tensor(px + "Sinb", [128, 8, 64], BF16))
        usin_s = E(nc.sbuf_tensor(px + "usin_s", [128, 1], F32))
        PS = E(nc.psum_tensor(px + "PS", [128, 4096], F32))
        ldg = S("ldg")
        vv = S("vv")
        wld0 = S("wld0")
        wld1 = S("wld1")
        wldp = [wld0, wld1]
        se1 = S("se1")
        sv0 = S("sv0")
        se2 = S("se2")
        v1 = S("v1")
        wt = S("wt")
        wv = S("wv")
        sto = S("sto")
        gs2 = S("gs2")
        gcc = S("gcc")
        ct = S("ct")
        cv = S("cv")
        block = E(nc.Block())
        PSB = PS.ap().bitcast(BF16)
        WVEND = NCH * NWW + 1
        WVSIN = WVEND + 1
        NCORR = 2 * 16 * 8

        def opchunk(buf, ch):
            grp, chl = ch // 32, ch % 32
            return buf.ap().rearrange("p (cc t) -> p cc t", cc=8)[
                :, 4 * grp:4 * grp + 4, chl * LCH:(chl + 1) * LCH]

        def scr3(buf, pr):
            return buf[:, pr, :].rearrange("p (cc t) -> p cc t", cc=4)

        corr_order = [(g2, tb, hl) for g2 in range(2) for tb in range(16)
                      for hl in range(8)]

        @block.sync
        def _(s):
            s.wait_ge(stg, kst)
            wsrc = wT_d.ap().rearrange("(cc p) t -> p cc t", p=128)
            for ch in range(NCH):
                pr = ch % 2
                grp, chl = ch // 32, ch % 32
                if ch >= 2:
                    s.wait_ge(se1, ch - 1)
                s.dma_start(out=wch[:, pr, :].rearrange("p (cc t) -> p cc t", cc=4),
                            in_=wsrc[:, 4 * grp:4 * grp + 4,
                                     chl * LCH:(chl + 1) * LCH]).then_inc(wldp[pr], 16)

        @block.gpsimd
        def _(g):
            g.wait_ge(stg, kst)
            g.dma_start(out=rT.ap().rearrange("p (cc t) -> p cc t", cc=8),
                        in_=rT_d.ap().rearrange("(cc p) t -> p cc t", p=128)).then_inc(ldg, 16)
            g.dma_start(out=kT.ap().rearrange("p (cc t) -> p cc t", cc=8),
                        in_=kT_d.ap().rearrange("(cc p) t -> p cc t", p=128)).then_inc(ldg, 16)
            g.dma_start(out=v.ap().rearrange("p (tt c) -> p tt c", tt=16),
                        in_=v_d.ap().rearrange("(tt p) c -> p tt c", p=128)).then_inc(ldg, 16)
            g.dma_start(out=ubc.ap().rearrange("p g t -> p (g t)"),
                        in_=ubc_d.ap()).then_inc(ldg, 16)
            g.dma_start(out=maskst.ap(), in_=mask_d.ap()).then_inc(ldg, 16)
            g.dma_start(out=eyem.ap(), in_=eye_d.ap()).then_inc(ldg, 16)
            g.dma_start(out=idt.ap(), in_=ident_d.ap()).then_inc(ldg, 16)
            g.dma_start(out=usin_s.ap(), in_=usin_d.ap()).then_inc(ldg, 16)
            g.wait_ge(wv, WVEND)
            g.dma_start(out=sfb_d.ap(),
                        in_=Sfin.ap().rearrange("p a b -> p (a b)")).then_inc(gs2, 16)
            g.wait_ge(gs2, 16)
            g.collective_compute(
                "AllGather", ALU.bypass, replica_groups=PAIRS4x2,
                ins=[sfb_d.ap()], outs=[agS_d.ap()]).then_inc(gcc, 1)
            g.wait_ge(gcc, 1)
            g.dma_start(out=Sin.ap().rearrange("p a b -> p (a b)"),
                        in_=agS_d.ap()[0:128, :]).then_inc(gs2, 16)
            g.wait_ge(gs2, 32)
            g.wait_ge(cv, NCORR)
            g.dma_start(out=y_d.ap().rearrange("(tt p) c -> p tt c", p=128),
                        in_=ytok.ap().rearrange("p (tt c) -> p tt c", tt=16)).then_inc(sto, 16)
            g.wait_ge(sto, 16)
            g.sem_inc(stg, 1)

        @block.scalar
        def _(sc):
            sc.wait_ge(stg, kst)
            sc.wait_ge(ldg, 128)
            for ch in range(NCH):
                pr = ch % 2
                if ch >= 2:
                    sc.wait_ge(sv0, ch - 1)
                sc.wait_ge(wldp[pr], 16 * (ch // 2 + 1))
                sc.activation(ew[:, pr, :], wch[:, pr, :], AF.Exp).then_inc(se1, 1)
                sc.wait_ge(sv0, ch + 1)
                sc.activation(pexp[:, pr, :], acum[:, pr, :], AF.Exp)
                sc.activation(pm[:, pr, :], acum[:, pr, :], AF.Exp, scale=-1.0).then_inc(se2, 1)
                sc.wait_ge(se2, 2 * ch + 1)
                sc.activation(pmb[:, pr, :], pm[:, pr, :], AF.Copy).then_inc(se2, 1)

        @block.vector
        def _(v_):
            v_.wait_ge(stg, kst)
            v_.wait_ge(ldg, 128)
            nv = 0
            v_.memset(ones_s[:, :], 1.0)
            v_.memset(sf32[:, :], 0.0)
            v_.memset(Poff[:, :], 1.0)
            v_.memset(sbf[:, :], 0.0).then_inc(vv, 1)
            nv += 1
            v_.wait_ge(vv, nv)
            for ch in range(NCH):
                pr = ch % 2
                ph = ch % 2
                grp, chl = ch // 32, ch % 32
                tt = chl // 2
                pb = 64 * ph
                if ch == 32:
                    v_.wait_ge(wv, NWW * 32)
                    v_.tensor_copy(Sfin[:, 0:4, :],
                                   sf32.ap().rearrange("p (j q) -> p j q", j=4)
                                   ).then_inc(vv, 1)
                    nv += 1
                    v_.wait_ge(vv, nv)
                    v_.memset(sf32[:, :], 0.0)
                    v_.memset(sbf[:, :], 0.0).then_inc(vv, 1)
                    nv += 1
                    v_.wait_ge(vv, nv)
                v_.wait_ge(se1, ch + 1)
                if ch >= 2:
                    v_.wait_ge(wt, NWW * (ch - 1))
                for cc in range(4):
                    ins = v_.tensor_tensor_scan(
                        acum[:, pr, cc * LCH:(cc + 1) * LCH], ew[:, pr, cc * LCH:(cc + 1) * LCH],
                        ones_s[:, :], 0.0, ALU.add, ALU.mult)
                ins.then_inc(sv0, 1)
                v_.wait_ge(se2, 2 * (ch + 1))
                pmb3 = pmb[:, pr, :].rearrange("p (cc t) -> p cc t", cc=4)
                v_.tensor_tensor(scr3(rt_t, pr)[:, :, 1:LCH], opchunk(rT, ch)[:, :, 1:LCH],
                                 pmb3[:, :, 0:LCH - 1], ALU.mult)
                v_.tensor_copy(scr3(rt_t, pr)[:, :, 0:1], opchunk(rT, ch)[:, :, 0:1])
                v_.tensor_tensor(scr3(kt_t, pr), opchunk(kT, ch), scr3(pexp, pr),
                                 ALU.mult).then_inc(vv, 1)
                nv += 1
                v_.wait_ge(vv, nv)
                v_.tensor_tensor(Rhat.ap()[:, 4 * grp:4 * grp + 4, chl * LCH:(chl + 1) * LCH],
                                 scr3(rt_t, pr),
                                 Poff[:, 4 * grp:4 * grp + 4]
                                 .rearrange("p (j one) -> p j one", one=1)
                                 .broadcast_to([128, 4, LCH]), ALU.mult).then_inc(vv, 1)
                nv += 1
                v_.wait_ge(vv, nv)
                v_.tensor_tensor(Poff[:, 4 * grp:4 * grp + 4]
                                 .rearrange("p (j one) -> p j one", one=1),
                                 Poff[:, 4 * grp:4 * grp + 4]
                                 .rearrange("p (j one) -> p j one", one=1),
                                 pm[:, pr, :].rearrange("p (cc t) -> p cc t", cc=4)
                                 [:, :, LCH - 1:LCH], ALU.mult)
                v_.tensor_tensor(scr3(kh_t, pr), scr3(kt_t, pr),
                                 pmb3[:, :, LCH - 1:LCH].broadcast_to([128, 4, LCH]), ALU.mult)
                v_.tensor_tensor(scr3(uk_t, pr), opchunk(kT, ch),
                                 ubc[:, grp, :].rearrange("p (cc t) -> p cc t", cc=4),
                                 ALU.mult).then_inc(v1, 1)
                ps8 = PS[pb:pb + 64, :].rearrange("p (b q) -> p b q", b=8)
                psb8 = PSB[pb:pb + 64, :].rearrange("p (b q) -> p b q", b=8)
                v_.wait_ge(wt, ch * NWW + 1)
                v_.tensor_copy(a_sb[pb:pb + 64, :].rearrange("p (h q) -> p h q", h=8),
                               ps8[:, :, 0:64])
                v_.tensor_copy(khtok[pb:pb + 64, :].rearrange("p (j q) -> p j q", j=4),
                               psb8[:, 0:4, 512:640]).then_inc(wv, 1)
                v_.wait_ge(wt, ch * NWW + 2)
                mb = maskst[pb:pb + 64, :].rearrange("p (one q) -> p one q", one=1) \
                    .broadcast_to([64, 8, 64])
                eb = eyem[pb:pb + 64, :].rearrange("p (one q) -> p one q", one=1) \
                    .broadcast_to([64, 8, 64])
                v_.tensor_tensor(ahat[pb:pb + 64, :].rearrange("p (h q) -> p h q", h=8),
                                 a_sb[pb:pb + 64, :].rearrange("p (h q) -> p h q", h=8),
                                 mb, ALU.mult)
                v_.tensor_tensor(ahat2[pb:pb + 64, :].rearrange("p (h q) -> p h q", h=8),
                                 ps8[:, :, 0:64], eb, ALU.mult).then_inc(vv, 1)
                nv += 1
                v_.wait_ge(vv, nv)
                v_.tensor_tensor(ahat[pb:pb + 64, :], ahat[pb:pb + 64, :],
                                 ahat2[pb:pb + 64, :], ALU.add).then_inc(wv, 1)
                v_.wait_ge(wt, ch * NWW + 3)
                v_.tensor_copy(ysb[pb:pb + 64, :].rearrange("p (h q) -> p h q", h=8),
                               ps8[:, :, 0:64]).then_inc(wv, 1)
                v_.wait_ge(wt, ch * NWW + 4)
                v_.tensor_tensor(ytok[pb:pb + 64, tt * 1024 + 512 * grp:
                                      tt * 1024 + 512 * grp + 512]
                                 .rearrange("p (h q) -> p h q", h=8),
                                 ps8[:, :, 0:64],
                                 ysb[pb:pb + 64, :].rearrange("p (h q) -> p h q", h=8),
                                 ALU.add).then_inc(wv, 1)
                v_.wait_ge(wt, ch * NWW + 5)
                pmB = pm[:, pr, :].rearrange("p (cc q) -> p cc q", cc=4)
                v_.tensor_tensor(sf32.ap().rearrange("p (j q) -> p j q", j=4),
                                 sf32.ap().rearrange("p (j q) -> p j q", j=4),
                                 pmB[:, :, LCH - 1:LCH].broadcast_to([128, 4, 64]),
                                 ALU.mult).then_inc(vv, 1)
                nv += 1
                v_.wait_ge(vv, nv)
                v_.tensor_tensor(sf32[0:64, :].rearrange("p (j q) -> p j q", j=4),
                                 sf32[0:64, :].rearrange("p (j q) -> p j q", j=4),
                                 PS[0:64, :].rearrange("p (b q) -> p b q", b=8)[:, 0:8:2, 0:64],
                                 ALU.add)
                v_.tensor_tensor(sf32[64:128, :].rearrange("p (j q) -> p j q", j=4),
                                 sf32[64:128, :].rearrange("p (j q) -> p j q", j=4),
                                 PS[64:128, :].rearrange("p (b q) -> p b q", b=8)[:, 1:8:2, 0:64],
                                 ALU.add).then_inc(vv, 1)
                nv += 1
                v_.wait_ge(vv, nv)
                v_.tensor_copy(sbf[:, :], sf32[:, :]).then_inc(wv, 1)
            v_.wait_ge(wv, NWW * NCH)
            v_.tensor_copy(Sfin[:, 4:8, :],
                           sf32.ap().rearrange("p (j q) -> p j q", j=4)).then_inc(wv, 1)
            v_.wait_ge(gs2, 32)
            v_.tensor_scalar(Sin.ap().rearrange("p a b -> p (a b)"),
                             Sin.ap().rearrange("p a b -> p (a b)"),
                             usin_s[:, 0:1], None, ALU.mult).then_inc(vv, 1)
            nv += 1
            v_.wait_ge(vv, nv)
            v_.tensor_copy(Sinb.ap(), Sin.ap()).then_inc(wv, 1)
            for n, (g2, tb, hl) in enumerate(corr_order):
                v_.wait_ge(ct, n + 1)
                sl = tb * 1024 + 512 * g2 + 64 * hl
                v_.tensor_tensor(ytok[:, sl:sl + 64], ytok[:, sl:sl + 64],
                                 PS[:, 512 * hl:512 * hl + 64], ALU.add).then_inc(cv, 1)

        @block.tensor
        def _(t):
            t.wait_ge(stg, kst)
            t.wait_ge(ldg, 128)
            for ch in range(NCH):
                pr = ch % 2
                ph = ch % 2
                grp, chl = ch // 32, ch % 32
                tt = chl // 2
                pb = 64 * ph
                t.wait_ge(v1, ch + 1)
                gk = ch * NWW
                if gk >= 1:
                    t.wait_ge(wv, gk)
                for hl in range(8):
                    j, hp = hl // 2, hl % 2
                    cb = 64 * hp
                    t.matmul(PS[pb:pb + 64, 512 * hl:512 * hl + 64],
                             kt_t[cb:cb + 64, pr, 64 * j:64 * (j + 1)],
                             rt_t[cb:cb + 64, pr, 64 * j:64 * (j + 1)],
                             start=True, stop=True)
                for j in range(4):
                    ins = t.transpose(PSB[pb:pb + 64, 1024 * j + 512:1024 * j + 640],
                                      kh_t[:, pr, 64 * j:64 * (j + 1)], idt[:, :])
                ins.then_inc(wt, 1)
                gk += 1
                t.wait_ge(wv, gk)
                for hl in range(8):
                    j, hp = hl // 2, hl % 2
                    cb = 64 * hp
                    ins = t.matmul(PS[pb:pb + 64, 512 * hl:512 * hl + 64],
                                   uk_t[cb:cb + 64, pr, 64 * j:64 * (j + 1)],
                                   opchunk(rT, ch)[cb:cb + 64, j, :],
                                   start=True, stop=True)
                ins.then_inc(wt, 1)
                gk += 1
                t.wait_ge(wv, gk)
                for hl in range(8):
                    vblk = v[pb:pb + 64, tt * 1024 + 512 * grp + 64 * hl:
                             tt * 1024 + 512 * grp + 64 * (hl + 1)]
                    ins = t.matmul(PS[pb:pb + 64, 512 * hl:512 * hl + 64],
                                   ahat[pb:pb + 64, 64 * hl:64 * (hl + 1)], vblk,
                                   start=True, stop=True)
                ins.then_inc(wt, 1)
                gk += 1
                t.wait_ge(wv, gk)
                for hl in range(8):
                    j, hp = hl // 2, hl % 2
                    cb = 64 * hp
                    ins = t.matmul(PS[pb:pb + 64, 512 * hl:512 * hl + 64],
                                   rt_t[cb:cb + 64, pr, 64 * j:64 * (j + 1)],
                                   sbf[cb:cb + 64, 64 * j:64 * (j + 1)],
                                   start=True, stop=True)
                ins.then_inc(wt, 1)
                gk += 1
                t.wait_ge(wv, gk)
                for hl in range(8):
                    j, hp = hl // 2, hl % 2
                    vblk = v[pb:pb + 64, tt * 1024 + 512 * grp + 64 * hl:
                             tt * 1024 + 512 * grp + 64 * (hl + 1)]
                    ins = t.matmul(PS[64 * hp:64 * hp + 64, 512 * hl:512 * hl + 64],
                                   khtok[pb:pb + 64, 128 * j + 64 * hp:128 * j + 64 * (hp + 1)],
                                   vblk, start=True, stop=True)
                ins.then_inc(wt, 1)
                gk += 1
            t.wait_ge(wv, WVSIN)
            for n, (g2, tb, hl) in enumerate(corr_order):
                j, hp = hl // 2, hl % 2
                cb = 64 * hp
                if n >= 8:
                    t.wait_ge(cv, n - 7)
                t.matmul(PS[:, 512 * hl:512 * hl + 64],
                         Rhat[cb:cb + 64, 4 * g2 + j, 128 * tb:128 * (tb + 1)],
                         Sinb[cb:cb + 64, 4 * g2 + j, :],
                         start=True, stop=True).then_inc(ct, 1)


def build_l0back():
    nc = bass.Bass("TRN2", target_bir_lowering=False)
    io = dict(
        y_in=nc.dram_tensor("y_in", [T, C], BF16, kind="ExternalInput"),
        g_in=nc.dram_tensor("g_in", [T, C], BF16, kind="ExternalInput"),
        x_in=nc.dram_tensor("x_in", [T, C], F32, kind="ExternalInput"),
        lngr=nc.dram_tensor("lngr", [128, C], F32, kind="ExternalInput"),
        lnbr=nc.dram_tensor("lnbr", [128, C], F32, kind="ExternalInput"),
        Wo=nc.dram_tensor("Wo", [C, C], F32, kind="ExternalInput"),
        ident=nc.dram_tensor("ident", [128, 128], BF16, kind="ExternalInput"),
        x1_o=nc.dram_tensor("x1_o", [T, C], F32, kind="ExternalOutput"),
    )
    with ExitStack() as ES:
        stgs = ES.enter_context(nc.semaphore("stgs"))
        build_l0back_body(nc, ES, io, "", stgs, 0)
    return nc


def build_l0back_body(nc, ES, io, px, stg, kst):
    y_d, g_d, x_d = io["y_in"], io["g_in"], io["x_in"]
    lng_d, lnb_d, Wo_d, id_d = io["lngr"], io["lnbr"], io["Wo"], io["ident"]
    x1_o = io["x1_o"]

    def S(n):
        return ES.enter_context(nc.semaphore(px + n))

    with ExitStack() as ctx:
        E = ctx.enter_context
        ys = E(nc.sbuf_tensor(px + "ys", [128, 16, C], BF16))
        gs = E(nc.sbuf_tensor(px + "gs", [128, 16, C], BF16))
        xs = E(nc.sbuf_tensor(px + "xs", [128, 16, C], F32))
        yng = E(nc.sbuf_tensor(px + "yng", [128, 2, C], BF16))
        yngT = E(nc.sbuf_tensor(px + "yngT", [128, 8, T], BF16))
        wos = E(nc.sbuf_tensor(px + "wos", [128, 8, C], BF16))
        lngr = E(nc.sbuf_tensor(px + "lngr_s", [128, C], F32))
        lnbr = E(nc.sbuf_tensor(px + "lnbr_s", [128, C], F32))
        idt = E(nc.sbuf_tensor(px + "idt", [128, 128], BF16))
        scrA = E(nc.sbuf_tensor(px + "scrA", [128, C], F32))
        scrB = E(nc.sbuf_tensor(px + "scrB", [128, C], F32))
        lngg = E(nc.sbuf_tensor(px + "lngg", [128, C], BF16))
        lnbg = E(nc.sbuf_tensor(px + "lnbg", [128, C], BF16))
        gsum = E(nc.sbuf_tensor(px + "gsum", [128, 16], F32))
        gssq = E(nc.sbuf_tensor(px + "gssq", [128, 16], F32))
        gmean = E(nc.sbuf_tensor(px + "gmean", [128, 16], F32))
        gvar = E(nc.sbuf_tensor(px + "gvar", [128, 16], F32))
        grstd = E(nc.sbuf_tensor(px + "grstd", [128, 16], F32))
        sums = E(nc.sbuf_tensor(px + "sums", [128, 16], F32))
        ssq = E(nc.sbuf_tensor(px + "ssq", [128, 16], F32))
        mean2 = E(nc.sbuf_tensor(px + "mean2", [128, 16], F32))
        var2 = E(nc.sbuf_tensor(px + "var2", [128, 16], F32))
        rstd2 = E(nc.sbuf_tensor(px + "rstd2", [128, 16], F32))
        PS = E(nc.psum_tensor(px + "PS", [128, 4096], F32))
        ldg = S("ldg")
        ld = S("ld")
        vv = S("vv")
        ss = S("ss")
        wt = S("wt")
        wvv = S("wvv")
        wvs = S("wvs")
        sto = S("sto")
        block = E(nc.Block())
        PSB = PS.ap().bitcast(BF16)
        nvst = [0]
        nsst = [0]
        W = Waves()

        def vedge(v):
            nvst[0] += 1
            v.sem_inc(vv, 0)
        for tt in range(16):
            def cf_stats(v, tt=tt):
                v.tensor_tensor(scrA[:, :], ys[:, tt, :], ys[:, tt, :],
                                ALU.mult).then_inc(vv, 1)
                nvst[0] += 1
                v.wait_ge(vv, nvst[0])
                v.reduce_sum(gsum[:, :], ys[:, tt, :].rearrange("p (h q) -> p h q", h=16),
                             mybir.AxisListType.X)
                v.reduce_sum(gssq[:, :], scrA.ap().rearrange("p (h q) -> p h q", h=16),
                             mybir.AxisListType.X).then_inc(vv, 1)
                nvst[0] += 1
                v.wait_ge(vv, nvst[0])
                v.tensor_scalar(gmean[:, :], gsum[:, :], 1.0 / 64, None, ALU.mult)
                v.tensor_scalar(gvar[:, :], gssq[:, :], 1.0 / 64, None,
                                ALU.mult).then_inc(vv, 1)
                nvst[0] += 1
                v.wait_ge(vv, nvst[0])
                v.tensor_tensor(scrB[:, 0:16], gmean[:, :], gmean[:, :],
                                ALU.mult).then_inc(vv, 1)
                nvst[0] += 1
                v.wait_ge(vv, nvst[0])
                v.tensor_tensor(gvar[:, :], gvar[:, :], scrB[:, 0:16],
                                ALU.subtract).then_inc(vv, 1)
                nvst[0] += 1
                v.wait_ge(vv, nvst[0])
                return v.tensor_scalar(gvar[:, :], gvar[:, :], 0.00064, None, ALU.add)
            W.wave('v', None, cf_stats)

            def cf_rstd(s, tt=tt):
                s.activation(grstd[:, :], gvar[:, :], AF.Ln).then_inc(ss, 1)
                nsst[0] += 1
                s.wait_ge(ss, nsst[0])
                return s.activation(grstd[:, :], grstd[:, :], AF.Exp, scale=-0.5)
            W.wave('s', None, cf_rstd)

            def cf_yng(v, tt=tt):
                v.tensor_tensor(lngg[:, :], lngr[:, :], gs[:, tt, :], ALU.mult)
                v.tensor_tensor(lnbg[:, :], lnbr[:, :], gs[:, tt, :], ALU.mult)
                v.tensor_tensor(scrA.ap().rearrange("p (h q) -> p h q", h=16),
                                ys[:, tt, :].rearrange("p (h q) -> p h q", h=16),
                                gmean.ap().rearrange("p (h one) -> p h one", one=1)
                                .broadcast_to([128, 16, 64]), ALU.subtract).then_inc(vv, 1)
                nvst[0] += 1
                v.wait_ge(vv, nvst[0])
                v.tensor_tensor(scrA.ap().rearrange("p (h q) -> p h q", h=16),
                                scrA.ap().rearrange("p (h q) -> p h q", h=16),
                                grstd.ap().rearrange("p (h one) -> p h one", one=1)
                                .broadcast_to([128, 16, 64]), ALU.mult).then_inc(vv, 1)
                nvst[0] += 1
                v.wait_ge(vv, nvst[0])
                v.tensor_tensor(scrA[:, :], scrA[:, :], lngg[:, :], ALU.mult).then_inc(vv, 1)
                nvst[0] += 1
                v.wait_ge(vv, nvst[0])
                return v.tensor_tensor(yng[:, tt % 2, :], scrA[:, :], lnbg[:, :], ALU.add)
            W.wave('v', None, cf_yng)

            def tf_tr(t, tt=tt):
                for n in range(8):
                    ins = t.transpose(PSB[:, 7 * 1024 + 128 * n:7 * 1024 + 128 * (n + 1)],
                                      yng[:, tt % 2, 128 * n:128 * (n + 1)], idt[:, :])
                return ins

            def cf_tr(v, tt=tt):
                return v.tensor_copy(yngT[:, :, 128 * tt:128 * (tt + 1)],
                                     PSB[:, 7 * 1024:8 * 1024]
                                     .rearrange("p (n q) -> p n q", n=8))
            W.wave('v', tf_tr, cf_tr)

        for tt in range(16):
            pp = tt % 2

            def tf_mm(t, tt=tt, pp=pp):
                if tt == 0:
                    t.wait_ge(ldg, 16 * NLOADS_B[0])
                for cc in range(8):
                    for half in range(2):
                        ins = t.matmul(
                            PS[:, 2048 * pp + 512 * half:2048 * pp + 512 * (half + 1)],
                            yngT[:, cc, 128 * tt:128 * (tt + 1)],
                            wos[:, cc, 512 * half:512 * (half + 1)],
                            start=(cc == 0), stop=(cc == 7))
                return ins

            def cf_mm(v, tt=tt, pp=pp):
                v.tensor_tensor(xs[:, tt, :], PS[:, 2048 * pp:2048 * pp + 1024],
                                xs[:, tt, :], ALU.add).then_inc(vv, 1)
                nvst[0] += 1
                v.wait_ge(vv, nvst[0])
                v.tensor_tensor(scrA[:, :], xs[:, tt, :], xs[:, tt, :],
                                ALU.mult).then_inc(vv, 1)
                nvst[0] += 1
                v.wait_ge(vv, nvst[0])
                v.reduce_sum(sums[:, tt:tt + 1],
                             xs[:, tt, :].rearrange("p (one q) -> p one q", one=1),
                             mybir.AxisListType.X)
                return v.reduce_sum(ssq[:, tt:tt + 1],
                                    scrA.ap().rearrange("p (one q) -> p one q", one=1),
                                    mybir.AxisListType.X)
            W.wave('v', tf_mm, cf_mm)

        def cf_c(v):
            v.tensor_scalar(mean2[:, :], sums[:, :], 1.0 / C, None, ALU.mult)
            v.tensor_scalar(var2[:, :], ssq[:, :], 1.0 / C, None, ALU.mult).then_inc(vv, 1)
            nvst[0] += 1
            v.wait_ge(vv, nvst[0])
            v.tensor_tensor(scrB[:, 0:16], mean2[:, :], mean2[:, :], ALU.mult).then_inc(vv, 1)
            nvst[0] += 1
            v.wait_ge(vv, nvst[0])
            v.tensor_tensor(var2[:, :], var2[:, :], scrB[:, 0:16],
                            ALU.subtract).then_inc(vv, 1)
            nvst[0] += 1
            v.wait_ge(vv, nvst[0])
            return v.tensor_scalar(var2[:, :], var2[:, :], 1e-5, None, ALU.add)
        W.wave('v', None, cf_c)

        def cf_d(s):
            s.activation(rstd2[:, :], var2[:, :], AF.Ln).then_inc(ss, 1)
            nsst[0] += 1
            s.wait_ge(ss, nsst[0])
            return s.activation(rstd2[:, :], rstd2[:, :], AF.Exp, scale=-0.5)
        W.wave('s', None, cf_d)

        def cf_e(v):
            for tt in range(16):
                ins = v.tensor_scalar(xs[:, tt, :], xs[:, tt, :], mean2[:, tt:tt + 1],
                                      rstd2[:, tt:tt + 1], ALU.subtract, ALU.mult)
            return ins
        W.wave('v', None, cf_e)

        NW = len(W.items)
        NLOADS_B = [5]

        @block.gpsimd
        def _(g):
            g.wait_ge(stg, kst)
            g.dma_start(out=ys.ap(), in_=y_d.ap().rearrange("(tt p) c -> p tt c", p=128)
                        ).then_inc(ldg, 16)
            g.dma_start(out=gs.ap(), in_=g_d.ap().rearrange("(tt p) c -> p tt c", p=128)
                        ).then_inc(ldg, 16)
            g.dma_start(out=wos.ap(), in_=Wo_d.ap().rearrange("(cc p) c -> p cc c", p=128)
                        ).then_inc(ldg, 16)
            g.dma_start(out=idt.ap(), in_=id_d.ap()).then_inc(ldg, 16)
            g.dma_start(out=xs.ap(), in_=x_d.ap().rearrange("(tt p) c -> p tt c", p=128)
                        ).then_inc(ldg, 16)

        @block.sync
        def _(s):
            s.wait_ge(stg, kst)
            s.dma_start(out=lngr.ap(), in_=lng_d.ap()).then_inc(ld, 16)
            s.dma_start(out=lnbr.ap(), in_=lnb_d.ap()).then_inc(ld, 16)
            nvec = sum(1 for it in W.items if it[1] == 'v')
            s.wait_ge(wvv, nvec)
            s.dma_start(out=x1_o.ap().rearrange("(tt p) c -> p tt c", p=128),
                        in_=xs.ap()).then_inc(sto, 16)
            s.wait_ge(sto, 16)
            s.sem_inc(stg, 1)

        @block.vector
        def _(v):
            v.wait_ge(stg, kst)
            v.wait_ge(ldg, 16 * 5)
            v.wait_ge(ld, 32)
            for k, (tfn, ceng, cfn, lag) in enumerate(W.items):
                if ceng != 'v':
                    continue
                v.wait_ge(wt, k + 1)
                cfn(v).then_inc(wvv, 1)

        @block.scalar
        def _(s):
            s.wait_ge(stg, kst)
            s.wait_ge(ldg, 16 * 5)
            for k, (tfn, ceng, cfn, lag) in enumerate(W.items):
                if ceng != 's':
                    continue
                s.wait_ge(wt, k + 1)
                cfn(s).then_inc(wvs, 1)

        @block.tensor
        def _(t):
            t.wait_ge(stg, kst)
            engs = [it[1] for it in W.items]
            for k, (tfn, ceng, cfn, lag) in enumerate(W.items):
                lim = max(0, k - lag)
                nvec = sum(1 for e in engs[:lim] if e == 'v')
                nsca = lim - nvec
                if nvec:
                    t.wait_ge(wvv, nvec)
                if nsca:
                    t.wait_ge(wvs, nsca)
                if tfn is not None:
                    tfn(t).then_inc(wt, 1)
                else:
                    t.sem_inc(wt, 1)


def build_l0ffn():
    nc = bass.Bass("TRN2", target_bir_lowering=False)
    io = dict(
        x1=nc.dram_tensor("x1", [T, C], F32, kind="ExternalInput"),
        W1a=nc.dram_tensor("W1a", [10, C, 4096], F32, kind="ExternalInput"),
        W2a=nc.dram_tensor("W2a", [10, 4096, C], F32, kind="ExternalInput"),
        b1a=nc.dram_tensor("b1a", [128, 10, 32], F32, kind="ExternalInput"),
        b2a=nc.dram_tensor("b2a", [10, C], F32, kind="ExternalInput"),
        cent=nc.dram_tensor("cent", [C, 8], F32, kind="ExternalInput"),
        gsc=nc.dram_tensor("gsc", [128, 8], F32, kind="ExternalInput"),
        grc=nc.dram_tensor("grc", [128, 8], F32, kind="ExternalInput"),
        ident=nc.dram_tensor("ident", [128, 128], BF16, kind="ExternalInput"),
        x2_o=nc.dram_tensor("x2_o", [T, C], F32, kind="ExternalOutput"),
    )
    with ExitStack() as ES:
        stgs = ES.enter_context(nc.semaphore("stgs"))
        build_l0ffn_body(nc, ES, io, "", stgs, 0)
    return nc


def build_l0ffn_body(nc, ES, io, px, stg, kst):
    x1_d, W1_d, W2_d, b1_d, b2_d = (io["x1"], io["W1a"], io["W2a"],
                                    io["b1a"], io["b2a"])
    cent_d, gs_d, gr_d, id_d = io["cent"], io["gsc"], io["grc"], io["ident"]
    x2_o = io["x2_o"]

    def S(n):
        return ES.enter_context(nc.semaphore(px + n))

    with ExitStack() as ctx:
        E = ctx.enter_context
        accum = E(nc.sbuf_tensor(px + "accum", [128, 16, C], F32))
        xnT = E(nc.sbuf_tensor(px + "xnT", [128, 8, T], BF16))
        hTq = E(nc.sbuf_tensor(px + "hTq", [128, 8, T], BF16))
        w1q = E(nc.sbuf_tensor(px + "w1q", [128, 1, 8192], BF16))
        w2q = E(nc.sbuf_tensor(px + "w2q", [128, 1, 8192], BF16))
        b1t = E(nc.sbuf_tensor(px + "b1t", [128, 10, 32], F32))
        b2row = E(nc.sbuf_tensor(px + "b2row", [1, 1, C], BF16))
        onesr = E(nc.sbuf_tensor(px + "onesr", [1, 128], BF16))
        cents = E(nc.sbuf_tensor(px + "cents", [128, 8, 8], BF16))
        gsr = E(nc.sbuf_tensor(px + "gsc_s", [128, 8], F32))
        grr = E(nc.sbuf_tensor(px + "grc_s", [128, 8], F32))
        identf = E(nc.sbuf_tensor(px + "identf", [128, 128], F32))
        scrA = E(nc.sbuf_tensor(px + "scrA", [128, C], F32))
        scb = E(nc.sbuf_tensor(px + "scb", [8, 512], F32))
        sco = E(nc.sbuf_tensor(px + "sco", [128, 16, 8], F32))
        gates = E(nc.sbuf_tensor(px + "gates", [128, 16, 10], F32))
        m1t = E(nc.sbuf_tensor(px + "m1t", [128, 16], F32))
        m2t = E(nc.sbuf_tensor(px + "m2t", [128, 16], F32))
        rsum = E(nc.sbuf_tensor(px + "rsum", [128, 16], F32))
        selt = E(nc.sbuf_tensor(px + "selt", [128, 16, 8], F32))
        ssq = E(nc.sbuf_tensor(px + "ssqf", [128, 16], F32))
        rs = E(nc.sbuf_tensor(px + "rsf", [128, 16], F32))
        rs2 = E(nc.sbuf_tensor(px + "rs2f", [128, 16], F32))
        sums = E(nc.sbuf_tensor(px + "sums", [128, 16], F32))
        mean2 = E(nc.sbuf_tensor(px + "mean2", [128, 16], F32))
        var2 = E(nc.sbuf_tensor(px + "var2", [128, 16], F32))
        rstd2 = E(nc.sbuf_tensor(px + "rstd2", [128, 16], F32))
        PS = E(nc.psum_tensor(px + "PS", [128, 4096], F32))
        ldg = S("ldg")
        ww0 = S("ww0")
        ww1 = S("ww1")
        wwp = [ww0, ww1]
        bb0 = S("bb0")
        bb1 = S("bb1")
        bbp = [bb0, bb1]
        ld = S("ld")
        vv = S("vv")
        ss = S("ss")
        wt = S("wt")
        wvv = S("wvv")
        wvs = S("wvs")
        sto = S("sto")
        block = E(nc.Block())
        PSB = PS.ap().bitcast(BF16)
        nvst = [0]
        nsst = [0]
        W = Waves()
        EPS_RMS = 1.1920929e-07
        QEND = {}
        BUSE = [0, 0]

        def vedge(v):
            nvst[0] += 1
            v.wait_ge(vv, nvst[0])

        for tt in range(16):
            def cf(v, tt=tt):
                v.tensor_tensor(scrA[:, :], accum[:, tt, :], accum[:, tt, :],
                                ALU.mult).then_inc(vv, 1)
                vedge(v)
                return v.reduce_sum(ssq[:, tt:tt + 1],
                                    scrA.ap().rearrange("p (one q) -> p one q", one=1),
                                    mybir.AxisListType.X)
            W.wave('v', None, cf)

        def cf1(v):
            return v.tensor_scalar(rs[:, :], ssq[:, :], 1.0 / C, EPS_RMS, ALU.mult, ALU.add)
        W.wave('v', None, cf1)

        def cf2(s):
            s.activation(rs[:, :], rs[:, :], AF.Ln).then_inc(ss, 1)
            nsst[0] += 1
            s.wait_ge(ss, nsst[0])
            return s.activation(rs[:, :], rs[:, :], AF.Exp, scale=-0.5)
        W.wave('s', None, cf2)

        def cf3(v):
            return v.tensor_scalar(rs2[:, :], rs[:, :], 0.5, None, ALU.mult)
        W.wave('v', None, cf3)

        for tt in range(16):
            def tf(t, tt=tt):
                if tt == 0:
                    t.wait_ge(ldg, 16 * 5)
                    t.wait_ge(ld, 16)
                for n in range(8):
                    ins = t.transpose(PS[:, 3072 + 128 * n:3072 + 128 * (n + 1)],
                                      accum[:, tt, 128 * n:128 * (n + 1)], identf[:, :])
                return ins

            def cf(v, tt=tt):
                return v.tensor_copy(hTq[:, :, 128 * tt:128 * (tt + 1)],
                                     PS[:, 3072:4096].rearrange("p (n q) -> p n q", n=8))
            W.wave('v', tf, cf)

        for tc in range(4):
            def tf(t, tc=tc):
                for cc in range(8):
                    ins = t.matmul(PS[0:8, 2560:2560 + 512], cents[:, cc, :],
                                   hTq[:, cc, 512 * tc:512 * (tc + 1)],
                                   start=(cc == 0), stop=(cc == 7))
                return ins

            def cf(s, tc=tc):
                return s.activation(scb[0:8, :], PS[0:8, 2560:2560 + 512], AF.Sigmoid)
            W.wave('s', tf, cf)

            def tf2(t, tc=tc):
                for n in range(4):
                    ins = t.transpose(PS[:, 2048 + 8 * n:2048 + 8 * (n + 1)],
                                      scb[0:8, 128 * n:128 * (n + 1)],
                                      identf[0:8, 0:8])
                return ins

            def cf2(v, tc=tc):
                return v.tensor_copy(sco[:, 4 * tc:4 * (tc + 1), :],
                                     PS[:, 2048:2048 + 32].rearrange("p (n q) -> p n q", n=4))
            W.wave('v', tf2, cf2)

        def cf4(v):
            v.memset(gates[:, :, 0:2], 1.0)
            v.reduce_max(m1t[:, :], sco.ap(), mybir.AxisListType.X)
            v.reduce_sum(rsum[:, :], sco.ap(), mybir.AxisListType.X).then_inc(vv, 1)
            vedge(v)
            v.tensor_tensor(selt.ap(), sco.ap(),
                            m1t.ap().rearrange("p (h one) -> p h one", one=1)
                            .broadcast_to([128, 16, 8]), ALU.is_ge).then_inc(vv, 1)
            vedge(v)
            v.scalar_tensor_tensor(selt.ap(), selt.ap(), -1e30, sco.ap(),
                                   ALU.mult, ALU.add).then_inc(vv, 1)
            vedge(v)
            v.reduce_max(m2t[:, :], selt.ap(), mybir.AxisListType.X).then_inc(vv, 1)
            vedge(v)
            v.tensor_tensor(selt.ap(), sco.ap(),
                            m2t.ap().rearrange("p (h one) -> p h one", one=1)
                            .broadcast_to([128, 16, 8]), ALU.is_ge)
            v.reciprocal(rsum[:, :], rsum[:, :]).then_inc(vv, 1)
            vedge(v)
            v.tensor_tensor(selt.ap(), selt.ap(), sco.ap(), ALU.mult).then_inc(vv, 1)
            vedge(v)
            v.tensor_tensor(gates[:, :, 2:10], selt.ap(),
                            rsum.ap().rearrange("p (h one) -> p h one", one=1)
                            .broadcast_to([128, 16, 8]), ALU.mult).then_inc(vv, 1)
            vedge(v)
            return v.tensor_scalar(accum.ap(), accum.ap(), 2.0, None, ALU.mult)
        W.wave('v', None, cf4)

        GQ = [0]

        def xnT_waves(rs_t):
            for tt in range(16):
                def cfa(v, tt=tt, rs_t=rs_t):
                    return v.tensor_scalar(scrA[:, :], accum[:, tt, :], rs_t[:, tt:tt + 1],
                                           None, ALU.mult)
                W.wave('v', None, cfa)

                def tfb(t, tt=tt):
                    for n in range(8):
                        ins = t.transpose(PS[:, 3072 + 128 * n:3072 + 128 * (n + 1)],
                                          scrA[:, 128 * n:128 * (n + 1)], identf[:, :])
                    return ins

                def cfb(v, tt=tt):
                    return v.tensor_copy(xnT[:, :, 128 * tt:128 * (tt + 1)],
                                         PS[:, 3072:4096].rearrange("p (n q) -> p n q", n=8))
                W.wave('v', tfb, cfb)

        def expert_waves(e):
            gain = gsr if e < 2 else grr
            for q in range(4):
                gq = GQ[0]
                GQ[0] += 1
                sl = 0

                def cfw(v, gq=gq, e=e, gain=gain):
                    v.wait_ge(wwp[0], 32 * (gq + 1))
                    if gq % 4 == 0:
                        v.wait_ge(bbp[0], 16 * (e + 1))
                    return v.tensor_tensor(
                        w1q[:, 0, :].rearrange("p (cg h) -> p cg h", cg=8),
                        w1q[:, 0, :].rearrange("p (cg h) -> p cg h", cg=8),
                        gain.ap().rearrange("p (cg one) -> p cg one", one=1)
                        .broadcast_to([128, 8, 1024]), ALU.mult)
                W.wave('v', None, cfw)
                for ht in range(8):
                    for th in range(2):
                        gu = gq * 16 + ht * 2 + th
                        pt = gu % 4
                        unit_i = ht * 2 + th

                        def tf(t, sl=sl, ht=ht, th=th, pt=pt, gq=gq, e=e):
                            for cg in range(8):
                                for half in range(2):
                                    ins = t.matmul(
                                        PS[:, 1024 * pt + 512 * half:1024 * pt + 512 * (half + 1)],
                                        w1q[:, 0, 1024 * cg + 128 * ht:1024 * cg + 128 * (ht + 1)],
                                        xnT[:, cg, 1024 * th + 512 * half:1024 * th + 512 * (half + 1)],
                                        start=(cg == 0), stop=(cg == 7))
                            return ins

                        def cf(s, e=e, q=q, ht=ht, th=th, pt=pt):
                            return s.activation(hTq[:, ht, 1024 * th:1024 * (th + 1)],
                                                PS[:, 1024 * pt:1024 * (pt + 1)],
                                                AF.Gelu, bias=b1t[:, e, 8 * q + ht:8 * q + ht + 1])
                        W.wave('s', tf, cf, lag=(unit_i if unit_i < 3 else 3))
                for tt in range(16):
                    gt = gq * 16 + tt
                    p2 = gt % 2

                    def tf(t, sl=sl, tt=tt, p2=p2, e=e, q=q):
                        for hc in range(8):
                            for half in range(2):
                                ins = t.matmul(
                                    PS[:, 1024 * p2 + 512 * half:1024 * p2 + 512 * (half + 1)],
                                    hTq[:, hc, 128 * tt:128 * (tt + 1)],
                                    w2q[:, 0, 1024 * hc + 512 * half:1024 * hc + 512 * (half + 1)],
                                    start=(hc == 0), stop=(q != 0 and hc == 7))
                        if q == 0:
                            for half in range(2):
                                ins = t.matmul(
                                    PS[:, 1024 * p2 + 512 * half:1024 * p2 + 512 * (half + 1)],
                                    onesr[0:1, 0:128],
                                    b2row[0:1, 0, 512 * half:512 * (half + 1)],
                                    start=False, stop=True)
                        return ins

                    def cf(v, tt=tt, p2=p2, e=e):
                        return v.scalar_tensor_tensor(
                            accum[:, tt, :], PS[:, 1024 * p2:1024 * (p2 + 1)],
                            gates[:, tt, e:e + 1], accum[:, tt, :], ALU.mult, ALU.add)
                    W.wave('v', tf, cf, lag=(0 if tt == 0 else 1))
                QEND[gq] = len(W.items)

        import os as _os
        _nexp = int(_os.environ.get("L0FFN_NEXP", "10"))
        xnT_waves(rs2)
        for e in range(_nexp):
            expert_waves(e)

        _noln = _os.environ.get("L0FFN_NOLN") == "1"
        for tt in range(16):
            def cf(v, tt=tt):
                v.tensor_tensor(scrA[:, :], accum[:, tt, :], accum[:, tt, :],
                                ALU.mult).then_inc(vv, 1)
                vedge(v)
                v.reduce_sum(sums[:, tt:tt + 1],
                             accum[:, tt, :].rearrange("p (one q) -> p one q", one=1),
                             mybir.AxisListType.X)
                return v.reduce_sum(ssq[:, tt:tt + 1],
                                    scrA.ap().rearrange("p (one q) -> p one q", one=1),
                                    mybir.AxisListType.X)
            W.wave('v', None, cf)

        def cfc(v):
            v.tensor_scalar(mean2[:, :], sums[:, :], 1.0 / C, None, ALU.mult)
            v.tensor_scalar(var2[:, :], ssq[:, :], 1.0 / C, None, ALU.mult).then_inc(vv, 1)
            vedge(v)
            v.tensor_tensor(scrA[:, 0:16], mean2[:, :], mean2[:, :], ALU.mult).then_inc(vv, 1)
            vedge(v)
            v.tensor_tensor(var2[:, :], var2[:, :], scrA[:, 0:16], ALU.subtract).then_inc(vv, 1)
            vedge(v)
            return v.tensor_scalar(var2[:, :], var2[:, :], 1e-5, None, ALU.add)
        W.wave('v', None, cfc)

        def cfd(s):
            s.activation(rstd2[:, :], var2[:, :], AF.Ln).then_inc(ss, 1)
            nsst[0] += 1
            s.wait_ge(ss, nsst[0])
            return s.activation(rstd2[:, :], rstd2[:, :], AF.Exp, scale=-0.5)
        W.wave('s', None, cfd)

        def cfe(v):
            for tt in range(16):
                if _noln:
                    ins = v.memset(mean2[:, tt:tt + 1], 0.0)
                else:
                    ins = v.tensor_scalar(accum[:, tt, :], accum[:, tt, :], mean2[:, tt:tt + 1],
                                          rstd2[:, tt:tt + 1], ALU.subtract, ALU.mult)
            return ins
        W.wave('v', None, cfe)

        @block.gpsimd
        def _(g):
            g.wait_ge(stg, kst)
            g.dma_start(out=accum.ap(), in_=x1_d.ap().rearrange("(tt p) c -> p tt c", p=128)
                        ).then_inc(ldg, 16)
            g.dma_start(out=cents.ap(), in_=cent_d.ap().rearrange("(cc p) e -> p cc e", p=128)
                        ).then_inc(ldg, 16)
            g.dma_start(out=identf.ap(), in_=id_d.ap()).then_inc(ldg, 16)
            g.dma_start(out=gsr.ap(), in_=gs_d.ap()).then_inc(ldg, 16)
            g.dma_start(out=grr.ap(), in_=gr_d.ap()).then_inc(ldg, 16)
            import os as _os
            for e in range(int(_os.environ.get("L0FFN_NEXP", "10"))):
                for q in range(4):
                    gq = 4 * e + q
                    if gq >= 1:
                        g.wait_ge(wt, QEND[gq - 1])
                    g.dma_start(out=w1q[:, 0, :].rearrange("p (cg h) -> p cg h", cg=8),
                                in_=W1_d.ap()[e].rearrange("(cg p) h -> p cg h", p=128)
                                [:, :, 1024 * q:1024 * (q + 1)]).then_inc(wwp[0], 16)
                    g.dma_start(out=w2q[:, 0, :].rearrange("p (hc c) -> p hc c", hc=8),
                                in_=W2_d.ap()[e].rearrange("(q hc p) c -> q p hc c", p=128, hc=8)[q]
                                ).then_inc(wwp[0], 16)
                    if q == 0:
                        g.dma_start(out=b2row[0:1, 0, :],
                                    in_=b2_d.ap()[e:e + 1, :]).then_inc(bbp[0], 16)

        @block.sync
        def _(s):
            s.wait_ge(stg, kst)
            s.dma_start(out=b1t.ap(), in_=b1_d.ap()).then_inc(ld, 16)
            nvec = sum(1 for it in W.items if it[1] == 'v')
            s.wait_ge(wvv, nvec)
            s.dma_start(out=x2_o.ap().rearrange("(tt p) c -> p tt c", p=128),
                        in_=accum.ap()).then_inc(sto, 16)
            s.wait_ge(sto, 16)
            s.sem_inc(stg, 1)

        @block.vector
        def _(v):
            v.wait_ge(stg, kst)
            v.wait_ge(ldg, 16 * 5)
            v.wait_ge(ld, 16)
            v.memset(onesr[0:1, :], 1.0)
            for k, (tfn, ceng, cfn, lag) in enumerate(W.items):
                if ceng != 'v':
                    continue
                v.wait_ge(wt, k + 1)
                cfn(v).then_inc(wvv, 1)

        @block.scalar
        def _(s):
            s.wait_ge(stg, kst)
            s.wait_ge(ldg, 16 * 5)
            s.wait_ge(ld, 16)
            for k, (tfn, ceng, cfn, lag) in enumerate(W.items):
                if ceng != 's':
                    continue
                s.wait_ge(wt, k + 1)
                cfn(s).then_inc(wvs, 1)

        @block.tensor
        def _(t):
            t.wait_ge(stg, kst)
            engs = [it[1] for it in W.items]
            for k, (tfn, ceng, cfn, lag) in enumerate(W.items):
                lim = max(0, k - lag)
                nvec = sum(1 for e in engs[:lim] if e == 'v')
                nsca = lim - nvec
                if nvec:
                    t.wait_ge(wvv, nvec)
                if nsca:
                    t.wait_ge(wvs, nsca)
                if tfn is not None:
                    tfn(t).then_inc(wt, 1)
                else:
                    t.sem_inc(wt, 1)


def build_l1attn():
    nc = bass.Bass("TRN2", target_bir_lowering=False)
    io = dict(
        x_in=nc.dram_tensor("x_in", [T, C], F32, kind="ExternalInput"),
        Wq=nc.dram_tensor("Wq", [C, C], F32, kind="ExternalInput"),
        Wk=nc.dram_tensor("Wk", [C, C], F32, kind="ExternalInput"),
        Wv=nc.dram_tensor("Wv", [C, C], F32, kind="ExternalInput"),
        Wo=nc.dram_tensor("Wo", [C, C], F32, kind="ExternalInput"),
        betas=nc.dram_tensor("betas", [64, 8], F32, kind="ExternalInput"),
        mem0z=nc.dram_tensor("mem0z", [64, 8 * 65], F32, kind="ExternalInput"),
        maskT=nc.dram_tensor("maskT", [128, 4 * 512], F32, kind="ExternalInput"),
        ident=nc.dram_tensor("ident", [128, 128], BF16, kind="ExternalInput"),
        x3_o=nc.dram_tensor("x3_o", [T, C], F32, kind="ExternalOutput"),
        qbuf=nc.dram_tensor("qbuf", [512 * 8, 64], BF16, kind="Internal"),
        kbuf=nc.dram_tensor("kbuf", [512 * 8, 64], BF16, kind="Internal"),
        vbuf=nc.dram_tensor("vbuf", [512 * 8, 64], BF16, kind="Internal"),
        abuf=nc.dram_tensor("abuf", [512 * 8, 64], BF16, kind="Internal"),
    )
    with ExitStack() as ES:
        stgs = ES.enter_context(nc.semaphore("stgs"))
        build_l1attn_body(nc, ES, io, "", stgs, 0)
    return nc


def build_l1attn_body(nc, ES, io, px, stg, kst):
    x_d, Wq_d, Wk_d, Wv_d, Wo_d = (io["x_in"], io["Wq"], io["Wk"],
                                   io["Wv"], io["Wo"])
    bet_d, m0_d, mt_d, id_d = io["betas"], io["mem0z"], io["maskT"], io["ident"]
    x3_o = io["x3_o"]
    qb_d, kb_d, vb_d, ab_d = io["qbuf"], io["kbuf"], io["vbuf"], io["abuf"]

    def S(n):
        return ES.enter_context(nc.semaphore(px + n))

    with ExitStack() as ctx:
        E = ctx.enter_context
        xseg = E(nc.sbuf_tensor(px + "xseg", [128, 2, C], F32))
        xTs = E(nc.sbuf_tensor(px + "xTsg", [128, 8, 256], BF16))
        qsg = E(nc.sbuf_tensor(px + "qsg", [128, 2, C], BF16))
        ksg = E(nc.sbuf_tensor(px + "ksg", [128, 2, C], BF16))
        vsg = E(nc.sbuf_tensor(px + "vsg", [128, 2, C], BF16))
        Qall = E(nc.sbuf_tensor(px + "Qall", [128, 32, 64], BF16))
        Kall = E(nc.sbuf_tensor(px + "Kall", [128, 32, 64], BF16))
        V65 = E(nc.sbuf_tensor(px + "V65", [128, 32, 65], BF16))
        skl = E(nc.sbuf_tensor(px + "skl", [128, 32, 64], BF16))
        QT = E(nc.sbuf_tensor(px + "QT", [64, 8, 512], BF16))
        KT = E(nc.sbuf_tensor(px + "KT", [64, 8, 512], BF16))
        sqT = E(nc.sbuf_tensor(px + "sqT", [64, 8, 512], BF16))
        scrq = E(nc.sbuf_tensor(px + "scrq", [64, 8, 512], BF16))
        PhT = E(nc.sbuf_tensor(px + "PhT", [128, 4, 512], BF16))
        maskT = E(nc.sbuf_tensor(px + "maskT_s", [128, 4, 512], BF16))
        tdot = E(nc.sbuf_tensor(px + "tdot", [64, 512], BF16))
        tmem = E(nc.sbuf_tensor(px + "tmem", [64, 512], BF16))
        am = E(nc.sbuf_tensor(px + "amx", [64, 512], BF16))
        recs = E(nc.sbuf_tensor(px + "recs", [128, 1024], BF16))
        atth = E(nc.sbuf_tensor(px + "atth", [64, 512], BF16))
        attI = E(nc.sbuf_tensor(px + "attI", [128, 32, 64], BF16))
        atts = E(nc.sbuf_tensor(px + "atts", [128, 2, C], BF16))
        attT = E(nc.sbuf_tensor(px + "attTs", [128, 8, 256], BF16))
        wq = E(nc.sbuf_tensor(px + "wqs", [128, 8, C], BF16))
        wk = E(nc.sbuf_tensor(px + "wks", [128, 8, C], BF16))
        wv = E(nc.sbuf_tensor(px + "wvs", [128, 8, C], BF16))
        wo = E(nc.sbuf_tensor(px + "wos", [128, 8, C], BF16))
        Mst = E(nc.sbuf_tensor(px + "Mst", [64, 8, 65], F32))
        Msb = E(nc.sbuf_tensor(px + "Msb", [64, 8, 65], BF16))
        gcol = E(nc.sbuf_tensor(px + "gcol", [64, 8], F32))
        g1col = E(nc.sbuf_tensor(px + "g1col", [64, 8], F32))
        onesb = E(nc.sbuf_tensor(px + "onesb", [128, 64], BF16))
        idt = E(nc.sbuf_tensor(px + "idt", [128, 128], BF16))
        identf = E(nc.sbuf_tensor(px + "identf", [128, 128], F32))
        sums = E(nc.sbuf_tensor(px + "sums", [128, 2], F32))
        ssq = E(nc.sbuf_tensor(px + "ssq", [128, 2], F32))
        mean2 = E(nc.sbuf_tensor(px + "mean2", [128, 2], F32))
        var2 = E(nc.sbuf_tensor(px + "var2", [128, 2], F32))
        rstd2 = E(nc.sbuf_tensor(px + "rstd2", [128, 2], F32))
        scrl = E(nc.sbuf_tensor(px + "scrl", [128, C], F32))
        PS = E(nc.psum_tensor(px + "PS", [128, 4096], F32))
        ldg = S("ldg")
        dq = S("dq")
        ld = S("ld")
        vv = S("vv")
        ss = S("ss")
        wt = S("wt")
        wvv = S("wvv")
        wvs = S("wvs")
        sto = S("sto")
        block = E(nc.Block())
        PSB = PS.ap().bitcast(BF16)
        nvst = [0]
        ssc = [0]
        W = Waves()

        def vedge(v):
            nvst[0] += 1
            v.wait_ge(vv, nvst[0])

        def nv_upto(k):
            return sum(1 for it in W.items[:k] if it[1] == 'v')

        def ns_upto(k):
            return sum(1 for it in W.items[:k] if it[1] == 's')

        gp = []
        ndq = [0]

        def gdma(fn, dq_bar=None, wt_wait=None, wvv_wait=None, wvs_wait=None):
            gp.append((dq_bar, wt_wait, wvv_wait, wvs_wait, fn))
            ndq[0] += 16
            return ndq[0]

        xloads = []

        def cfg(s):
            s.wait_ge(ld, 48)
            return s.activation(gcol[:, :], gcol[:, :], AF.Sigmoid)
        W.wave('s', None, cfg)

        def cfg2(v):
            v.memset(onesb[:, :], 1.0)
            v.memset(V65.ap()[:, :, 64:65], 1.0)
            return v.tensor_scalar(g1col[:, :], gcol[:, :], -1.0, 1.0, ALU.mult, ALU.add)
        W.wave('v', None, cfg2)

        seg_end_wave = [0]
        seg_ends = []
        for seg in range(8):
            myx = len(xloads)
            xloads.append((16 * (3 + myx + 1), nv_upto(seg_end_wave[0]) if seg >= 1 else None))

            def tf_xt(t, seg=seg, myx=myx):
                if seg == 0:
                    t.wait_ge(ldg, 16 * 7)
                t.wait_ge(ld, 16 * (3 + myx + 1))
                for u in range(2):
                    for n in range(8):
                        ins = t.transpose(PS[:, 1024 * u + 128 * n:1024 * u + 128 * (n + 1)],
                                          xseg[:, u, 128 * n:128 * (n + 1)], identf[:, :])
                return ins

            def cf_xt(v, seg=seg):
                return v.tensor_copy(
                    xTs.ap().rearrange("p cc (u q) -> p cc u q", u=2),
                    PS[:, 0:2048].rearrange("p (u n q) -> p n u q", u=2, n=8))
            W.wave('v', tf_xt, cf_xt)

            proj_wvs_start = None
            for pi, (wsb, outb) in enumerate([(wq, qsg), (wk, ksg), (wv, vsg)]):
                for u in range(2):
                    pp = (2 * pi + u) % 2

                    def tf(t, wsb=wsb, u=u, pp=pp):
                        for cin in range(8):
                            for half in range(2):
                                ins = t.matmul(
                                    PS[:, 2048 * pp + 512 * half:2048 * pp + 512 * (half + 1)],
                                    xTs[:, cin, 128 * u:128 * (u + 1)],
                                    wsb[:, cin, 512 * half:512 * (half + 1)],
                                    start=(cin == 0), stop=(cin == 7))
                        return ins

                    mydq = ndq[0] if seg >= 1 else None

                    def cf(s, outb=outb, u=u, pp=pp, mydq=mydq):
                        if mydq is not None and outb is qsg and u == 0:
                            s.wait_ge(dq, mydq)
                        return s.activation(outb[:, u, :], PS[:, 2048 * pp:2048 * pp + 1024],
                                            AF.Copy)
                    W.wave('s', tf, cf, lag=(0 if (pi == 0 and u == 0) else 1))
            proj_end = len(W.items)

            dq_before = ndq[0]
            for (srcb, dstd) in [(qsg, qb_d), (ksg, kb_d), (vsg, vb_d)]:
                def f(g, srcb=srcb, dstd=dstd):
                    return g.dma_start(
                        out=dstd.ap().rearrange("(u p cb) d -> p u (cb d)", p=128, cb=16),
                        in_=srcb.ap())
                gdma(f, wvs_wait=ns_upto(proj_end))
            outs_done = ndq[0]
            prev_tensor_end = seg_end_wave[0]
            for (srcd, dstb, wid) in [(qb_d, Qall, 64), (kb_d, Kall, 64), (vb_d, V65, 65)]:
                def f(g, srcd=srcd, dstb=dstb, wid=wid):
                    if wid == 65:
                        return g.dma_start(out=dstb.ap()[:, :, 0:64],
                                           in_=srcd.ap().rearrange("(m p) d -> p m d", p=128))
                    return g.dma_start(out=dstb.ap(),
                                       in_=srcd.ap().rearrange("(m p) d -> p m d", p=128))
                gdma(f, dq_bar=outs_done, wt_wait=(prev_tensor_end if seg >= 1 else None))
            dq_need = ndq[0]

            def cf_p1(v, dq_need=dq_need):
                v.wait_ge(dq, dq_need)
                v.tensor_scalar_min(skl.ap(), Kall.ap(), 0.0).then_inc(vv, 1)
                vedge(v)
                return v.tensor_copy(Msb.ap(), Mst.ap())
            W.wave('v', None, cf_p1)

            def cf_p2(s):
                return s.activation(skl.ap().rearrange("p m d -> p (m d)"),
                                    skl.ap().rearrange("p m d -> p (m d)"), AF.Exp)
            W.wave('s', None, cf_p2)

            def cf_p3(v):
                return v.scalar_tensor_tensor(skl.ap(), Kall.ap(), 0.0, skl.ap(),
                                              ALU.max, ALU.add)
            W.wave('v', None, cf_p3)

            for hl in range(8):
                def tf_t(t, hl=hl):
                    for n in range(4):
                        t.transpose(PSB[0:64, 128 * n:128 * (n + 1)],
                                    Qall[:, 4 * hl + n, :], idt[:, :])
                    for n in range(4):
                        ins = t.transpose(PSB[0:64, 1024 + 128 * n:1024 + 128 * (n + 1)],
                                          Kall[:, 4 * hl + n, :], idt[:, :])
                    return ins

                def cf_t(v, hl=hl):
                    v.tensor_copy(QT[:, hl, :], PSB[0:64, 0:512])
                    return v.tensor_copy(KT[:, hl, :], PSB[0:64, 1024:1536])
                W.wave('v', tf_t, cf_t)

            def cf_q1(v):
                return v.tensor_scalar_min(scrq.ap(), QT.ap(), 0.0)
            W.wave('v', None, cf_q1)

            def cf_q2(s):
                return s.activation(scrq.ap().rearrange("p m d -> p (m d)"),
                                    scrq.ap().rearrange("p m d -> p (m d)"), AF.Exp)
            W.wave('s', None, cf_q2)

            def cf_q3(v):
                return v.scalar_tensor_tensor(sqT.ap(), QT.ap(), 0.0, scrq.ap(),
                                              ALU.max, ALU.add)
            W.wave('v', None, cf_q3)

            for hl in range(8):
                def tf_sc(t, hl=hl):
                    for j in range(4):
                        ins = t.matmul(PS[:, 512 * j:512 * (j + 1)],
                                       KT[:, hl, 128 * j:128 * (j + 1)],
                                       QT[:, hl, :], start=True, stop=True)
                    return ins

                def cf_sc(s, hl=hl):
                    for j in range(4):
                        ins = s.activation(PhT[:, j, :], PS[:, 512 * j:512 * (j + 1)],
                                           AF.Exp, scale=0.125)
                    return ins
                W.wave('s', tf_sc, cf_sc)

                def cf_mask(v, hl=hl):
                    return v.tensor_tensor(PhT.ap(), PhT.ap(), maskT.ap(), ALU.mult)
                W.wave('v', None, cf_mask)

                def tf_pv(t, hl=hl):
                    for j in range(4):
                        t.matmul(PS[0:65, 2048:2048 + 512], V65[:, 4 * hl + j, :],
                                 PhT[:, j, :], start=(j == 0), stop=(j == 3))
                    t.matmul(PS[0:65, 2560:2560 + 512], Msb[:, hl, :], sqT[:, hl, :],
                             start=True, stop=True)
                    for j in range(4):
                        ins = t.matmul(PS[0:64, 3072:3072 + 65], skl[:, 4 * hl + j, :],
                                       V65[:, 4 * hl + j, :], start=(j == 0), stop=(j == 3))
                    return ins

                def cf_pv(v, hl=hl):
                    v.tensor_copy(tdot[:, :], PS[0:64, 2048:2048 + 512])
                    v.tensor_copy(tmem[:, :], PS[0:64, 2560:2560 + 512])
                    with nc.allow_low_precision(reason="bf16 recip rows for bcast"):
                        v.reciprocal(recs[64:65, 0:512], PS[64:65, 2048:2048 + 512])
                        v.reciprocal(recs[64:65, 512:1024], PS[64:65, 2560:2560 + 512]
                                     ).then_inc(vv, 1)
                    vedge(v)
                    return v.tensor_tensor(Mst[:, hl, :], Mst[:, hl, :],
                                           PS[0:64, 3072:3072 + 65], ALU.add)
                W.wave('v', tf_pv, cf_pv)

                def tf_bc(t, hl=hl):
                    t.matmul(PS[0:64, 0:512], onesb[64:65, 0:64], recs[64:65, 0:512],
                             start=True, stop=True)
                    ins = t.matmul(PS[0:64, 512:1024], onesb[64:65, 0:64],
                                   recs[64:65, 512:1024], start=True, stop=True)
                    return ins

                def cf_bc(v, hl=hl):
                    v.tensor_tensor(am[:, :], tmem[:, :], PS[0:64, 512:1024],
                                    ALU.mult).then_inc(vv, 1)
                    vedge(v)
                    v.tensor_scalar(am[:, :], am[:, :], gcol[:, hl:hl + 1], None,
                                    ALU.mult)
                    v.tensor_tensor(atth[:, :], tdot[:, :], PS[0:64, 0:512],
                                    ALU.mult).then_inc(vv, 1)
                    vedge(v)
                    return v.scalar_tensor_tensor(atth[:, :], atth[:, :],
                                                  g1col[:, hl:hl + 1], am[:, :],
                                                  ALU.mult, ALU.add)
                W.wave('v', tf_bc, cf_bc)

                def tf_at(t, hl=hl):
                    for n in range(4):
                        ins = t.transpose(PSB[:, 2048 + 64 * n:2048 + 64 * (n + 1)],
                                          atth[0:64, 128 * n:128 * (n + 1)], idt[0:64, 0:64])
                    return ins

                def cf_at(v, hl=hl):
                    return v.tensor_copy(attI[:, 4 * hl:4 * (hl + 1), :],
                                         PSB[:, 2048:2304].rearrange("p (n q) -> p n q", n=4))
                W.wave('v', tf_at, cf_at)
            heads_end = len(W.items)

            def f(g):
                return g.dma_start(out=ab_d.ap().rearrange("(m p) d -> p m d", p=128),
                                   in_=attI.ap())
            gdma(f, wvv_wait=nv_upto(heads_end))
            att_out_done = ndq[0]

            def f2(g):
                return g.dma_start(
                    out=atts.ap(),
                    in_=ab_d.ap().rearrange("(u p cb) d -> p u (cb d)", p=128, cb=16))
            gdma(f2, dq_bar=att_out_done)
            dq_att = ndq[0]

            def tf_at2(t, seg=seg, dq_att=dq_att):
                t.wait_ge(dq, dq_att)
                for u in range(2):
                    for n in range(8):
                        ins = t.transpose(PSB[:, 1024 * u + 128 * n:1024 * u + 128 * (n + 1)],
                                          atts[:, u, 128 * n:128 * (n + 1)], idt[:, :])
                return ins

            def cf_at2(v, seg=seg):
                return v.tensor_copy(
                    attT.ap().rearrange("p cc (u q) -> p cc u q", u=2),
                    PSB[:, 0:2048].rearrange("p (u n q) -> p n u q", u=2, n=8))
            W.wave('v', tf_at2, cf_at2)

            for u in range(2):
                wob = 2048 if u == 0 else 1024

                def tf_wo(t, u=u, wob=wob):
                    for cin in range(8):
                        for half in range(2):
                            ins = t.matmul(
                                PS[:, wob + 512 * half:wob + 512 * (half + 1)],
                                attT[:, cin, 128 * u:128 * (u + 1)],
                                wo[:, cin, 512 * half:512 * (half + 1)],
                                start=(cin == 0), stop=(cin == 7))
                    return ins

                def cf_wo(v, u=u, wob=wob):
                    v.tensor_tensor(xseg[:, u, :], PS[:, wob:wob + 1024],
                                    xseg[:, u, :], ALU.add).then_inc(vv, 1)
                    vedge(v)
                    return v.reduce_sum(sums[:, u:u + 1],
                                        xseg[:, u, :].rearrange("p (one q) -> p one q", one=1),
                                        mybir.AxisListType.X)
                W.wave('v', tf_wo, cf_wo)

            def cf_c(v, seg=seg):
                v.tensor_scalar(mean2[:, :], sums[:, :], 1.0 / C, None, ALU.mult).then_inc(vv, 1)
                vedge(v)
                for u in range(2):
                    v.tensor_scalar(xseg[:, u, :], xseg[:, u, :], mean2[:, u:u + 1],
                                    None, ALU.subtract).then_inc(vv, 1)
                    vedge(v)
                    v.tensor_tensor(scrl[:, :], xseg[:, u, :], xseg[:, u, :],
                                    ALU.mult).then_inc(vv, 1)
                    vedge(v)
                    v.reduce_sum(ssq[:, u:u + 1],
                                 scrl.ap().rearrange("p (one q) -> p one q", one=1),
                                 mybir.AxisListType.X).then_inc(vv, 1)
                    vedge(v)
                v.tensor_scalar(var2[:, :], ssq[:, :], 1.0 / C, None, ALU.mult).then_inc(vv, 1)
                vedge(v)
                v.tensor_scalar(var2[:, :], var2[:, :], 1e-5, None, ALU.add).then_inc(vv, 1)
                vedge(v)
                return v.tensor_scalar(var2[:, :], var2[:, :], 2.0 ** -40, None, ALU.mult)
            W.wave('v', None, cf_c)

            def cf_d(s, seg=seg):
                return s.activation(rstd2[:, :], var2[:, :], AF.Ln)
            W.wave('s', None, cf_d)

            def cf_d2(v, seg=seg):
                return v.tensor_scalar(rstd2[:, :], rstd2[:, :], -0.5,
                                       -13.862943611198906, ALU.mult, ALU.add)
            W.wave('v', None, cf_d2)

            def cf_d3(s, seg=seg):
                return s.activation(rstd2[:, :], rstd2[:, :], AF.Exp)
            W.wave('s', None, cf_d3)

            def cf_e(v, seg=seg):
                for u in range(2):
                    ins = v.tensor_scalar(xseg[:, u, :], xseg[:, u, :],
                                          rstd2[:, u:u + 1], None, ALU.mult)
                return ins
            W.wave('v', None, cf_e)
            seg_end_wave[0] = len(W.items)
            seg_ends.append((nv_upto(len(W.items)), len(W.items)))

        NW = len(W.items)

        @block.gpsimd
        def _(g):
            g.wait_ge(stg, kst)
            g.dma_start(out=wq.ap(), in_=Wq_d.ap().rearrange("(cc p) c -> p cc c", p=128)
                        ).then_inc(ldg, 16)
            g.dma_start(out=wk.ap(), in_=Wk_d.ap().rearrange("(cc p) c -> p cc c", p=128)
                        ).then_inc(ldg, 16)
            g.dma_start(out=wv.ap(), in_=Wv_d.ap().rearrange("(cc p) c -> p cc c", p=128)
                        ).then_inc(ldg, 16)
            g.dma_start(out=wo.ap(), in_=Wo_d.ap().rearrange("(cc p) c -> p cc c", p=128)
                        ).then_inc(ldg, 16)
            g.dma_start(out=idt.ap(), in_=id_d.ap()).then_inc(ldg, 16)
            g.dma_start(out=identf.ap(), in_=id_d.ap()).then_inc(ldg, 16)
            g.dma_start(out=maskT.ap(), in_=mt_d.ap().rearrange("p (j q) -> p j q", j=4)
                        ).then_inc(ldg, 16)
            for (dq_bar, wt_wait, wvv_wait, wvs_wait, fn) in gp:
                if dq_bar is not None:
                    g.wait_ge(dq, dq_bar)
                if wt_wait is not None:
                    g.wait_ge(wt, wt_wait)
                if wvv_wait is not None:
                    g.wait_ge(wvv, wvv_wait)
                if wvs_wait is not None:
                    g.wait_ge(wvs, wvs_wait)
                fn(g).then_inc(dq, 16)

        @block.sync
        def _(s):
            s.wait_ge(stg, kst)
            s.dma_start(out=gcol.ap(), in_=bet_d.ap()).then_inc(ld, 16)
            s.dma_start(out=Mst.ap().rearrange("p a b -> p (a b)"), in_=m0_d.ap()
                        ).then_inc(ld, 16)
            s.sem_inc(ld, 16)
            for seg in range(8):
                if seg >= 1:
                    s.wait_ge(wvv, seg_ends[seg - 1][0])
                    s.wait_ge(sto, 16 * seg)
                s.dma_start(out=xseg.ap(),
                            in_=x_d.ap()[256 * seg:256 * (seg + 1), :]
                            .rearrange("(u p) c -> p u c", p=128)).then_inc(ld, 16)
                s.wait_ge(wvv, seg_ends[seg][0])
                s.dma_start(out=x3_o.ap()[256 * seg:256 * (seg + 1), :]
                            .rearrange("(u p) c -> p u c", p=128),
                            in_=xseg.ap()).then_inc(sto, 16)
            s.wait_ge(sto, 16 * 8)
            s.sem_inc(stg, 1)

        @block.vector
        def _(v):
            v.wait_ge(stg, kst)
            v.wait_ge(ldg, 16 * 7)
            v.wait_ge(ld, 48)
            for k, (tfn, ceng, cfn, lag) in enumerate(W.items):
                if ceng != 'v':
                    continue
                v.wait_ge(wt, k + 1)
                cfn(v).then_inc(wvv, 1)

        @block.scalar
        def _(s):
            s.wait_ge(stg, kst)
            s.wait_ge(ldg, 16 * 7)
            for k, (tfn, ceng, cfn, lag) in enumerate(W.items):
                if ceng != 's':
                    continue
                s.wait_ge(wt, k + 1)
                cfn(s).then_inc(wvs, 1)

        @block.tensor
        def _(t):
            t.wait_ge(stg, kst)
            engs = [it[1] for it in W.items]
            for k, (tfn, ceng, cfn, lag) in enumerate(W.items):
                lim = max(0, k - lag)
                nvec = sum(1 for e in engs[:lim] if e == 'v')
                nsca = lim - nvec
                if nvec:
                    t.wait_ge(wvv, nvec)
                if nsca:
                    t.wait_ge(wvs, nsca)
                if tfn is not None:
                    tfn(t).then_inc(wt, 1)
                else:
                    t.sem_inc(wt, 1)


def build_l1ffn():
    nc = bass.Bass("TRN2", target_bir_lowering=False)
    io = dict(
        x3=nc.dram_tensor("x3", [T, C], F32, kind="ExternalInput"),
        w1T=nc.dram_tensor("w1T", [8, C, 4096], F32, kind="ExternalInput"),
        w3T=nc.dram_tensor("w3T", [8, C, 4096], F32, kind="ExternalInput"),
        w2T=nc.dram_tensor("w2T", [8, 4096, C], F32, kind="ExternalInput"),
        gwT=nc.dram_tensor("gwT", [C, 8], F32, kind="ExternalInput"),
        ident=nc.dram_tensor("ident", [128, 128], BF16, kind="ExternalInput"),
        out_o=nc.dram_tensor("out_o", [T, C], F32, kind="ExternalOutput"),
    )
    with ExitStack() as ES:
        stgs = ES.enter_context(nc.semaphore("stgs"))
        build_l1ffn_body(nc, ES, io, "", stgs, 0)
    return nc


def build_l1ffn_body(nc, ES, io, px, stg, kst):
    x3_d, W1_d, W3_d, W2_d = io["x3"], io["w1T"], io["w3T"], io["w2T"]
    gw_d, id_d, out_o = io["gwT"], io["ident"], io["out_o"]

    def S(n):
        return ES.enter_context(nc.semaphore(px + n))

    with ExitStack() as ctx:
        E = ctx.enter_context
        accum = E(nc.sbuf_tensor(px + "accum", [128, 16, C], F32))
        xT = E(nc.sbuf_tensor(px + "xT", [128, 8, T], BF16))
        h1T = E(nc.sbuf_tensor(px + "h1T", [128, 8, T], BF16))
        w1qb = E(nc.sbuf_tensor(px + "w1qb", [128, 8192], BF16))
        w3qb = E(nc.sbuf_tensor(px + "w3qb", [128, 8192], BF16))
        w2qb = E(nc.sbuf_tensor(px + "w2qb", [128, 8192], BF16))
        gwt = E(nc.sbuf_tensor(px + "gwt", [128, 8, 8], BF16))
        identf = E(nc.sbuf_tensor(px + "identf", [128, 128], F32))
        scrA = E(nc.sbuf_tensor(px + "scrA", [128, C], F32))
        scb = E(nc.sbuf_tensor(px + "scb", [8, 512], F32))
        sco = E(nc.sbuf_tensor(px + "sco", [128, 16, 8], F32))
        gates = E(nc.sbuf_tensor(px + "gates", [128, 16, 8], F32))
        m1t = E(nc.sbuf_tensor(px + "m1t", [128, 16], F32))
        m2t = E(nc.sbuf_tensor(px + "m2t", [128, 16], F32))
        rsum = E(nc.sbuf_tensor(px + "rsum", [128, 16], F32))
        selt = E(nc.sbuf_tensor(px + "selt", [128, 16, 8], F32))
        ssq = E(nc.sbuf_tensor(px + "ssqf", [128, 16], F32))
        sums = E(nc.sbuf_tensor(px + "sums", [128, 16], F32))
        mean2 = E(nc.sbuf_tensor(px + "mean2", [128, 16], F32))
        var2 = E(nc.sbuf_tensor(px + "var2", [128, 16], F32))
        rstd2 = E(nc.sbuf_tensor(px + "rstd2", [128, 16], F32))
        PS = E(nc.psum_tensor(px + "PS", [128, 4096], F32))
        ldg = S("ldg")
        ww0 = S("ww0")
        vv = S("vv")
        ss = S("ss")
        wt = S("wt")
        wvv = S("wvv")
        wvs = S("wvs")
        sto = S("sto")
        block = E(nc.Block())
        nvst = [0]
        ssc = [0]
        W = Waves()
        QEND = {}

        def vedge(v):
            nvst[0] += 1
            v.wait_ge(vv, nvst[0])

        for tt in range(16):
            def cfa(v, tt=tt):
                return v.tensor_copy(scrA[:, :], accum[:, tt, :])
            W.wave('v', None, cfa)

            def tfb(t, tt=tt):
                if tt == 0:
                    t.wait_ge(ldg, 16 * 3)
                for n in range(8):
                    ins = t.transpose(PS[:, 3072 + 128 * n:3072 + 128 * (n + 1)],
                                      scrA[:, 128 * n:128 * (n + 1)], identf[:, :])
                return ins

            def cfb(v, tt=tt):
                return v.tensor_copy(xT[:, :, 128 * tt:128 * (tt + 1)],
                                     PS[:, 3072:4096].rearrange("p (n q) -> p n q", n=8))
            W.wave('v', tfb, cfb)

        for tc in range(4):
            def tf(t, tc=tc):
                for cc in range(8):
                    ins = t.matmul(PS[0:8, 2560:2560 + 512], gwt[:, cc, :],
                                   xT[:, cc, 512 * tc:512 * (tc + 1)],
                                   start=(cc == 0), stop=(cc == 7))
                return ins

            def cf(s, tc=tc):
                return s.activation(scb[0:8, :], PS[0:8, 2560:2560 + 512], AF.Exp)
            W.wave('s', tf, cf)

            def tf2(t, tc=tc):
                for n in range(4):
                    ins = t.transpose(PS[:, 2048 + 8 * n:2048 + 8 * (n + 1)],
                                      scb[0:8, 128 * n:128 * (n + 1)],
                                      identf[0:8, 0:8])
                return ins

            def cf2(v, tc=tc):
                return v.tensor_copy(sco[:, 4 * tc:4 * (tc + 1), :],
                                     PS[:, 2048:2048 + 32].rearrange("p (n q) -> p n q", n=4))
            W.wave('v', tf2, cf2)

        def cf4(v):
            v.reduce_max(m1t[:, :], sco.ap(), mybir.AxisListType.X).then_inc(vv, 1)
            vedge(v)
            v.tensor_tensor(selt.ap(), sco.ap(),
                            m1t.ap().rearrange("p (h one) -> p h one", one=1)
                            .broadcast_to([128, 16, 8]), ALU.is_ge).then_inc(vv, 1)
            vedge(v)
            v.scalar_tensor_tensor(selt.ap(), selt.ap(), -1e30, sco.ap(),
                                   ALU.mult, ALU.add).then_inc(vv, 1)
            vedge(v)
            v.reduce_max(m2t[:, :], selt.ap(), mybir.AxisListType.X).then_inc(vv, 1)
            vedge(v)
            v.tensor_tensor(selt.ap(), sco.ap(),
                            m2t.ap().rearrange("p (h one) -> p h one", one=1)
                            .broadcast_to([128, 16, 8]), ALU.is_ge).then_inc(vv, 1)
            vedge(v)
            v.tensor_tensor(selt.ap(), selt.ap(), sco.ap(), ALU.mult).then_inc(vv, 1)
            vedge(v)
            v.reduce_sum(rsum[:, :], selt.ap(), mybir.AxisListType.X).then_inc(vv, 1)
            vedge(v)
            v.reciprocal(rsum[:, :], rsum[:, :]).then_inc(vv, 1)
            vedge(v)
            return v.tensor_tensor(gates.ap(), selt.ap(),
                                   rsum.ap().rearrange("p (h one) -> p h one", one=1)
                                   .broadcast_to([128, 16, 8]), ALU.mult)
        W.wave('v', None, cf4)

        GQ = [0]
        for e in range(8):
            for q in range(4):
                gq = GQ[0]
                GQ[0] += 1
                gu = 0
                for phase in range(2):
                    for ht in range(8):
                        for th in range(2):
                            unit_i = phase * 16 + ht * 2 + th
                            pt = (gq * 32 + unit_i) % 4

                            def tf(t, ht=ht, th=th, pt=pt, gq=gq, phase=phase):
                                if phase == 0 and ht == 0 and th == 0:
                                    t.wait_ge(ww0, 48 * (gq + 1))
                                wsb = w1qb if phase == 0 else w3qb
                                for cg in range(8):
                                    for half in range(2):
                                        ins = t.matmul(
                                            PS[:, 1024 * pt + 512 * half:1024 * pt + 512 * (half + 1)],
                                            wsb[:, 1024 * cg + 128 * ht:1024 * cg + 128 * (ht + 1)],
                                            xT[:, cg, 1024 * th + 512 * half:1024 * th + 512 * (half + 1)],
                                            start=(cg == 0), stop=(cg == 7))
                                return ins

                            if phase == 0:
                                def cf(s, ht=ht, th=th, pt=pt):
                                    return s.activation(h1T[:, ht, 1024 * th:1024 * (th + 1)],
                                                        PS[:, 1024 * pt:1024 * (pt + 1)],
                                                        AF.Silu)
                                W.wave('s', tf, cf, lag=(unit_i if unit_i < 3 else 3))
                            else:
                                def cf(v, ht=ht, th=th, pt=pt):
                                    return v.tensor_tensor(
                                        h1T[:, ht, 1024 * th:1024 * (th + 1)],
                                        h1T[:, ht, 1024 * th:1024 * (th + 1)],
                                        PS[:, 1024 * pt:1024 * (pt + 1)], ALU.mult)
                                W.wave('v', tf, cf, lag=3)
                for tt in range(16):
                    gt = gq * 16 + tt
                    p2 = gt % 2

                    def tf(t, tt=tt, p2=p2):
                        for hc in range(8):
                            for half in range(2):
                                ins = t.matmul(
                                    PS[:, 1024 * p2 + 512 * half:1024 * p2 + 512 * (half + 1)],
                                    h1T[:, hc, 128 * tt:128 * (tt + 1)],
                                    w2qb[:, 1024 * hc + 512 * half:1024 * hc + 512 * (half + 1)],
                                    start=(hc == 0), stop=(hc == 7))
                        return ins

                    def cf(v, tt=tt, p2=p2, e=e):
                        return v.scalar_tensor_tensor(
                            accum[:, tt, :], PS[:, 1024 * p2:1024 * (p2 + 1)],
                            gates[:, tt, e:e + 1], accum[:, tt, :], ALU.mult, ALU.add)
                    W.wave('v', tf, cf, lag=(0 if tt == 0 else 1))
                QEND[gq] = len(W.items)

        for tt in range(16):
            def cf(v, tt=tt):
                v.tensor_tensor(scrA[:, :], accum[:, tt, :], accum[:, tt, :],
                                ALU.mult).then_inc(vv, 1)
                vedge(v)
                v.reduce_sum(sums[:, tt:tt + 1],
                             accum[:, tt, :].rearrange("p (one q) -> p one q", one=1),
                             mybir.AxisListType.X)
                return v.reduce_sum(ssq[:, tt:tt + 1],
                                    scrA.ap().rearrange("p (one q) -> p one q", one=1),
                                    mybir.AxisListType.X)
            W.wave('v', None, cf)

        def cfc(v):
            v.tensor_scalar(mean2[:, :], sums[:, :], 1.0 / C, None, ALU.mult)
            v.tensor_scalar(var2[:, :], ssq[:, :], 1.0 / C, None, ALU.mult).then_inc(vv, 1)
            vedge(v)
            v.tensor_tensor(scrA[:, 0:16], mean2[:, :], mean2[:, :], ALU.mult).then_inc(vv, 1)
            vedge(v)
            v.tensor_tensor(var2[:, :], var2[:, :], scrA[:, 0:16], ALU.subtract).then_inc(vv, 1)
            vedge(v)
            return v.tensor_scalar(var2[:, :], var2[:, :], 1e-5, None, ALU.add)
        W.wave('v', None, cfc)

        def cfd(s):
            s.activation(rstd2[:, :], var2[:, :], AF.Ln).then_inc(ss, 1)
            ssc[0] += 1
            s.wait_ge(ss, ssc[0])
            return s.activation(rstd2[:, :], rstd2[:, :], AF.Exp, scale=-0.5)
        W.wave('s', None, cfd)

        def cfe(v):
            for tt in range(16):
                ins = v.tensor_scalar(accum[:, tt, :], accum[:, tt, :], mean2[:, tt:tt + 1],
                                      rstd2[:, tt:tt + 1], ALU.subtract, ALU.mult)
            return ins
        W.wave('v', None, cfe)

        @block.gpsimd
        def _(g):
            g.wait_ge(stg, kst)
            g.dma_start(out=accum.ap(), in_=x3_d.ap().rearrange("(tt p) c -> p tt c", p=128)
                        ).then_inc(ldg, 16)
            g.dma_start(out=gwt.ap(), in_=gw_d.ap().rearrange("(cc p) e -> p cc e", p=128)
                        ).then_inc(ldg, 16)
            g.dma_start(out=identf.ap(), in_=id_d.ap()).then_inc(ldg, 16)
            for e in range(8):
                for q in range(4):
                    gq = 4 * e + q
                    if gq >= 1:
                        g.wait_ge(wt, QEND[gq - 1])
                    g.dma_start(out=w1qb.ap().rearrange("p (cg h) -> p cg h", cg=8),
                                in_=W1_d.ap()[e].rearrange("(cg p) h -> p cg h", p=128)
                                [:, :, 1024 * q:1024 * (q + 1)]).then_inc(ww0, 16)
                    g.dma_start(out=w3qb.ap().rearrange("p (cg h) -> p cg h", cg=8),
                                in_=W3_d.ap()[e].rearrange("(cg p) h -> p cg h", p=128)
                                [:, :, 1024 * q:1024 * (q + 1)]).then_inc(ww0, 16)
                    g.dma_start(out=w2qb.ap().rearrange("p (hc c) -> p hc c", hc=8),
                                in_=W2_d.ap()[e].rearrange("(q hc p) c -> q p hc c", p=128, hc=8)[q]
                                ).then_inc(ww0, 16)

        @block.sync
        def _(s):
            s.wait_ge(stg, kst)
            nvec = sum(1 for it in W.items if it[1] == 'v')
            s.wait_ge(wvv, nvec)
            s.dma_start(out=out_o.ap().rearrange("(tt p) c -> p tt c", p=128),
                        in_=accum.ap()).then_inc(sto, 16)
            s.wait_ge(sto, 16)
            s.sem_inc(stg, 1)

        @block.vector
        def _(v):
            v.wait_ge(stg, kst)
            v.wait_ge(ldg, 16 * 3)
            for k, (tfn, ceng, cfn, lag) in enumerate(W.items):
                if ceng != 'v':
                    continue
                v.wait_ge(wt, k + 1)
                cfn(v).then_inc(wvv, 1)

        @block.scalar
        def _(s):
            s.wait_ge(stg, kst)
            s.wait_ge(ldg, 16 * 3)
            for k, (tfn, ceng, cfn, lag) in enumerate(W.items):
                if ceng != 's':
                    continue
                s.wait_ge(wt, k + 1)
                cfn(s).then_inc(wvs, 1)

        @block.tensor
        def _(t):
            t.wait_ge(stg, kst)
            engs = [it[1] for it in W.items]
            for k, (tfn, ceng, cfn, lag) in enumerate(W.items):
                lim = max(0, k - lag)
                nvec = sum(1 for e2 in engs[:lim] if e2 == 'v')
                nsca = lim - nvec
                if nvec:
                    t.wait_ge(wvv, nvec)
                if nsca:
                    t.wait_ge(wvs, nsca)
                if tfn is not None:
                    tfn(t).then_inc(wt, 1)
                else:
                    t.sem_inc(wt, 1)


def build_xexch_body(nc, ES, io, px, stg, kst):
    x2_d, agx_d, xin_d = io["x2"], io["agx"], io["xin5"]
    me_d, mo_d = io["me"], io["mo"]

    def S(n):
        return ES.enter_context(nc.semaphore(px + n))

    with ExitStack() as ctx:
        E = ctx.enter_context
        bufA = E(nc.sbuf_tensor(px + "bufA", [128, 2, 2, C], F32))
        bufB = E(nc.sbuf_tensor(px + "bufB", [128, 2, 2, C], F32))
        xo = E(nc.sbuf_tensor(px + "xo", [128, 2, 2, C], F32))
        mes = E(nc.sbuf_tensor(px + "mes", [128, 1], F32))
        mos = E(nc.sbuf_tensor(px + "mos", [128, 1], F32))
        ld = S("ld")
        cc = S("cc")
        l0 = S("l0")
        l1 = S("l1")
        so0 = S("so0")
        so1 = S("so1")
        vb = S("vb")
        lds = [l0, l1]
        sos = [so0, so1]
        block = E(nc.Block())

        @block.gpsimd
        def _(g):
            g.wait_ge(stg, kst)
            g.collective_compute(
                "AllGather", ALU.bypass, replica_groups=PAIRS4x2,
                ins=[x2_d.ap()], outs=[agx_d.ap()]).then_inc(cc, 1)
            g.wait_ge(cc, 1)
            ag = agx_d.ap()
            for s8 in range(8):
                sl = s8 % 2
                if s8 >= 2:
                    g.wait_ge(vb, 2 * (s8 - 1))
                g.dma_start(out=bufA[:, sl, :, :],
                            in_=ag[512 * s8:512 * s8 + 256, :]
                            .rearrange("(u p) c -> p u c", p=128)).then_inc(lds[sl], 16)
                g.dma_start(out=bufB[:, sl, :, :],
                            in_=ag[512 * s8 + 256:512 * s8 + 512, :]
                            .rearrange("(u p) c -> p u c", p=128)).then_inc(lds[sl], 16)

        @block.sync
        def _(s):
            s.wait_ge(stg, kst)
            s.dma_start(out=mes.ap(), in_=me_d.ap()).then_inc(ld, 16)
            s.dma_start(out=mos.ap(), in_=mo_d.ap()).then_inc(ld, 16)
            for s8 in range(8):
                sl = s8 % 2
                s.wait_ge(vb, 2 * s8 + 2)
                s.dma_start(out=xin_d.ap()[256 * s8:256 * (s8 + 1), :]
                            .rearrange("(u p) c -> p u c", p=128),
                            in_=xo[:, sl, :, :]).then_inc(sos[sl], 16)
            s.wait_ge(sos[0], 16 * 4)
            s.wait_ge(sos[1], 16 * 4)
            s.sem_inc(stg, 1)

        @block.vector
        def _(v):
            v.wait_ge(stg, kst)
            v.wait_ge(ld, 32)
            for s8 in range(8):
                sl = s8 % 2
                v.wait_ge(lds[sl], 32 * (s8 // 2 + 1))
                if s8 >= 2:
                    v.wait_ge(sos[sl], 16 * (s8 // 2))
                xof = xo[:, sl, :, :].rearrange("p u c -> p (u c)")
                v.tensor_scalar(xof, bufA[:, sl, :, :].rearrange("p u c -> p (u c)"),
                                mes[:, 0:1], None, ALU.mult).then_inc(vb, 1)
                v.wait_ge(vb, 2 * s8 + 1)
                v.scalar_tensor_tensor(xof, bufB[:, sl, :, :].rearrange("p u c -> p (u c)"),
                                       mos[:, 0:1], xof,
                                       ALU.mult, ALU.add).then_inc(vb, 1)


def build_fused():
    nc = bass.Bass("TRN2", target_bir_lowering=False)
    EI, EO, IN = "ExternalInput", "ExternalOutput", "Internal"
    D = nc.dram_tensor
    t_ident = D("ident", [128, 128], BF16, kind=EI)
    io_front = dict(
        xT=D("xT", [C, T + 1], F32, kind=EI),
        maas=D("maas", [128, 7 * 8], F32, kind=EI),
        maa_w1=D("maa_w1", [C, 160], F32, kind=EI),
        maa_w2=D("maa_w2", [5, 32, C], F32, kind=EI),
        Wk=D("Wk0", [C, C], F32, kind=EI),
        Wv=D("Wv0", [C, C], F32, kind=EI),
        Wr=D("Wr0", [C, C], F32, kind=EI),
        gate_w1=D("gate_w1", [C, 64], F32, kind=EI),
        gate_w2=D("gate_w2", [64, C], F32, kind=EI),
        decay_w1=D("decay_w1", [C, 64], F32, kind=EI),
        decay_w2=D("decay_w2", [64, C], F32, kind=EI),
        rT_o=D("rTf", [C, T], BF16, kind=IN),
        kT_o=D("kTf", [C, T], BF16, kind=IN),
        v_o=D("vf", [T, C], BF16, kind=IN),
        wrawT_o=D("wTf", [C, T], F32, kind=IN),
        g_o=D("gf", [T, C], BF16, kind=IN),
    )
    io_wkv = dict(
        rTf=io_front["rT_o"], kTf=io_front["kT_o"], wTf=io_front["wrawT_o"],
        vf=io_front["v_o"],
        ubc2=D("ubc2", [128, 512], F32, kind=EI),
        maskst=D("maskst", [128, 64], F32, kind=EI),
        eyem=D("eyem", [128, 64], F32, kind=EI),
        ident=t_ident,
        usin=D("usin", [128, 1], F32, kind=EI),
        y2=D("y2b", [T, C], BF16, kind=IN),
        sfb=D("sfb", [128, 512], F32, kind=IN),
        agS=D("agS", [256, 512], F32, kind=IN),
    )
    io_back = dict(
        y_in=io_wkv["y2"], g_in=io_front["g_o"],
        x_in=D("xr", [T, C], F32, kind=EI),
        lngr=D("lngr", [128, C], F32, kind=EI),
        lnbr=D("lnbr", [128, C], F32, kind=EI),
        Wo=D("Wo0", [C, C], F32, kind=EI),
        ident=t_ident,
        x1_o=D("x1b", [T, C], F32, kind=IN),
    )
    io_ffn0 = dict(
        x1=io_back["x1_o"],
        W1a=D("W1a", [10, C, 4096], F32, kind=EI),
        W2a=D("W2a", [10, 4096, C], F32, kind=EI),
        b1a=D("b1a", [128, 10, 32], F32, kind=EI),
        b2a=D("b2a", [10, C], F32, kind=EI),
        cent=D("cent", [C, 8], F32, kind=EI),
        gsc=D("gsc", [128, 8], F32, kind=EI),
        grc=D("grc", [128, 8], F32, kind=EI),
        ident=t_ident,
        x2_o=D("x2b", [T, C], F32, kind=IN),
    )
    io_exch = dict(
        x2=io_ffn0["x2_o"],
        agx=D("agx", [2 * T, C], F32, kind=IN),
        xin5=D("xin5", [T, C], F32, kind=IN),
        me=D("me", [128, 1], F32, kind=EI),
        mo=D("mo", [128, 1], F32, kind=EI),
    )
    io_attn = dict(
        x_in=io_exch["xin5"],
        Wq=D("Wq1", [C, C], F32, kind=EI),
        Wk=D("Wk1", [C, C], F32, kind=EI),
        Wv=D("Wv1", [C, C], F32, kind=EI),
        Wo=D("Wo1", [C, C], F32, kind=EI),
        betas=D("betas", [64, 8], F32, kind=EI),
        mem0z=D("mem0z", [64, 8 * 65], F32, kind=EI),
        maskT=D("maskT", [128, 4 * 512], F32, kind=EI),
        ident=t_ident,
        x3_o=D("x3b", [T, C], F32, kind=IN),
        qbuf=D("qbuf", [512 * 8, 64], BF16, kind=IN),
        kbuf=D("kbuf", [512 * 8, 64], BF16, kind=IN),
        vbuf=D("vbuf", [512 * 8, 64], BF16, kind=IN),
        abuf=D("abuf", [512 * 8, 64], BF16, kind=IN),
    )
    io_ffn1 = dict(
        x3=io_attn["x3_o"],
        w1T=D("w1T", [8, C, 4096], F32, kind=EI),
        w3T=D("w3T", [8, C, 4096], F32, kind=EI),
        w2T=D("w2T", [8, 4096, C], F32, kind=EI),
        gwT=D("gwT", [C, 8], F32, kind=EI),
        ident=t_ident,
        out_o=D("out_o", [T, C], F32, kind=EO),
    )
    with ExitStack() as ES:
        stg = ES.enter_context(nc.semaphore("stg"))
        build_l0front_body(nc, ES, io_front, "f_", stg, 0)
        build_wkv2_body(nc, ES, io_wkv, "w_", stg, 1)
        build_l0back_body(nc, ES, io_back, "b_", stg, 2)
        build_l0ffn_body(nc, ES, io_ffn0, "m_", stg, 3)
        build_xexch_body(nc, ES, io_exch, "e_", stg, 4)
        build_l1attn_body(nc, ES, io_attn, "a_", stg, 5)
        build_l1ffn_body(nc, ES, io_ffn1, "n_", stg, 6)
    return nc


def prep_fused(inputs):
    f32 = np.float32
    x = np.asarray(inputs["L0_attn_x"], f32)
    maas = np.stack([_cmaj(np.asarray(inputs[k], f32)) for k in
                     ["L0_attn_maa_x", "L0_attn_maa_w", "L0_attn_maa_k",
                      "L0_attn_maa_v", "L0_attn_maa_r", "L0_attn_maa_g",
                      "L0_attn_time_decay"]], axis=1)
    maas = np.ascontiguousarray(maas.reshape(128, 56), f32)
    u = np.asarray(inputs["L0_attn_u"], f32)
    ubc2 = np.zeros((128, 512), f32)
    for g in range(2):
        for hl in range(8):
            j, hp = hl // 2, hl % 2
            ubc2[64 * hp:64 * hp + 64, 256 * g + 64 * j:256 * g + 64 * (j + 1)] = \
                u[8 * g + hl][:, None]
    maskst = np.zeros((128, 64), f32)
    eyem = np.zeros((128, 64), f32)
    for p in range(128):
        for tcol in range(64):
            if (p % 64) < tcol:
                maskst[p, tcol] = 1.0
            if (p % 64) == tcol:
                eyem[p, tcol] = 1.0
    lngr = np.ascontiguousarray(np.broadcast_to(
        np.asarray(inputs["L0_attn_ln_g"], f32), (128, C)))
    lnbr = np.ascontiguousarray(np.broadcast_to(
        np.asarray(inputs["L0_attn_ln_b"], f32), (128, C)))
    W1a = np.ascontiguousarray(np.concatenate(
        [np.asarray(inputs["L0_ffn_W1_s"], f32), np.asarray(inputs["L0_ffn_W1_r"], f32)], 0))
    W2a = np.ascontiguousarray(np.concatenate(
        [np.asarray(inputs["L0_ffn_W2_s"], f32), np.asarray(inputs["L0_ffn_W2_r"], f32)], 0))
    b1v = np.concatenate([np.asarray(inputs["L0_ffn_b1_s"], f32),
                          np.asarray(inputs["L0_ffn_b1_r"], f32)], 0)
    b1t = np.ascontiguousarray(b1v.reshape(10, 32, 128).transpose(2, 0, 1))
    b2a = np.ascontiguousarray(np.concatenate(
        [np.asarray(inputs["L0_ffn_b2_s"], f32), np.asarray(inputs["L0_ffn_b2_r"], f32)], 0))
    maskT = np.zeros((128, 4, 512), f32)
    for j in range(4):
        for p in range(128):
            maskT[p, j, 128 * j + p:] = 1.0
    maskT = np.ascontiguousarray(maskT.reshape(128, 2048))
    betas = np.asarray(inputs["L1_attn_betas"], f32)[0, :, 0, :]
    z0 = np.asarray(inputs["L1_attn_z0"], f32)[0, :, 0, :]
    m0 = np.asarray(inputs["L1_attn_memory0"], f32)[0]
    m0z = np.concatenate([m0, z0[:, :, None]], axis=2)
    shared = dict(
        ident=_IDENT, maas=maas,
        maa_w1=np.asarray(inputs["L0_attn_maa_w1"], f32),
        maa_w2=np.asarray(inputs["L0_attn_maa_w2"], f32),
        Wk0=np.asarray(inputs["L0_attn_Wk"], f32),
        Wv0=np.asarray(inputs["L0_attn_Wv"], f32),
        Wr0=np.asarray(inputs["L0_attn_Wr"], f32),
        gate_w1=np.asarray(inputs["L0_attn_gate_w1"], f32),
        gate_w2=np.asarray(inputs["L0_attn_gate_w2"], f32),
        decay_w1=np.asarray(inputs["L0_attn_decay_w1"], f32),
        decay_w2=np.asarray(inputs["L0_attn_decay_w2"], f32),
        ubc2=ubc2, maskst=maskst, eyem=eyem,
        lngr=lngr, lnbr=lnbr,
        Wo0=np.asarray(inputs["L0_attn_Wo"], f32),
        W1a=W1a, W2a=W2a, b1a=b1t, b2a=b2a,
        cent=np.asarray(inputs["L0_ffn_centroids"], f32),
        gsc=np.ascontiguousarray(
            np.asarray(inputs["L0_ffn_g_shared"], f32).reshape(8, 128).T),
        grc=np.ascontiguousarray(
            np.asarray(inputs["L0_ffn_g_routed"], f32).reshape(8, 128).T),
        Wq1=np.asarray(inputs["L1_attn_Wq"], f32),
        Wk1=np.asarray(inputs["L1_attn_Wk"], f32),
        Wv1=np.asarray(inputs["L1_attn_Wv"], f32),
        Wo1=np.asarray(inputs["L1_attn_Wo"], f32),
        maskT=maskT,
        w1T=np.ascontiguousarray(np.asarray(inputs["L1_ffn_w1"], f32).transpose(0, 2, 1)),
        w3T=np.ascontiguousarray(np.asarray(inputs["L1_ffn_w3"], f32).transpose(0, 2, 1)),
        w2T=np.ascontiguousarray(np.asarray(inputs["L1_ffn_w2"], f32).transpose(0, 2, 1)),
        gwT=np.ascontiguousarray(np.asarray(inputs["L1_ffn_gate_w"], f32).T),
    )
    maps = []
    for c in range(8):
        b, hf = c // 2, c % 2
        xb = np.asarray(x[b], f32)
        if hf == 0:
            xT = np.concatenate([np.zeros((C, 1), f32), xb[0:2048].T], axis=1)
        else:
            xT = xb[2047:4096].T
        h0 = 8 * hf
        maps.append(dict(
            shared,
            xT=np.ascontiguousarray(xT),
            xr=np.ascontiguousarray(xb[2048 * hf:2048 * (hf + 1)]),
            usin=np.full((128, 1), float(hf), f32),
            me=np.full((128, 1), 1.0 - hf, f32),
            mo=np.full((128, 1), float(hf), f32),
            betas=np.ascontiguousarray(betas[h0:h0 + 8].T),
            mem0z=np.ascontiguousarray(
                m0z[h0:h0 + 8].transpose(1, 0, 2).reshape(64, 8 * 65)),
        ))
    return maps


def kernel_fused(**inputs):
    nc = build_fused()
    maps = prep_fused(inputs)
    res = run_bass_kernel_spmd(nc, maps, core_ids=list(range(8))).results
    out = np.empty((4, 4096, C), np.float32)
    srows = [_striped_rows(0), _striped_rows(1)]
    for c in range(8):
        b, hg = c // 2, c % 2
        out[b][srows[hg]] = np.asarray(res[c]["out_o"])
    return out


_IDENT = np.eye(128, dtype=BF)


def _striped_rows(hg):
    idx = []
    for s in range(8):
        idx.extend(range(512 * s + 256 * hg, 512 * s + 256 * (hg + 1)))
    return np.array(idx)


def kernel(**inputs):
    return kernel_fused(**inputs)


def kernel_multi(**inputs):
    f32 = np.float32
    x = np.asarray(inputs["L0_attn_x"], f32)

    nc1 = build_l0front()
    maps1 = prep_l0front(inputs)
    r1 = run_bass_kernel_spmd(nc1, maps1, core_ids=list(range(8))).results

    nc2 = build_wkv()
    maps2 = prep_wkv(r1, np.asarray(inputs["L0_attn_u"], f32))
    r2 = run_bass_kernel_spmd(nc2, maps2, core_ids=list(range(8))).results

    nc3 = build_l0back()
    lngr = np.ascontiguousarray(np.broadcast_to(
        np.asarray(inputs["L0_attn_ln_g"], f32), (128, C)))
    lnbr = np.ascontiguousarray(np.broadcast_to(
        np.asarray(inputs["L0_attn_ln_b"], f32), (128, C)))
    Wo0 = np.asarray(inputs["L0_attn_Wo"], f32)
    maps3 = []
    for c in range(8):
        b, hf = c // 2, c % 2
        yb = np.concatenate([np.asarray(r2[2 * b]["y"]),
                             np.asarray(r2[2 * b + 1]["y"])], axis=1)
        maps3.append(dict(y_in=np.ascontiguousarray(yb[2048 * hf:2048 * (hf + 1)]),
                          g_in=np.asarray(r1[c]["g_o"]),
                          x_in=np.ascontiguousarray(x[b, 2048 * hf:2048 * (hf + 1)]),
                          lngr=lngr, lnbr=lnbr, Wo=Wo0, ident=_IDENT))
    r3 = run_bass_kernel_spmd(nc3, maps3, core_ids=list(range(8))).results

    nc4 = build_l0ffn()
    W1a = np.ascontiguousarray(np.concatenate(
        [np.asarray(inputs["L0_ffn_W1_s"], f32), np.asarray(inputs["L0_ffn_W1_r"], f32)], 0))
    W2a = np.ascontiguousarray(np.concatenate(
        [np.asarray(inputs["L0_ffn_W2_s"], f32), np.asarray(inputs["L0_ffn_W2_r"], f32)], 0))
    b1v = np.concatenate([np.asarray(inputs["L0_ffn_b1_s"], f32),
                          np.asarray(inputs["L0_ffn_b1_r"], f32)], 0)
    b1t = np.ascontiguousarray(b1v.reshape(10, 32, 128).transpose(2, 0, 1))
    b2a = np.ascontiguousarray(np.concatenate(
        [np.asarray(inputs["L0_ffn_b2_s"], f32), np.asarray(inputs["L0_ffn_b2_r"], f32)], 0))
    shared4 = dict(W1a=W1a, W2a=W2a, b1a=b1t, b2a=b2a,
                   cent=np.asarray(inputs["L0_ffn_centroids"], f32),
                   gsc=np.ascontiguousarray(
                       np.asarray(inputs["L0_ffn_g_shared"], f32).reshape(8, 128).T),
                   grc=np.ascontiguousarray(
                       np.asarray(inputs["L0_ffn_g_routed"], f32).reshape(8, 128).T),
                   ident=_IDENT)
    maps4 = [dict(shared4, x1=np.asarray(r3[c]["x1_o"])) for c in range(8)]
    r4 = run_bass_kernel_spmd(nc4, maps4, core_ids=list(range(8))).results

    nc5 = build_l1attn()
    maskT = np.zeros((128, 4, 512), f32)
    for j in range(4):
        for p in range(128):
            maskT[p, j, 128 * j + p:] = 1.0
    maskT = np.ascontiguousarray(maskT.reshape(128, 2048))
    betas = np.asarray(inputs["L1_attn_betas"], f32)[0, :, 0, :]
    z0 = np.asarray(inputs["L1_attn_z0"], f32)[0, :, 0, :]
    m0 = np.asarray(inputs["L1_attn_memory0"], f32)[0]
    m0z = np.concatenate([m0, z0[:, :, None]], axis=2)
    shared5 = dict(Wq=np.asarray(inputs["L1_attn_Wq"], f32),
                   Wk=np.asarray(inputs["L1_attn_Wk"], f32),
                   Wv=np.asarray(inputs["L1_attn_Wv"], f32),
                   Wo=np.asarray(inputs["L1_attn_Wo"], f32),
                   maskT=maskT, ident=_IDENT)
    maps5 = []
    srows = [_striped_rows(0), _striped_rows(1)]
    for c in range(8):
        b, hg = c // 2, c % 2
        x2b = np.concatenate([np.asarray(r4[2 * b]["x2_o"]),
                              np.asarray(r4[2 * b + 1]["x2_o"])], axis=0)
        h0 = 8 * hg
        maps5.append(dict(shared5,
                          x_in=np.ascontiguousarray(x2b[srows[hg]]),
                          betas=np.ascontiguousarray(betas[h0:h0 + 8].T),
                          mem0z=np.ascontiguousarray(
                              m0z[h0:h0 + 8].transpose(1, 0, 2).reshape(64, 8 * 65))))
    r5 = run_bass_kernel_spmd(nc5, maps5, core_ids=list(range(8))).results

    nc6 = build_l1ffn()
    shared6 = dict(
        w1T=np.ascontiguousarray(np.asarray(inputs["L1_ffn_w1"], f32).transpose(0, 2, 1)),
        w3T=np.ascontiguousarray(np.asarray(inputs["L1_ffn_w3"], f32).transpose(0, 2, 1)),
        w2T=np.ascontiguousarray(np.asarray(inputs["L1_ffn_w2"], f32).transpose(0, 2, 1)),
        gwT=np.ascontiguousarray(np.asarray(inputs["L1_ffn_gate_w"], f32).T),
        ident=_IDENT)
    maps6 = []
    for c in range(8):
        b, hf = c // 2, c % 2
        x3b = np.empty((4096, C), f32)
        x3b[srows[0]] = np.asarray(r5[2 * b]["x3_o"])
        x3b[srows[1]] = np.asarray(r5[2 * b + 1]["x3_o"])
        maps6.append(dict(shared6, x3=np.ascontiguousarray(x3b[2048 * hf:2048 * (hf + 1)])))
    r6 = run_bass_kernel_spmd(nc6, maps6, core_ids=list(range(8))).results

    out = np.empty((4, 4096, C), f32)
    for c in range(8):
        b, hf = c // 2, c % 2
        out[b, 2048 * hf:2048 * (hf + 1)] = np.asarray(r6[c]["out_o"])
    return out


# version 04 (final)
# speedup vs baseline: 1.2040x; 1.2040x over previous
import numpy as np
import ml_dtypes
from contextlib import ExitStack
import concourse.bass as bass
import concourse.mybir as mybir
from concourse.bass_utils import run_bass_kernel_spmd

F32 = mybir.dt.float32
BF16 = mybir.dt.bfloat16
AF = mybir.ActivationFunctionType
ALU = mybir.AluOpType

T = 2048
C = 1024
BF = ml_dtypes.bfloat16


class Waves:

    def __init__(self):
        self.items = []

    def wave(self, ceng, tfn, cfn, lag=0):
        self.items.append((tfn, ceng, cfn, lag))


def build_l0front():
    nc = bass.Bass("TRN2", target_bir_lowering=False)
    io = dict(
        xT=nc.dram_tensor("xT", [C, T + 1], F32, kind="ExternalInput"),
        maas=nc.dram_tensor("maas", [128, 7 * 8], F32, kind="ExternalInput"),
        maa_w1=nc.dram_tensor("maa_w1", [C, 160], F32, kind="ExternalInput"),
        maa_w2=nc.dram_tensor("maa_w2", [5, 32, C], F32, kind="ExternalInput"),
        Wk=nc.dram_tensor("Wk", [C, C], F32, kind="ExternalInput"),
        Wv=nc.dram_tensor("Wv", [C, C], F32, kind="ExternalInput"),
        Wr=nc.dram_tensor("Wr", [C, C], F32, kind="ExternalInput"),
        gate_w1=nc.dram_tensor("gate_w1", [C, 64], F32, kind="ExternalInput"),
        gate_w2=nc.dram_tensor("gate_w2", [64, C], F32, kind="ExternalInput"),
        decay_w1=nc.dram_tensor("decay_w1", [C, 64], F32, kind="ExternalInput"),
        decay_w2=nc.dram_tensor("decay_w2", [64, C], F32, kind="ExternalInput"),
        rT_o=nc.dram_tensor("rT_o", [C, T], BF16, kind="ExternalOutput"),
        kT_o=nc.dram_tensor("kT_o", [C, T], BF16, kind="ExternalOutput"),
        v_o=nc.dram_tensor("v_o", [T, C], BF16, kind="ExternalOutput"),
        wrawT_o=nc.dram_tensor("wrawT_o", [C, T], F32, kind="ExternalOutput"),
        g_o=nc.dram_tensor("g_o", [T, C], BF16, kind="ExternalOutput"),
    )
    with ExitStack() as ES:
        stgs = ES.enter_context(nc.semaphore("stgs"))
        build_l0front_body(nc, ES, io, "", stgs, 0)
    return nc


def build_l0front_body(nc, ES, io, px, stg, kst):
    xT_d, maas_d, mw1_d, mw2_d = io["xT"], io["maas"], io["maa_w1"], io["maa_w2"]
    Wk_d, Wv_d, Wr_d = io["Wk"], io["Wv"], io["Wr"]
    gw1_d, gw2_d, dw1_d, dw2_d = (io["gate_w1"], io["gate_w2"],
                                  io["decay_w1"], io["decay_w2"])
    rT_o, kT_o, v_o = io["rT_o"], io["kT_o"], io["v_o"]
    wrawT_o, g_o = io["wrawT_o"], io["g_o"]

    def S(n):
        return ES.enter_context(nc.semaphore(px + n))

    ctx = ExitStack()
    with ctx:
        E = ctx.enter_context
        xTs = E(nc.sbuf_tensor(px + "xTs", [128, 8, T + 1], BF16))
        xxs = E(nc.sbuf_tensor(px + "xxs", [128, 8, T], BF16))
        cur = E(nc.sbuf_tensor(px + "cur", [128, 8, T], BF16))
        loraT = E(nc.sbuf_tensor(px + "loraT", [32, 5, T], BF16))
        dtanh = E(nc.sbuf_tensor(px + "dtanh", [64, T], BF16))
        maas = E(nc.sbuf_tensor(px + "maas_s", [128, 7, 8], F32))
        w1s = E(nc.sbuf_tensor(px + "w1s", [128, 8, 160], BF16))
        w2s = E(nc.sbuf_tensor(px + "w2s", [32, 5, C], BF16))
        Wbig = E(nc.sbuf_tensor(px + "Wbig", [128, 8, C], BF16))
        gw1 = E(nc.sbuf_tensor(px + "gw1s", [128, 8, 64], BF16))
        gw2 = E(nc.sbuf_tensor(px + "gw2s", [64, C], BF16))
        dw1 = E(nc.sbuf_tensor(px + "dw1s", [128, 8, 64], BF16))
        dw2 = E(nc.sbuf_tensor(px + "dw2s", [64, C], BF16))
        stgb_f = E(nc.sbuf_tensor(px + "stgf", [128, 2, T], F32))
        stgb = E(nc.sbuf_tensor(px + "stgb", [128, 2, T], BF16))
        PS = E(nc.psum_tensor(px + "PS", [128, 4096], F32))
        ldg = S("ldg")
        gw = S("gw")
        ld = S("ld")
        vv = S("vv")
        sv = S("sv")
        wt = S("wt")
        wvv = S("wvv")
        wvs = S("wvs")
        sy0 = S("sy0")
        sy1 = S("sy1")
        syp = [sy0, sy1]
        block = E(nc.Block())

        KINDS = [("k", 2), ("v", 3), ("r", 4), ("g", 5), ("w", 1)]
        FIDX = {"w": 0, "k": 1, "v": 2, "r": 3, "g": 4}
        nvst = [0]

        wvs_l = Waves()
        outs = []
        slot_uses = [0, 0]

        def tile(pp):
            return PS[:, 2048 * pp:2048 * (pp + 1)]

        for tc in range(4):
            def tf(t, tc=tc):
                if tc == 0:
                    t.wait_ge(sv, 1)
                    t.wait_ge(ldg, 112)
                for f in range(5):
                    for cc in range(8):
                        ins = t.matmul(PS[0:32, 512 * f:512 * f + 512],
                                       w1s[:, cc, 32 * f:32 * (f + 1)],
                                       cur[:, cc, 512 * tc:512 * (tc + 1)],
                                       start=(cc == 0), stop=(cc == 7))
                return ins

            def cf(s, tc=tc):
                for f in range(5):
                    ins = s.activation(loraT[0:32, f, 512 * tc:512 * (tc + 1)],
                                       PS[0:32, 512 * f:512 * f + 512], AF.Tanh)
                return ins
            wvs_l.wave('s', tf, cf)

        ngw = [0]
        for ni, (kind, ki) in enumerate(KINDS):
            for ct in range(8):
                pp = ct % 2

                def tf(t, ct=ct, pp=pp, f=FIDX[kind]):
                    for tc in range(4):
                        ins = t.matmul(
                            PS[:, 2048 * pp + 512 * tc:2048 * pp + 512 * tc + 512],
                            w2s[0:32, f, 128 * ct:128 * (ct + 1)],
                            loraT[0:32, f, 512 * tc:512 * (tc + 1)],
                            start=True, stop=True)
                    return ins

                def cf(v, ct=ct, pp=pp, ki=ki):
                    v.scalar_tensor_tensor(cur[:, ct, :], tile(pp),
                                           maas[:, ki, ct:ct + 1], xxs[:, ct, :],
                                           ALU.add, ALU.mult).then_inc(vv, 1)
                    nvst[0] += 1
                    v.wait_ge(vv, nvst[0])
                    return v.tensor_tensor(cur[:, ct, :], cur[:, ct, :],
                                           xTs[:, ct, 1:T + 1], ALU.add)
                wvs_l.wave('v', tf, cf)

            if kind in ("k", "v", "r"):
                ngw[0] += 1
                mygw = ngw[0]

            if kind in ("k", "r"):
                for ct in range(8):
                    pp = ct % 2

                    def tf(t, ct=ct, pp=pp, mygw=mygw):
                        if ct == 0:
                            t.wait_ge(gw, 16 * mygw)
                        for cin in range(8):
                            for tc in range(4):
                                ins = t.matmul(
                                    PS[:, 2048 * pp + 512 * tc:2048 * pp + 512 * tc + 512],
                                    Wbig[:, cin, 128 * ct:128 * (ct + 1)],
                                    cur[:, cin, 512 * tc:512 * (tc + 1)],
                                    start=(cin == 0), stop=(cin == 7))
                        return ins

                    sl = sum(1 for o in outs) % 2
                    myuse = slot_uses[sl]
                    slot_uses[sl] += 1

                    def cf(s, pp=pp, sl=sl, myuse=myuse):
                        if myuse >= 1:
                            s.wait_ge(syp[sl], 16 * myuse)
                        return s.activation(stgb[:, sl, :], tile(pp), AF.Copy)
                    wvs_l.wave('s', tf, cf)
                    outs.append((len(wvs_l.items) - 1, kind, ct, sl))
            elif kind == "v":
                for ct in range(8):
                    pp = ct % 2

                    def tf(t, ct=ct, pp=pp, mygw=mygw):
                        if ct == 0:
                            t.wait_ge(gw, 16 * mygw)
                        for u in range(2):
                            tt = 2 * ct + u
                            for cin in range(8):
                                for half in range(2):
                                    ins = t.matmul(
                                        PS[:, 2048 * pp + 1024 * u + 512 * half:
                                           2048 * pp + 1024 * u + 512 * (half + 1)],
                                        cur[:, cin, 128 * tt:128 * (tt + 1)],
                                        Wbig[:, cin, 512 * half:512 * (half + 1)],
                                        start=(cin == 0), stop=(cin == 7))
                        return ins

                    sl = len(outs) % 2
                    myuse = slot_uses[sl]
                    slot_uses[sl] += 1

                    def cf(s, pp=pp, sl=sl, myuse=myuse):
                        if myuse >= 1:
                            s.wait_ge(syp[sl], 16 * myuse)
                        return s.activation(stgb[:, sl, :], tile(pp), AF.Copy)
                    wvs_l.wave('s', tf, cf)
                    outs.append((len(wvs_l.items) - 1, kind, ct, sl))
            else:
                ws1 = gw1 if kind == "g" else dw1
                ws2 = gw2 if kind == "g" else dw2
                for tc in range(4):
                    def tf(t, tc=tc, ws1=ws1):
                        for cin in range(8):
                            ins = t.matmul(PS[0:64, 3584:3584 + 512],
                                           ws1[:, cin, :],
                                           cur[:, cin, 512 * tc:512 * (tc + 1)],
                                           start=(cin == 0), stop=(cin == 7))
                        return ins

                    def cf(s, tc=tc):
                        return s.activation(dtanh[0:64, 512 * tc:512 * (tc + 1)],
                                            PS[0:64, 3584:3584 + 512], AF.Tanh)
                    wvs_l.wave('s', tf, cf)
                if kind == "g":
                    for ct in range(8):
                        pp = ct % 2

                        def tf(t, ct=ct, pp=pp, ws2=ws2):
                            for u in range(2):
                                tt = 2 * ct + u
                                for half in range(2):
                                    ins = t.matmul(
                                        PS[:, 2048 * pp + 1024 * u + 512 * half:
                                           2048 * pp + 1024 * u + 512 * (half + 1)],
                                        dtanh[0:64, 128 * tt:128 * (tt + 1)],
                                        ws2[0:64, 512 * half:512 * (half + 1)],
                                        start=True, stop=True)
                            return ins

                        sl = len(outs) % 2
                        myuse = slot_uses[sl]
                        slot_uses[sl] += 1

                        def cf(s, pp=pp, sl=sl, myuse=myuse):
                            if myuse >= 1:
                                s.wait_ge(syp[sl], 16 * myuse)
                            return s.activation(stgb[:, sl, :], tile(pp), AF.Copy)
                        wvs_l.wave('s', tf, cf)
                        outs.append((len(wvs_l.items) - 1, kind, ct, sl))
                else:
                    for ct in range(8):
                        pp = ct % 2

                        def tf(t, ct=ct, pp=pp, ws2=ws2):
                            for tc in range(4):
                                ins = t.matmul(
                                    PS[:, 2048 * pp + 512 * tc:2048 * pp + 512 * tc + 512],
                                    ws2[0:64, 128 * ct:128 * (ct + 1)],
                                    dtanh[0:64, 512 * tc:512 * (tc + 1)],
                                    start=True, stop=True)
                            return ins

                        sl = len(outs) % 2
                        myuse = slot_uses[sl]
                        slot_uses[sl] += 1

                        def cf(v, ct=ct, pp=pp, sl=sl, myuse=myuse):
                            if myuse >= 1:
                                v.wait_ge(syp[sl], 16 * myuse)
                            return v.tensor_scalar(stgb_f[:, sl, :], tile(pp),
                                                   maas[:, 6, ct:ct + 1], None, ALU.add)
                        wvs_l.wave('v', tf, cf)
                        outs.append((len(wvs_l.items) - 1, kind, ct, sl))

        NW = len(wvs_l.items)

        @block.gpsimd
        def _(g):
            g.wait_ge(stg, kst)
            g.dma_start(out=xTs.ap(), in_=xT_d.ap().rearrange("(cc p) t -> p cc t", p=128)
                        ).then_inc(ldg, 16)
            g.dma_start(out=w1s.ap(), in_=mw1_d.ap().rearrange("(cc p) f -> p cc f", p=128)
                        ).then_inc(ldg, 16)
            g.dma_start(out=w2s.ap(), in_=mw2_d.ap().rearrange("f d c -> d f c")
                        ).then_inc(ldg, 16)
            g.dma_start(out=gw1.ap(), in_=gw1_d.ap().rearrange("(cc p) f -> p cc f", p=128)
                        ).then_inc(ldg, 16)
            g.dma_start(out=gw2.ap(), in_=gw2_d.ap()).then_inc(ldg, 16)
            g.dma_start(out=dw1.ap(), in_=dw1_d.ap().rearrange("(cc p) f -> p cc f", p=128)
                        ).then_inc(ldg, 16)
            g.dma_start(out=dw2.ap(), in_=dw2_d.ap()).then_inc(ldg, 16)
            kproj_end = 4 + 8 + 8
            vproj_end = kproj_end + 8 + 8
            for n, (Wd, need) in enumerate([(Wk_d, 0), (Wv_d, kproj_end),
                                            (Wr_d, vproj_end)]):
                if need:
                    g.wait_ge(wt, need)
                g.dma_start(out=Wbig.ap(), in_=Wd.ap().rearrange("(cc p) c -> p cc c", p=128)
                            ).then_inc(gw, 16)

        @block.sync
        def _(s):
            s.wait_ge(stg, kst)
            s.dma_start(out=maas.ap(), in_=maas_d.ap().rearrange("p (k e) -> p k e", k=7)
                        ).then_inc(ld, 16)
            nvec_done = nsca_done = 0
            done = {}
            for k, (tfn, ceng, cfn, lag) in enumerate(wvs_l.items):
                if ceng == 'v':
                    nvec_done += 1
                else:
                    nsca_done += 1
                done[k] = (nvec_done, nsca_done)
            for (widx, kind, ct, sl) in outs:
                if wvs_l.items[widx][1] == 'v':
                    s.wait_ge(wvv, done[widx][0])
                else:
                    s.wait_ge(wvs, done[widx][1])
                if kind in ("k", "r"):
                    dst = {"k": kT_o, "r": rT_o}[kind]
                    s.dma_start(out=dst.ap()[128 * ct:128 * (ct + 1), :],
                                in_=stgb[:, sl, :]).then_inc(syp[sl], 16)
                elif kind == "v":
                    s.dma_start(out=v_o.ap()[256 * ct:256 * (ct + 1), :]
                                .rearrange("(u p) c -> p u c", p=128),
                                in_=stgb[:, sl, :].rearrange("p (u c) -> p u c", u=2)
                                ).then_inc(syp[sl], 16)
                elif kind == "g":
                    s.dma_start(out=g_o.ap()[256 * ct:256 * (ct + 1), :]
                                .rearrange("(u p) c -> p u c", p=128),
                                in_=stgb[:, sl, :].rearrange("p (u c) -> p u c", u=2)
                                ).then_inc(syp[sl], 16)
                else:
                    s.dma_start(out=wrawT_o.ap()[128 * ct:128 * (ct + 1), :],
                                in_=stgb_f[:, sl, :]).then_inc(syp[sl], 16)
            s.wait_ge(syp[0], 16 * slot_uses[0])
            s.wait_ge(syp[1], 16 * slot_uses[1])
            s.sem_inc(stg, 1)

        orig_items = wvs_l.items

        @block.vector
        def _(v):
            v.wait_ge(stg, kst)
            v.wait_ge(ldg, 112)
            v.wait_ge(ld, 16)
            v.tensor_tensor(xxs.ap(), xTs[:, :, 0:T], xTs[:, :, 1:T + 1],
                            ALU.subtract).then_inc(vv, 1)
            nvst[0] += 1
            v.wait_ge(vv, nvst[0])
            v.tensor_tensor(cur.ap(), xxs.ap(),
                            maas[:, 0, :].rearrange("p (cc one) -> p cc one", one=1)
                            .broadcast_to([128, 8, T]), ALU.mult).then_inc(vv, 1)
            nvst[0] += 1
            v.wait_ge(vv, nvst[0])
            v.tensor_tensor(cur.ap(), cur.ap(), xTs[:, :, 1:T + 1],
                            ALU.add).then_inc(sv, 1)
            for k, (tfn, ceng, cfn, lag) in enumerate(orig_items):
                if ceng != 'v':
                    continue
                v.wait_ge(wt, k + 1)
                cfn(v).then_inc(wvv, 1)

        @block.scalar
        def _(s):
            s.wait_ge(stg, kst)
            s.wait_ge(ldg, 112)
            for k, (tfn, ceng, cfn, lag) in enumerate(orig_items):
                if ceng != 's':
                    continue
                s.wait_ge(wt, k + 1)
                cfn(s).then_inc(wvs, 1)

        @block.tensor
        def _(t):
            t.wait_ge(stg, kst)
            engs = [it[1] for it in orig_items]
            for k, (tfn, ceng, cfn, lag) in enumerate(orig_items):
                lim = max(0, k - lag)
                nvec = sum(1 for e in engs[:lim] if e == 'v')
                nsca = lim - nvec
                if nvec:
                    t.wait_ge(wvv, nvec)
                if nsca:
                    t.wait_ge(wvs, nsca)
                tfn(t).then_inc(wt, 1)


def _cmaj(vec):
    return np.ascontiguousarray(vec.reshape(8, 128).T)


def prep_l0front(inputs):
    x = inputs["L0_attn_x"]
    maas = np.stack([_cmaj(np.asarray(inputs[k], np.float32)) for k in
                     ["L0_attn_maa_x", "L0_attn_maa_w", "L0_attn_maa_k",
                      "L0_attn_maa_v", "L0_attn_maa_r", "L0_attn_maa_g",
                      "L0_attn_time_decay"]], axis=1)
    maas = np.ascontiguousarray(maas.reshape(128, 56), np.float32)
    shared = dict(
        maas=maas,
        maa_w1=np.asarray(inputs["L0_attn_maa_w1"], np.float32),
        maa_w2=np.asarray(inputs["L0_attn_maa_w2"], np.float32),
        Wk=np.asarray(inputs["L0_attn_Wk"], np.float32),
        Wv=np.asarray(inputs["L0_attn_Wv"], np.float32),
        Wr=np.asarray(inputs["L0_attn_Wr"], np.float32),
        gate_w1=np.asarray(inputs["L0_attn_gate_w1"], np.float32),
        gate_w2=np.asarray(inputs["L0_attn_gate_w2"], np.float32),
        decay_w1=np.asarray(inputs["L0_attn_decay_w1"], np.float32),
        decay_w2=np.asarray(inputs["L0_attn_decay_w2"], np.float32),
    )
    maps = []
    for c in range(8):
        b, hf = c // 2, c % 2
        xb = np.asarray(x[b], np.float32)
        sl = xb[2048 * hf - 1: 2048 * (hf + 1)].T if hf else None
        if hf == 0:
            xT = np.concatenate([np.zeros((C, 1), np.float32), xb[0:2048].T], axis=1)
        else:
            xT = xb[2047:4096].T
        m = dict(shared)
        m["xT"] = np.ascontiguousarray(xT)
        maps.append(m)
    return maps




TW = 4096
LCH = 64
NCHW = TW // LCH
NWW = 5


def build_wkv():
    nc = bass.Bass("TRN2", target_bir_lowering=False)
    rT_d = nc.dram_tensor("rT", [512, TW], BF16, kind="ExternalInput")
    kT_d = nc.dram_tensor("kT", [512, TW], BF16, kind="ExternalInput")
    wT_d = nc.dram_tensor("wT", [512, TW], F32, kind="ExternalInput")
    v_d = nc.dram_tensor("v", [TW, 512], BF16, kind="ExternalInput")
    ubc_d = nc.dram_tensor("ubc", [128, 256], F32, kind="ExternalInput")
    mask_d = nc.dram_tensor("maskst", [128, 64], F32, kind="ExternalInput")
    eye_d = nc.dram_tensor("eyem", [128, 64], F32, kind="ExternalInput")
    ident_d = nc.dram_tensor("ident", [128, 128], BF16, kind="ExternalInput")
    y_d = nc.dram_tensor("y", [TW, 512], BF16, kind="ExternalOutput")

    with ExitStack() as ctx:
        E = ctx.enter_context
        rT = E(nc.sbuf_tensor("rT_s", [128, 4 * TW], BF16))
        kT = E(nc.sbuf_tensor("kT_s", [128, 4 * TW], BF16))
        v = E(nc.sbuf_tensor("v_s", [128, 32 * 512], BF16))
        ytok = E(nc.sbuf_tensor("ytok", [128, 32 * 512], BF16))
        ubc = E(nc.sbuf_tensor("ubc_s", [128, 256], BF16))
        maskst = E(nc.sbuf_tensor("mask_s", [128, 64], BF16))
        eyem = E(nc.sbuf_tensor("eye_s", [128, 64], BF16))
        idt = E(nc.sbuf_tensor("id_s", [128, 128], BF16))
        ones_s = E(nc.sbuf_tensor("ones_s", [128, 64], F32))
        wch = E(nc.sbuf_tensor("wch_s", [128, 2, 256], F32))
        ew = E(nc.sbuf_tensor("ew_s", [128, 2, 256], F32))
        acum = E(nc.sbuf_tensor("ac_s", [128, 2, 256], F32))
        pexp = E(nc.sbuf_tensor("pexp_s", [128, 2, 256], BF16))
        pm = E(nc.sbuf_tensor("pm_s", [128, 2, 256], F32))
        pmb = E(nc.sbuf_tensor("pmb_s", [128, 2, 256], BF16))
        rt_t = E(nc.sbuf_tensor("rt_s", [128, 2, 256], BF16))
        kt_t = E(nc.sbuf_tensor("ktl_s", [128, 2, 256], BF16))
        kh_t = E(nc.sbuf_tensor("kh_s", [128, 2, 256], BF16))
        uk_t = E(nc.sbuf_tensor("uk_s", [128, 2, 256], BF16))
        a_sb = E(nc.sbuf_tensor("a_sb", [128, 512], F32))
        ahat = E(nc.sbuf_tensor("ahat_s", [128, 512], BF16))
        ahat2 = E(nc.sbuf_tensor("ahat2_s", [128, 512], BF16))
        khtok = E(nc.sbuf_tensor("khtok_s", [128, 512], BF16))
        sf32 = E(nc.sbuf_tensor("sf32_s", [128, 256], F32))
        sbf = E(nc.sbuf_tensor("sbf_s", [128, 256], BF16))
        ysb = E(nc.sbuf_tensor("ysb_s", [128, 512], BF16))
        PS = E(nc.psum_tensor("PS", [128, 4096], F32))
        ld = E(nc.semaphore("ld"))
        ldg = E(nc.semaphore("ldg"))
        vv = E(nc.semaphore("vv"))
        wld0 = E(nc.semaphore("wld0"))
        wld1 = E(nc.semaphore("wld1"))
        wldp = [wld0, wld1]
        se1 = E(nc.semaphore("se1"))
        sv0 = E(nc.semaphore("sv0"))
        se2 = E(nc.semaphore("se2"))
        v1 = E(nc.semaphore("v1"))
        wt = E(nc.semaphore("wt"))
        wv = E(nc.semaphore("wv"))
        sto = E(nc.semaphore("sto"))
        block = E(nc.Block())
        PSB = PS.ap().bitcast(BF16)

        def opchunk(buf, ch):
            return buf.ap().rearrange("p (cc t) -> p cc t", cc=4)[:, :, ch * LCH:(ch + 1) * LCH]

        def scr3(buf, pr):
            return buf[:, pr, :].rearrange("p (cc t) -> p cc t", cc=4)

        @block.sync
        def _(s):
            wsrc = wT_d.ap().rearrange("(cc p) t -> p cc t", p=128)
            for ch in range(NCHW):
                pr = ch % 2
                if ch >= 2:
                    s.wait_ge(se1, ch - 1)
                s.dma_start(out=wch[:, pr, :].rearrange("p (cc t) -> p cc t", cc=4),
                            in_=wsrc[:, :, ch * LCH:(ch + 1) * LCH]).then_inc(wldp[pr], 16)

        @block.gpsimd
        def _(g):
            g.dma_start(out=rT.ap().rearrange("p (cc t) -> p cc t", cc=4),
                        in_=rT_d.ap().rearrange("(cc p) t -> p cc t", p=128)).then_inc(ldg, 16)
            g.dma_start(out=kT.ap().rearrange("p (cc t) -> p cc t", cc=4),
                        in_=kT_d.ap().rearrange("(cc p) t -> p cc t", p=128)).then_inc(ldg, 16)
            g.dma_start(out=v.ap().rearrange("p (tt c) -> p tt c", tt=32),
                        in_=v_d.ap().rearrange("(tt p) c -> p tt c", p=128)).then_inc(ldg, 16)
            g.dma_start(out=ubc.ap(), in_=ubc_d.ap()).then_inc(ldg, 16)
            g.dma_start(out=maskst.ap(), in_=mask_d.ap()).then_inc(ldg, 16)
            g.dma_start(out=eyem.ap(), in_=eye_d.ap()).then_inc(ldg, 16)
            g.dma_start(out=idt.ap(), in_=ident_d.ap()).then_inc(ldg, 16)
            g.wait_ge(wv, NCHW * NWW)
            g.dma_start(out=y_d.ap().rearrange("(tt p) c -> p tt c", p=128),
                        in_=ytok.ap().rearrange("p (tt c) -> p tt c", tt=32)).then_inc(sto, 16)
            g.wait_ge(sto, 16)

        @block.scalar
        def _(sc):
            sc.wait_ge(ldg, 112)
            for ch in range(NCHW):
                pr = ch % 2
                if ch >= 2:
                    sc.wait_ge(sv0, ch - 1)
                sc.wait_ge(wldp[pr], 16 * (ch // 2 + 1))
                sc.activation(ew[:, pr, :], wch[:, pr, :], AF.Exp).then_inc(se1, 1)
                sc.wait_ge(sv0, ch + 1)
                sc.activation(pexp[:, pr, :], acum[:, pr, :], AF.Exp)
                sc.activation(pm[:, pr, :], acum[:, pr, :], AF.Exp, scale=-1.0).then_inc(se2, 1)
                sc.wait_ge(se2, 2 * ch + 1)
                sc.activation(pmb[:, pr, :], pm[:, pr, :], AF.Copy).then_inc(se2, 1)

        @block.vector
        def _(v_):
            v_.wait_ge(ldg, 112)
            nv = 0
            v_.memset(ones_s[:, :], 1.0)
            v_.memset(sf32[:, :], 0.0)
            v_.memset(sbf[:, :], 0.0).then_inc(vv, 1)
            nv += 1
            v_.wait_ge(vv, nv)
            for ch in range(NCHW):
                pr = ch % 2
                ph = ch % 2
                tt = ch // 2
                pb = 64 * ph
                v_.wait_ge(se1, ch + 1)
                if ch >= 2:
                    v_.wait_ge(wt, NWW * (ch - 1))
                for cc in range(4):
                    ins = v_.tensor_tensor_scan(
                        acum[:, pr, cc * LCH:(cc + 1) * LCH], ew[:, pr, cc * LCH:(cc + 1) * LCH],
                        ones_s[:, :], 0.0, ALU.add, ALU.mult)
                ins.then_inc(sv0, 1)
                v_.wait_ge(se2, 2 * (ch + 1))
                pmb3 = pmb[:, pr, :].rearrange("p (cc t) -> p cc t", cc=4)
                v_.tensor_tensor(scr3(rt_t, pr)[:, :, 1:LCH], opchunk(rT, ch)[:, :, 1:LCH],
                                 pmb3[:, :, 0:LCH - 1], ALU.mult)
                v_.tensor_copy(scr3(rt_t, pr)[:, :, 0:1], opchunk(rT, ch)[:, :, 0:1])
                v_.tensor_tensor(scr3(kt_t, pr), opchunk(kT, ch), scr3(pexp, pr),
                                 ALU.mult).then_inc(vv, 1)
                nv += 1
                v_.wait_ge(vv, nv)
                v_.tensor_tensor(scr3(kh_t, pr), scr3(kt_t, pr),
                                 pmb3[:, :, LCH - 1:LCH].broadcast_to([128, 4, LCH]), ALU.mult)
                v_.tensor_tensor(scr3(uk_t, pr), opchunk(kT, ch),
                                 ubc.ap().rearrange("p (cc t) -> p cc t", cc=4),
                                 ALU.mult).then_inc(v1, 1)
                ps8 = PS[pb:pb + 64, :].rearrange("p (b q) -> p b q", b=8)
                psb8 = PSB[pb:pb + 64, :].rearrange("p (b q) -> p b q", b=8)
                v_.wait_ge(wt, ch * NWW + 1)
                v_.tensor_copy(a_sb[pb:pb + 64, :].rearrange("p (h q) -> p h q", h=8),
                               ps8[:, :, 0:64])
                v_.tensor_copy(khtok[pb:pb + 64, :].rearrange("p (j q) -> p j q", j=4),
                               psb8[:, 0:4, 512:640]).then_inc(wv, 1)
                v_.wait_ge(wt, ch * NWW + 2)
                mb = maskst[pb:pb + 64, :].rearrange("p (one q) -> p one q", one=1) \
                    .broadcast_to([64, 8, 64])
                eb = eyem[pb:pb + 64, :].rearrange("p (one q) -> p one q", one=1) \
                    .broadcast_to([64, 8, 64])
                v_.tensor_tensor(ahat[pb:pb + 64, :].rearrange("p (h q) -> p h q", h=8),
                                 a_sb[pb:pb + 64, :].rearrange("p (h q) -> p h q", h=8),
                                 mb, ALU.mult)
                v_.tensor_tensor(ahat2[pb:pb + 64, :].rearrange("p (h q) -> p h q", h=8),
                                 ps8[:, :, 0:64], eb, ALU.mult).then_inc(vv, 1)
                nv += 1
                v_.wait_ge(vv, nv)
                v_.tensor_tensor(ahat[pb:pb + 64, :], ahat[pb:pb + 64, :],
                                 ahat2[pb:pb + 64, :], ALU.add).then_inc(wv, 1)
                v_.wait_ge(wt, ch * NWW + 3)
                v_.tensor_copy(ysb[pb:pb + 64, :].rearrange("p (h q) -> p h q", h=8),
                               ps8[:, :, 0:64]).then_inc(wv, 1)
                v_.wait_ge(wt, ch * NWW + 4)
                v_.tensor_tensor(ytok[pb:pb + 64, tt * 512:(tt + 1) * 512]
                                 .rearrange("p (h q) -> p h q", h=8),
                                 ps8[:, :, 0:64],
                                 ysb[pb:pb + 64, :].rearrange("p (h q) -> p h q", h=8),
                                 ALU.add).then_inc(wv, 1)
                v_.wait_ge(wt, ch * NWW + 5)
                pmB = pm[:, pr, :].rearrange("p (cc q) -> p cc q", cc=4)
                v_.tensor_tensor(sf32.ap().rearrange("p (j q) -> p j q", j=4),
                                 sf32.ap().rearrange("p (j q) -> p j q", j=4),
                                 pmB[:, :, LCH - 1:LCH].broadcast_to([128, 4, 64]),
                                 ALU.mult).then_inc(vv, 1)
                nv += 1
                v_.wait_ge(vv, nv)
                v_.tensor_tensor(sf32[0:64, :].rearrange("p (j q) -> p j q", j=4),
                                 sf32[0:64, :].rearrange("p (j q) -> p j q", j=4),
                                 PS[0:64, :].rearrange("p (b q) -> p b q", b=8)[:, 0:8:2, 0:64],
                                 ALU.add)
                v_.tensor_tensor(sf32[64:128, :].rearrange("p (j q) -> p j q", j=4),
                                 sf32[64:128, :].rearrange("p (j q) -> p j q", j=4),
                                 PS[64:128, :].rearrange("p (b q) -> p b q", b=8)[:, 1:8:2, 0:64],
                                 ALU.add).then_inc(vv, 1)
                nv += 1
                v_.wait_ge(vv, nv)
                v_.tensor_copy(sbf[:, :], sf32[:, :]).then_inc(wv, 1)

        @block.tensor
        def _(t):
            t.wait_ge(ldg, 112)
            for ch in range(NCHW):
                pr = ch % 2
                ph = ch % 2
                tt = ch // 2
                pb = 64 * ph
                t.wait_ge(v1, ch + 1)
                gk = ch * NWW
                if gk >= 1:
                    t.wait_ge(wv, gk)
                for hl in range(8):
                    j, hp = hl // 2, hl % 2
                    cb = 64 * hp
                    t.matmul(PS[pb:pb + 64, 512 * hl:512 * hl + 64],
                             kt_t[cb:cb + 64, pr, 64 * j:64 * (j + 1)],
                             rt_t[cb:cb + 64, pr, 64 * j:64 * (j + 1)],
                             start=True, stop=True)
                for j in range(4):
                    ins = t.transpose(PSB[pb:pb + 64, 1024 * j + 512:1024 * j + 640],
                                      kh_t[:, pr, 64 * j:64 * (j + 1)], idt[:, :])
                ins.then_inc(wt, 1)
                gk += 1
                t.wait_ge(wv, gk)
                for hl in range(8):
                    j, hp = hl // 2, hl % 2
                    cb = 64 * hp
                    ins = t.matmul(PS[pb:pb + 64, 512 * hl:512 * hl + 64],
                                   uk_t[cb:cb + 64, pr, 64 * j:64 * (j + 1)],
                                   opchunk(rT, ch)[cb:cb + 64, j, :],
                                   start=True, stop=True)
                ins.then_inc(wt, 1)
                gk += 1
                t.wait_ge(wv, gk)
                for hl in range(8):
                    vblk = v[pb:pb + 64, tt * 512 + 64 * hl: tt * 512 + 64 * (hl + 1)]
                    ins = t.matmul(PS[pb:pb + 64, 512 * hl:512 * hl + 64],
                                   ahat[pb:pb + 64, 64 * hl:64 * (hl + 1)], vblk,
                                   start=True, stop=True)
                ins.then_inc(wt, 1)
                gk += 1
                t.wait_ge(wv, gk)
                for hl in range(8):
                    j, hp = hl // 2, hl % 2
                    cb = 64 * hp
                    ins = t.matmul(PS[pb:pb + 64, 512 * hl:512 * hl + 64],
                                   rt_t[cb:cb + 64, pr, 64 * j:64 * (j + 1)],
                                   sbf[cb:cb + 64, 64 * j:64 * (j + 1)],
                                   start=True, stop=True)
                ins.then_inc(wt, 1)
                gk += 1
                t.wait_ge(wv, gk)
                for hl in range(8):
                    j, hp = hl // 2, hl % 2
                    vblk = v[pb:pb + 64, tt * 512 + 64 * hl: tt * 512 + 64 * (hl + 1)]
                    ins = t.matmul(PS[64 * hp:64 * hp + 64, 512 * hl:512 * hl + 64],
                                   khtok[pb:pb + 64, 128 * j + 64 * hp:128 * j + 64 * (hp + 1)],
                                   vblk, start=True, stop=True)
                ins.then_inc(wt, 1)
                gk += 1
    return nc


def prep_wkv(front_outs, u_full):
    maps = []
    maskst = np.zeros((128, 64), np.float32)
    eyem = np.zeros((128, 64), np.float32)
    for p in range(128):
        for tcol in range(64):
            if (p % 64) < tcol:
                maskst[p, tcol] = 1.0
            if (p % 64) == tcol:
                eyem[p, tcol] = 1.0
    ident = np.eye(128, dtype=BF)
    for c in range(8):
        b, hg = c // 2, c % 2
        csl = slice(512 * hg, 512 * (hg + 1))
        rT = np.concatenate([np.asarray(front_outs[2 * b]["rT_o"])[csl],
                             np.asarray(front_outs[2 * b + 1]["rT_o"])[csl]], axis=1)
        kT = np.concatenate([np.asarray(front_outs[2 * b]["kT_o"])[csl],
                             np.asarray(front_outs[2 * b + 1]["kT_o"])[csl]], axis=1)
        wT = np.concatenate([np.asarray(front_outs[2 * b]["wrawT_o"])[csl],
                             np.asarray(front_outs[2 * b + 1]["wrawT_o"])[csl]], axis=1)
        vv_ = np.concatenate([np.asarray(front_outs[2 * b]["v_o"])[:, csl],
                              np.asarray(front_outs[2 * b + 1]["v_o"])[:, csl]], axis=0)
        ubc = np.zeros((128, 256), np.float32)
        u = np.asarray(u_full, np.float32)
        for hl in range(8):
            j, hp = hl // 2, hl % 2
            ubc[64 * hp:64 * hp + 64, 64 * j:64 * (j + 1)] = u[8 * hg + hl][:, None]
        maps.append(dict(rT=np.ascontiguousarray(rT), kT=np.ascontiguousarray(kT),
                         wT=np.ascontiguousarray(wT), v=np.ascontiguousarray(vv_),
                         ubc=ubc, maskst=maskst, eyem=eyem, ident=ident))
    return maps



PAIRS4x2 = [[0, 1], [2, 3], [4, 5], [6, 7]]


def build_wkv2():
    nc = bass.Bass("TRN2", target_bir_lowering=False)
    io = dict(
        rTf=nc.dram_tensor("rTf", [1024, T], BF16, kind="ExternalInput"),
        kTf=nc.dram_tensor("kTf", [1024, T], BF16, kind="ExternalInput"),
        wTf=nc.dram_tensor("wTf", [1024, T], F32, kind="ExternalInput"),
        vf=nc.dram_tensor("vf", [T, 1024], BF16, kind="ExternalInput"),
        ubc2=nc.dram_tensor("ubc2", [128, 512], F32, kind="ExternalInput"),
        maskst=nc.dram_tensor("maskst", [128, 64], F32, kind="ExternalInput"),
        eyem=nc.dram_tensor("eyem", [128, 64], F32, kind="ExternalInput"),
        ident=nc.dram_tensor("ident", [128, 128], BF16, kind="ExternalInput"),
        usin=nc.dram_tensor("usin", [128, 1], F32, kind="ExternalInput"),
        y2=nc.dram_tensor("y2", [T, 1024], BF16, kind="ExternalOutput"),
        sfb=nc.dram_tensor("sfb", [128, 512], F32, kind="Internal"),
        agS=nc.dram_tensor("agS", [256, 512], F32, kind="Internal"),
    )
    with ExitStack() as ES:
        stgs = ES.enter_context(nc.semaphore("stgs"))
        build_wkv2_body(nc, ES, io, "", stgs, 0)
    return nc


def build_wkv2_body(nc, ES, io, px, stg, kst):
    rT_d, kT_d, wT_d, v_d = io["rTf"], io["kTf"], io["wTf"], io["vf"]
    ubc_d, mask_d, eye_d, ident_d = (io["ubc2"], io["maskst"], io["eyem"],
                                     io["ident"])
    usin_d, y_d, sfb_d, agS_d = io["usin"], io["y2"], io["sfb"], io["agS"]
    NCH = 64

    def S(n):
        return ES.enter_context(nc.semaphore(px + n))

    with ExitStack() as ctx:
        E = ctx.enter_context
        rT = E(nc.sbuf_tensor(px + "rT_s", [128, 8 * T], BF16))
        kT = E(nc.sbuf_tensor(px + "kT_s", [128, 8 * T], BF16))
        v = E(nc.sbuf_tensor(px + "v_s", [128, 16 * 1024], BF16))
        ytok = E(nc.sbuf_tensor(px + "ytok", [128, 16 * 1024], BF16))
        Rhat = E(nc.sbuf_tensor(px + "Rhat", [128, 8, T], BF16))
        ubc = E(nc.sbuf_tensor(px + "ubc_s", [128, 2, 256], BF16))
        maskst = E(nc.sbuf_tensor(px + "mask_s", [128, 64], BF16))
        eyem = E(nc.sbuf_tensor(px + "eye_s", [128, 64], BF16))
        idt = E(nc.sbuf_tensor(px + "id_s", [128, 128], BF16))
        ones_s = E(nc.sbuf_tensor(px + "ones_s", [128, 64], F32))
        wch = E(nc.sbuf_tensor(px + "wch_s", [128, 2, 256], F32))
        ew = E(nc.sbuf_tensor(px + "ew_s", [128, 2, 256], F32))
        acum = E(nc.sbuf_tensor(px + "ac_s", [128, 2, 256], F32))
        pexp = E(nc.sbuf_tensor(px + "pexp_s", [128, 2, 256], BF16))
        pm = E(nc.sbuf_tensor(px + "pm_s", [128, 2, 256], F32))
        pmb = E(nc.sbuf_tensor(px + "pmb_s", [128, 2, 256], BF16))
        rt_t = E(nc.sbuf_tensor(px + "rt_s", [128, 2, 256], BF16))
        kt_t = E(nc.sbuf_tensor(px + "ktl_s", [128, 2, 256], BF16))
        kh_t = E(nc.sbuf_tensor(px + "kh_s", [128, 2, 256], BF16))
        uk_t = E(nc.sbuf_tensor(px + "uk_s", [128, 2, 256], BF16))
        a_sb = E(nc.sbuf_tensor(px + "a_sb", [128, 512], F32))
        ahat = E(nc.sbuf_tensor(px + "ahat_s", [128, 512], BF16))
        ahat2 = E(nc.sbuf_tensor(px + "ahat2_s", [128, 512], BF16))
        khtok = E(nc.sbuf_tensor(px + "khtok_s", [128, 512], BF16))
        sf32 = E(nc.sbuf_tensor(px + "sf32_s", [128, 256], F32))
        sbf = E(nc.sbuf_tensor(px + "sbf_s", [128, 256], BF16))
        ysb = E(nc.sbuf_tensor(px + "ysb_s", [128, 512], BF16))
        Poff = E(nc.sbuf_tensor(px + "Poff", [128, 8], F32))
        Sfin = E(nc.sbuf_tensor(px + "Sfin", [128, 8, 64], F32))
        Sin = E(nc.sbuf_tensor(px + "Sin", [128, 8, 64], F32))
        Sinb = E(nc.sbuf_tensor(px + "Sinb", [128, 8, 64], BF16))
        usin_s = E(nc.sbuf_tensor(px + "usin_s", [128, 1], F32))
        PS = E(nc.psum_tensor(px + "PS", [128, 4096], F32))
        ldg = S("ldg")
        vv = S("vv")
        wld0 = S("wld0")
        wld1 = S("wld1")
        wldp = [wld0, wld1]
        se1 = S("se1")
        sv0 = S("sv0")
        se2 = S("se2")
        v1 = S("v1")
        wt = S("wt")
        wv = S("wv")
        sto = S("sto")
        gs2 = S("gs2")
        gcc = S("gcc")
        ct = S("ct")
        cv = S("cv")
        block = E(nc.Block())
        PSB = PS.ap().bitcast(BF16)
        WVEND = NCH * NWW + 1
        WVSIN = WVEND + 1
        NCORR = 2 * 16 * 8

        def opchunk(buf, ch):
            grp, chl = ch // 32, ch % 32
            return buf.ap().rearrange("p (cc t) -> p cc t", cc=8)[
                :, 4 * grp:4 * grp + 4, chl * LCH:(chl + 1) * LCH]

        def scr3(buf, pr):
            return buf[:, pr, :].rearrange("p (cc t) -> p cc t", cc=4)

        corr_order = [(g2, tb, hl) for g2 in range(2) for tb in range(16)
                      for hl in range(8)]

        @block.sync
        def _(s):
            s.wait_ge(stg, kst)
            wsrc = wT_d.ap().rearrange("(cc p) t -> p cc t", p=128)
            for ch in range(NCH):
                pr = ch % 2
                grp, chl = ch // 32, ch % 32
                if ch >= 2:
                    s.wait_ge(se1, ch - 1)
                s.dma_start(out=wch[:, pr, :].rearrange("p (cc t) -> p cc t", cc=4),
                            in_=wsrc[:, 4 * grp:4 * grp + 4,
                                     chl * LCH:(chl + 1) * LCH]).then_inc(wldp[pr], 16)

        @block.gpsimd
        def _(g):
            g.wait_ge(stg, kst)
            g.dma_start(out=rT.ap().rearrange("p (cc t) -> p cc t", cc=8),
                        in_=rT_d.ap().rearrange("(cc p) t -> p cc t", p=128)).then_inc(ldg, 16)
            g.dma_start(out=kT.ap().rearrange("p (cc t) -> p cc t", cc=8),
                        in_=kT_d.ap().rearrange("(cc p) t -> p cc t", p=128)).then_inc(ldg, 16)
            g.dma_start(out=v.ap().rearrange("p (tt c) -> p tt c", tt=16),
                        in_=v_d.ap().rearrange("(tt p) c -> p tt c", p=128)).then_inc(ldg, 16)
            g.dma_start(out=ubc.ap().rearrange("p g t -> p (g t)"),
                        in_=ubc_d.ap()).then_inc(ldg, 16)
            g.dma_start(out=maskst.ap(), in_=mask_d.ap()).then_inc(ldg, 16)
            g.dma_start(out=eyem.ap(), in_=eye_d.ap()).then_inc(ldg, 16)
            g.dma_start(out=idt.ap(), in_=ident_d.ap()).then_inc(ldg, 16)
            g.dma_start(out=usin_s.ap(), in_=usin_d.ap()).then_inc(ldg, 16)
            g.wait_ge(wv, WVEND)
            g.dma_start(out=sfb_d.ap(),
                        in_=Sfin.ap().rearrange("p a b -> p (a b)")).then_inc(gs2, 16)
            g.wait_ge(gs2, 16)
            g.collective_compute(
                "AllGather", ALU.bypass, replica_groups=PAIRS4x2,
                ins=[sfb_d.ap()], outs=[agS_d.ap()]).then_inc(gcc, 1)
            g.wait_ge(gcc, 1)
            g.dma_start(out=Sin.ap().rearrange("p a b -> p (a b)"),
                        in_=agS_d.ap()[0:128, :]).then_inc(gs2, 16)
            g.wait_ge(gs2, 32)
            g.wait_ge(cv, NCORR)
            g.dma_start(out=y_d.ap().rearrange("(tt p) c -> p tt c", p=128),
                        in_=ytok.ap().rearrange("p (tt c) -> p tt c", tt=16)).then_inc(sto, 16)
            g.wait_ge(sto, 16)
            g.sem_inc(stg, 1)

        @block.scalar
        def _(sc):
            sc.wait_ge(stg, kst)
            sc.wait_ge(ldg, 128)
            for ch in range(NCH):
                pr = ch % 2
                if ch >= 2:
                    sc.wait_ge(sv0, ch - 1)
                sc.wait_ge(wldp[pr], 16 * (ch // 2 + 1))
                sc.activation(ew[:, pr, :], wch[:, pr, :], AF.Exp).then_inc(se1, 1)
                sc.wait_ge(sv0, ch + 1)
                sc.activation(pexp[:, pr, :], acum[:, pr, :], AF.Exp)
                sc.activation(pm[:, pr, :], acum[:, pr, :], AF.Exp, scale=-1.0).then_inc(se2, 1)
                sc.wait_ge(se2, 2 * ch + 1)
                sc.activation(pmb[:, pr, :], pm[:, pr, :], AF.Copy).then_inc(se2, 1)

        @block.vector
        def _(v_):
            v_.wait_ge(stg, kst)
            v_.wait_ge(ldg, 128)
            nv = 0
            v_.memset(ones_s[:, :], 1.0)
            v_.memset(sf32[:, :], 0.0)
            v_.memset(Poff[:, :], 1.0)
            v_.memset(sbf[:, :], 0.0).then_inc(vv, 1)
            nv += 1
            v_.wait_ge(vv, nv)
            for ch in range(NCH):
                pr = ch % 2
                ph = ch % 2
                grp, chl = ch // 32, ch % 32
                tt = chl // 2
                pb = 64 * ph
                if ch == 32:
                    v_.wait_ge(wv, NWW * 32)
                    v_.tensor_copy(Sfin[:, 0:4, :],
                                   sf32.ap().rearrange("p (j q) -> p j q", j=4)
                                   ).then_inc(vv, 1)
                    nv += 1
                    v_.wait_ge(vv, nv)
                    v_.memset(sf32[:, :], 0.0)
                    v_.memset(sbf[:, :], 0.0).then_inc(vv, 1)
                    nv += 1
                    v_.wait_ge(vv, nv)
                v_.wait_ge(se1, ch + 1)
                if ch >= 2:
                    v_.wait_ge(wt, NWW * (ch - 1))
                for cc in range(4):
                    ins = v_.tensor_tensor_scan(
                        acum[:, pr, cc * LCH:(cc + 1) * LCH], ew[:, pr, cc * LCH:(cc + 1) * LCH],
                        ones_s[:, :], 0.0, ALU.add, ALU.mult)
                ins.then_inc(sv0, 1)
                v_.wait_ge(se2, 2 * (ch + 1))
                pmb3 = pmb[:, pr, :].rearrange("p (cc t) -> p cc t", cc=4)
                v_.tensor_tensor(scr3(rt_t, pr)[:, :, 1:LCH], opchunk(rT, ch)[:, :, 1:LCH],
                                 pmb3[:, :, 0:LCH - 1], ALU.mult)
                v_.tensor_copy(scr3(rt_t, pr)[:, :, 0:1], opchunk(rT, ch)[:, :, 0:1])
                v_.tensor_tensor(scr3(kt_t, pr), opchunk(kT, ch), scr3(pexp, pr),
                                 ALU.mult).then_inc(vv, 1)
                nv += 1
                v_.wait_ge(vv, nv)
                v_.tensor_tensor(Rhat.ap()[:, 4 * grp:4 * grp + 4, chl * LCH:(chl + 1) * LCH],
                                 scr3(rt_t, pr),
                                 Poff[:, 4 * grp:4 * grp + 4]
                                 .rearrange("p (j one) -> p j one", one=1)
                                 .broadcast_to([128, 4, LCH]), ALU.mult).then_inc(vv, 1)
                nv += 1
                v_.wait_ge(vv, nv)
                v_.tensor_tensor(Poff[:, 4 * grp:4 * grp + 4]
                                 .rearrange("p (j one) -> p j one", one=1),
                                 Poff[:, 4 * grp:4 * grp + 4]
                                 .rearrange("p (j one) -> p j one", one=1),
                                 pm[:, pr, :].rearrange("p (cc t) -> p cc t", cc=4)
                                 [:, :, LCH - 1:LCH], ALU.mult)
                v_.tensor_tensor(scr3(kh_t, pr), scr3(kt_t, pr),
                                 pmb3[:, :, LCH - 1:LCH].broadcast_to([128, 4, LCH]), ALU.mult)
                v_.tensor_tensor(scr3(uk_t, pr), opchunk(kT, ch),
                                 ubc[:, grp, :].rearrange("p (cc t) -> p cc t", cc=4),
                                 ALU.mult).then_inc(v1, 1)
                ps8 = PS[pb:pb + 64, :].rearrange("p (b q) -> p b q", b=8)
                psb8 = PSB[pb:pb + 64, :].rearrange("p (b q) -> p b q", b=8)
                v_.wait_ge(wt, ch * NWW + 1)
                v_.tensor_copy(a_sb[pb:pb + 64, :].rearrange("p (h q) -> p h q", h=8),
                               ps8[:, :, 0:64])
                v_.tensor_copy(khtok[pb:pb + 64, :].rearrange("p (j q) -> p j q", j=4),
                               psb8[:, 0:4, 512:640]).then_inc(wv, 1)
                v_.wait_ge(wt, ch * NWW + 2)
                mb = maskst[pb:pb + 64, :].rearrange("p (one q) -> p one q", one=1) \
                    .broadcast_to([64, 8, 64])
                eb = eyem[pb:pb + 64, :].rearrange("p (one q) -> p one q", one=1) \
                    .broadcast_to([64, 8, 64])
                v_.tensor_tensor(ahat[pb:pb + 64, :].rearrange("p (h q) -> p h q", h=8),
                                 a_sb[pb:pb + 64, :].rearrange("p (h q) -> p h q", h=8),
                                 mb, ALU.mult)
                v_.tensor_tensor(ahat2[pb:pb + 64, :].rearrange("p (h q) -> p h q", h=8),
                                 ps8[:, :, 0:64], eb, ALU.mult).then_inc(vv, 1)
                nv += 1
                v_.wait_ge(vv, nv)
                v_.tensor_tensor(ahat[pb:pb + 64, :], ahat[pb:pb + 64, :],
                                 ahat2[pb:pb + 64, :], ALU.add).then_inc(wv, 1)
                v_.wait_ge(wt, ch * NWW + 3)
                v_.tensor_copy(ysb[pb:pb + 64, :].rearrange("p (h q) -> p h q", h=8),
                               ps8[:, :, 0:64]).then_inc(wv, 1)
                v_.wait_ge(wt, ch * NWW + 4)
                v_.tensor_tensor(ytok[pb:pb + 64, tt * 1024 + 512 * grp:
                                      tt * 1024 + 512 * grp + 512]
                                 .rearrange("p (h q) -> p h q", h=8),
                                 ps8[:, :, 0:64],
                                 ysb[pb:pb + 64, :].rearrange("p (h q) -> p h q", h=8),
                                 ALU.add).then_inc(wv, 1)
                v_.wait_ge(wt, ch * NWW + 5)
                pmB = pm[:, pr, :].rearrange("p (cc q) -> p cc q", cc=4)
                v_.tensor_tensor(sf32.ap().rearrange("p (j q) -> p j q", j=4),
                                 sf32.ap().rearrange("p (j q) -> p j q", j=4),
                                 pmB[:, :, LCH - 1:LCH].broadcast_to([128, 4, 64]),
                                 ALU.mult).then_inc(vv, 1)
                nv += 1
                v_.wait_ge(vv, nv)
                v_.tensor_tensor(sf32[0:64, :].rearrange("p (j q) -> p j q", j=4),
                                 sf32[0:64, :].rearrange("p (j q) -> p j q", j=4),
                                 PS[0:64, :].rearrange("p (b q) -> p b q", b=8)[:, 0:8:2, 0:64],
                                 ALU.add)
                v_.tensor_tensor(sf32[64:128, :].rearrange("p (j q) -> p j q", j=4),
                                 sf32[64:128, :].rearrange("p (j q) -> p j q", j=4),
                                 PS[64:128, :].rearrange("p (b q) -> p b q", b=8)[:, 1:8:2, 0:64],
                                 ALU.add).then_inc(vv, 1)
                nv += 1
                v_.wait_ge(vv, nv)
                v_.tensor_copy(sbf[:, :], sf32[:, :]).then_inc(wv, 1)
            v_.wait_ge(wv, NWW * NCH)
            v_.tensor_copy(Sfin[:, 4:8, :],
                           sf32.ap().rearrange("p (j q) -> p j q", j=4)).then_inc(wv, 1)
            v_.wait_ge(gs2, 32)
            v_.tensor_scalar(Sin.ap().rearrange("p a b -> p (a b)"),
                             Sin.ap().rearrange("p a b -> p (a b)"),
                             usin_s[:, 0:1], None, ALU.mult).then_inc(vv, 1)
            nv += 1
            v_.wait_ge(vv, nv)
            v_.tensor_copy(Sinb.ap(), Sin.ap()).then_inc(wv, 1)
            for n, (g2, tb, hl) in enumerate(corr_order):
                v_.wait_ge(ct, n + 1)
                sl = tb * 1024 + 512 * g2 + 64 * hl
                v_.tensor_tensor(ytok[:, sl:sl + 64], ytok[:, sl:sl + 64],
                                 PS[:, 512 * hl:512 * hl + 64], ALU.add).then_inc(cv, 1)

        @block.tensor
        def _(t):
            t.wait_ge(stg, kst)
            t.wait_ge(ldg, 128)
            for ch in range(NCH):
                pr = ch % 2
                ph = ch % 2
                grp, chl = ch // 32, ch % 32
                tt = chl // 2
                pb = 64 * ph
                t.wait_ge(v1, ch + 1)
                gk = ch * NWW
                if gk >= 1:
                    t.wait_ge(wv, gk)
                for hl in range(8):
                    j, hp = hl // 2, hl % 2
                    cb = 64 * hp
                    t.matmul(PS[pb:pb + 64, 512 * hl:512 * hl + 64],
                             kt_t[cb:cb + 64, pr, 64 * j:64 * (j + 1)],
                             rt_t[cb:cb + 64, pr, 64 * j:64 * (j + 1)],
                             start=True, stop=True)
                for j in range(4):
                    ins = t.transpose(PSB[pb:pb + 64, 1024 * j + 512:1024 * j + 640],
                                      kh_t[:, pr, 64 * j:64 * (j + 1)], idt[:, :])
                ins.then_inc(wt, 1)
                gk += 1
                t.wait_ge(wv, gk)
                for hl in range(8):
                    j, hp = hl // 2, hl % 2
                    cb = 64 * hp
                    ins = t.matmul(PS[pb:pb + 64, 512 * hl:512 * hl + 64],
                                   uk_t[cb:cb + 64, pr, 64 * j:64 * (j + 1)],
                                   opchunk(rT, ch)[cb:cb + 64, j, :],
                                   start=True, stop=True)
                ins.then_inc(wt, 1)
                gk += 1
                t.wait_ge(wv, gk)
                for hl in range(8):
                    vblk = v[pb:pb + 64, tt * 1024 + 512 * grp + 64 * hl:
                             tt * 1024 + 512 * grp + 64 * (hl + 1)]
                    ins = t.matmul(PS[pb:pb + 64, 512 * hl:512 * hl + 64],
                                   ahat[pb:pb + 64, 64 * hl:64 * (hl + 1)], vblk,
                                   start=True, stop=True)
                ins.then_inc(wt, 1)
                gk += 1
                t.wait_ge(wv, gk)
                for hl in range(8):
                    j, hp = hl // 2, hl % 2
                    cb = 64 * hp
                    ins = t.matmul(PS[pb:pb + 64, 512 * hl:512 * hl + 64],
                                   rt_t[cb:cb + 64, pr, 64 * j:64 * (j + 1)],
                                   sbf[cb:cb + 64, 64 * j:64 * (j + 1)],
                                   start=True, stop=True)
                ins.then_inc(wt, 1)
                gk += 1
                t.wait_ge(wv, gk)
                for hl in range(8):
                    j, hp = hl // 2, hl % 2
                    vblk = v[pb:pb + 64, tt * 1024 + 512 * grp + 64 * hl:
                             tt * 1024 + 512 * grp + 64 * (hl + 1)]
                    ins = t.matmul(PS[64 * hp:64 * hp + 64, 512 * hl:512 * hl + 64],
                                   khtok[pb:pb + 64, 128 * j + 64 * hp:128 * j + 64 * (hp + 1)],
                                   vblk, start=True, stop=True)
                ins.then_inc(wt, 1)
                gk += 1
            t.wait_ge(wv, WVSIN)
            for n, (g2, tb, hl) in enumerate(corr_order):
                j, hp = hl // 2, hl % 2
                cb = 64 * hp
                if n >= 8:
                    t.wait_ge(cv, n - 7)
                t.matmul(PS[:, 512 * hl:512 * hl + 64],
                         Rhat[cb:cb + 64, 4 * g2 + j, 128 * tb:128 * (tb + 1)],
                         Sinb[cb:cb + 64, 4 * g2 + j, :],
                         start=True, stop=True).then_inc(ct, 1)


def build_l0back():
    nc = bass.Bass("TRN2", target_bir_lowering=False)
    io = dict(
        y_in=nc.dram_tensor("y_in", [T, C], BF16, kind="ExternalInput"),
        g_in=nc.dram_tensor("g_in", [T, C], BF16, kind="ExternalInput"),
        x_in=nc.dram_tensor("x_in", [T, C], F32, kind="ExternalInput"),
        lngr=nc.dram_tensor("lngr", [128, C], F32, kind="ExternalInput"),
        lnbr=nc.dram_tensor("lnbr", [128, C], F32, kind="ExternalInput"),
        Wo=nc.dram_tensor("Wo", [C, C], F32, kind="ExternalInput"),
        ident=nc.dram_tensor("ident", [128, 128], BF16, kind="ExternalInput"),
        x1_o=nc.dram_tensor("x1_o", [T, C], F32, kind="ExternalOutput"),
    )
    with ExitStack() as ES:
        stgs = ES.enter_context(nc.semaphore("stgs"))
        build_l0back_body(nc, ES, io, "", stgs, 0)
    return nc


def build_l0back_body(nc, ES, io, px, stg, kst):
    y_d, g_d, x_d = io["y_in"], io["g_in"], io["x_in"]
    lng_d, lnb_d, Wo_d, id_d = io["lngr"], io["lnbr"], io["Wo"], io["ident"]
    x1_o = io["x1_o"]

    def S(n):
        return ES.enter_context(nc.semaphore(px + n))

    with ExitStack() as ctx:
        E = ctx.enter_context
        ys = E(nc.sbuf_tensor(px + "ys", [128, 16, C], BF16))
        gs = E(nc.sbuf_tensor(px + "gs", [128, 16, C], BF16))
        xs = E(nc.sbuf_tensor(px + "xs", [128, 16, C], F32))
        yng = E(nc.sbuf_tensor(px + "yng", [128, 2, C], BF16))
        yngT = E(nc.sbuf_tensor(px + "yngT", [128, 8, T], BF16))
        wos = E(nc.sbuf_tensor(px + "wos", [128, 8, C], BF16))
        lngr = E(nc.sbuf_tensor(px + "lngr_s", [128, C], F32))
        lnbr = E(nc.sbuf_tensor(px + "lnbr_s", [128, C], F32))
        idt = E(nc.sbuf_tensor(px + "idt", [128, 128], BF16))
        scrA = E(nc.sbuf_tensor(px + "scrA", [128, C], F32))
        scrB = E(nc.sbuf_tensor(px + "scrB", [128, C], F32))
        lngg = E(nc.sbuf_tensor(px + "lngg", [128, C], BF16))
        lnbg = E(nc.sbuf_tensor(px + "lnbg", [128, C], BF16))
        gsum = E(nc.sbuf_tensor(px + "gsum", [128, 16], F32))
        gssq = E(nc.sbuf_tensor(px + "gssq", [128, 16], F32))
        gmean = E(nc.sbuf_tensor(px + "gmean", [128, 16], F32))
        gvar = E(nc.sbuf_tensor(px + "gvar", [128, 16], F32))
        grstd = E(nc.sbuf_tensor(px + "grstd", [128, 16], F32))
        sums = E(nc.sbuf_tensor(px + "sums", [128, 16], F32))
        ssq = E(nc.sbuf_tensor(px + "ssq", [128, 16], F32))
        mean2 = E(nc.sbuf_tensor(px + "mean2", [128, 16], F32))
        var2 = E(nc.sbuf_tensor(px + "var2", [128, 16], F32))
        rstd2 = E(nc.sbuf_tensor(px + "rstd2", [128, 16], F32))
        PS = E(nc.psum_tensor(px + "PS", [128, 4096], F32))
        ldg = S("ldg")
        ld = S("ld")
        vv = S("vv")
        ss = S("ss")
        wt = S("wt")
        wvv = S("wvv")
        wvs = S("wvs")
        sto = S("sto")
        block = E(nc.Block())
        PSB = PS.ap().bitcast(BF16)
        nvst = [0]
        nsst = [0]
        W = Waves()

        def vedge(v):
            nvst[0] += 1
            v.sem_inc(vv, 0)
        for tt in range(16):
            def cf_stats(v, tt=tt):
                v.tensor_tensor(scrA[:, :], ys[:, tt, :], ys[:, tt, :],
                                ALU.mult).then_inc(vv, 1)
                nvst[0] += 1
                v.wait_ge(vv, nvst[0])
                v.reduce_sum(gsum[:, :], ys[:, tt, :].rearrange("p (h q) -> p h q", h=16),
                             mybir.AxisListType.X)
                v.reduce_sum(gssq[:, :], scrA.ap().rearrange("p (h q) -> p h q", h=16),
                             mybir.AxisListType.X).then_inc(vv, 1)
                nvst[0] += 1
                v.wait_ge(vv, nvst[0])
                v.tensor_scalar(gmean[:, :], gsum[:, :], 1.0 / 64, None, ALU.mult)
                v.tensor_scalar(gvar[:, :], gssq[:, :], 1.0 / 64, None,
                                ALU.mult).then_inc(vv, 1)
                nvst[0] += 1
                v.wait_ge(vv, nvst[0])
                v.tensor_tensor(scrB[:, 0:16], gmean[:, :], gmean[:, :],
                                ALU.mult).then_inc(vv, 1)
                nvst[0] += 1
                v.wait_ge(vv, nvst[0])
                v.tensor_tensor(gvar[:, :], gvar[:, :], scrB[:, 0:16],
                                ALU.subtract).then_inc(vv, 1)
                nvst[0] += 1
                v.wait_ge(vv, nvst[0])
                return v.tensor_scalar(gvar[:, :], gvar[:, :], 0.00064, None, ALU.add)
            W.wave('v', None, cf_stats)

            def cf_rstd(s, tt=tt):
                s.activation(grstd[:, :], gvar[:, :], AF.Ln).then_inc(ss, 1)
                nsst[0] += 1
                s.wait_ge(ss, nsst[0])
                return s.activation(grstd[:, :], grstd[:, :], AF.Exp, scale=-0.5)
            W.wave('s', None, cf_rstd)

            def cf_yng(v, tt=tt):
                v.tensor_tensor(lngg[:, :], lngr[:, :], gs[:, tt, :], ALU.mult)
                v.tensor_tensor(lnbg[:, :], lnbr[:, :], gs[:, tt, :], ALU.mult)
                v.tensor_tensor(scrA.ap().rearrange("p (h q) -> p h q", h=16),
                                ys[:, tt, :].rearrange("p (h q) -> p h q", h=16),
                                gmean.ap().rearrange("p (h one) -> p h one", one=1)
                                .broadcast_to([128, 16, 64]), ALU.subtract).then_inc(vv, 1)
                nvst[0] += 1
                v.wait_ge(vv, nvst[0])
                v.tensor_tensor(scrA.ap().rearrange("p (h q) -> p h q", h=16),
                                scrA.ap().rearrange("p (h q) -> p h q", h=16),
                                grstd.ap().rearrange("p (h one) -> p h one", one=1)
                                .broadcast_to([128, 16, 64]), ALU.mult).then_inc(vv, 1)
                nvst[0] += 1
                v.wait_ge(vv, nvst[0])
                v.tensor_tensor(scrA[:, :], scrA[:, :], lngg[:, :], ALU.mult).then_inc(vv, 1)
                nvst[0] += 1
                v.wait_ge(vv, nvst[0])
                return v.tensor_tensor(yng[:, tt % 2, :], scrA[:, :], lnbg[:, :], ALU.add)
            W.wave('v', None, cf_yng)

            def tf_tr(t, tt=tt):
                for n in range(8):
                    ins = t.transpose(PSB[:, 7 * 1024 + 128 * n:7 * 1024 + 128 * (n + 1)],
                                      yng[:, tt % 2, 128 * n:128 * (n + 1)], idt[:, :])
                return ins

            def cf_tr(v, tt=tt):
                return v.tensor_copy(yngT[:, :, 128 * tt:128 * (tt + 1)],
                                     PSB[:, 7 * 1024:8 * 1024]
                                     .rearrange("p (n q) -> p n q", n=8))
            W.wave('v', tf_tr, cf_tr)

        for tt in range(16):
            pp = tt % 2

            def tf_mm(t, tt=tt, pp=pp):
                if tt == 0:
                    t.wait_ge(ldg, 16 * NLOADS_B[0])
                for cc in range(8):
                    for half in range(2):
                        ins = t.matmul(
                            PS[:, 2048 * pp + 512 * half:2048 * pp + 512 * (half + 1)],
                            yngT[:, cc, 128 * tt:128 * (tt + 1)],
                            wos[:, cc, 512 * half:512 * (half + 1)],
                            start=(cc == 0), stop=(cc == 7))
                return ins

            def cf_mm(v, tt=tt, pp=pp):
                v.tensor_tensor(xs[:, tt, :], PS[:, 2048 * pp:2048 * pp + 1024],
                                xs[:, tt, :], ALU.add).then_inc(vv, 1)
                nvst[0] += 1
                v.wait_ge(vv, nvst[0])
                v.tensor_tensor(scrA[:, :], xs[:, tt, :], xs[:, tt, :],
                                ALU.mult).then_inc(vv, 1)
                nvst[0] += 1
                v.wait_ge(vv, nvst[0])
                v.reduce_sum(sums[:, tt:tt + 1],
                             xs[:, tt, :].rearrange("p (one q) -> p one q", one=1),
                             mybir.AxisListType.X)
                return v.reduce_sum(ssq[:, tt:tt + 1],
                                    scrA.ap().rearrange("p (one q) -> p one q", one=1),
                                    mybir.AxisListType.X)
            W.wave('v', tf_mm, cf_mm)

        def cf_c(v):
            v.tensor_scalar(mean2[:, :], sums[:, :], 1.0 / C, None, ALU.mult)
            v.tensor_scalar(var2[:, :], ssq[:, :], 1.0 / C, None, ALU.mult).then_inc(vv, 1)
            nvst[0] += 1
            v.wait_ge(vv, nvst[0])
            v.tensor_tensor(scrB[:, 0:16], mean2[:, :], mean2[:, :], ALU.mult).then_inc(vv, 1)
            nvst[0] += 1
            v.wait_ge(vv, nvst[0])
            v.tensor_tensor(var2[:, :], var2[:, :], scrB[:, 0:16],
                            ALU.subtract).then_inc(vv, 1)
            nvst[0] += 1
            v.wait_ge(vv, nvst[0])
            return v.tensor_scalar(var2[:, :], var2[:, :], 1e-5, None, ALU.add)
        W.wave('v', None, cf_c)

        def cf_d(s):
            s.activation(rstd2[:, :], var2[:, :], AF.Ln).then_inc(ss, 1)
            nsst[0] += 1
            s.wait_ge(ss, nsst[0])
            return s.activation(rstd2[:, :], rstd2[:, :], AF.Exp, scale=-0.5)
        W.wave('s', None, cf_d)

        def cf_e(v):
            for tt in range(16):
                ins = v.tensor_scalar(xs[:, tt, :], xs[:, tt, :], mean2[:, tt:tt + 1],
                                      rstd2[:, tt:tt + 1], ALU.subtract, ALU.mult)
            return ins
        W.wave('v', None, cf_e)

        NW = len(W.items)
        NLOADS_B = [5]

        @block.gpsimd
        def _(g):
            g.wait_ge(stg, kst)
            g.dma_start(out=ys.ap(), in_=y_d.ap().rearrange("(tt p) c -> p tt c", p=128)
                        ).then_inc(ldg, 16)
            g.dma_start(out=gs.ap(), in_=g_d.ap().rearrange("(tt p) c -> p tt c", p=128)
                        ).then_inc(ldg, 16)
            g.dma_start(out=wos.ap(), in_=Wo_d.ap().rearrange("(cc p) c -> p cc c", p=128)
                        ).then_inc(ldg, 16)
            g.dma_start(out=idt.ap(), in_=id_d.ap()).then_inc(ldg, 16)
            g.dma_start(out=xs.ap(), in_=x_d.ap().rearrange("(tt p) c -> p tt c", p=128)
                        ).then_inc(ldg, 16)

        @block.sync
        def _(s):
            s.wait_ge(stg, kst)
            s.dma_start(out=lngr.ap(), in_=lng_d.ap()).then_inc(ld, 16)
            s.dma_start(out=lnbr.ap(), in_=lnb_d.ap()).then_inc(ld, 16)
            nvec = sum(1 for it in W.items if it[1] == 'v')
            s.wait_ge(wvv, nvec)
            s.dma_start(out=x1_o.ap().rearrange("(tt p) c -> p tt c", p=128),
                        in_=xs.ap()).then_inc(sto, 16)
            s.wait_ge(sto, 16)
            s.sem_inc(stg, 1)

        @block.vector
        def _(v):
            v.wait_ge(stg, kst)
            v.wait_ge(ldg, 16 * 5)
            v.wait_ge(ld, 32)
            for k, (tfn, ceng, cfn, lag) in enumerate(W.items):
                if ceng != 'v':
                    continue
                v.wait_ge(wt, k + 1)
                cfn(v).then_inc(wvv, 1)

        @block.scalar
        def _(s):
            s.wait_ge(stg, kst)
            s.wait_ge(ldg, 16 * 5)
            for k, (tfn, ceng, cfn, lag) in enumerate(W.items):
                if ceng != 's':
                    continue
                s.wait_ge(wt, k + 1)
                cfn(s).then_inc(wvs, 1)

        @block.tensor
        def _(t):
            t.wait_ge(stg, kst)
            engs = [it[1] for it in W.items]
            for k, (tfn, ceng, cfn, lag) in enumerate(W.items):
                lim = max(0, k - lag)
                nvec = sum(1 for e in engs[:lim] if e == 'v')
                nsca = lim - nvec
                if nvec:
                    t.wait_ge(wvv, nvec)
                if nsca:
                    t.wait_ge(wvs, nsca)
                if tfn is not None:
                    tfn(t).then_inc(wt, 1)
                else:
                    t.sem_inc(wt, 1)


def build_l0ffn():
    nc = bass.Bass("TRN2", target_bir_lowering=False)
    io = dict(
        x1=nc.dram_tensor("x1", [T, C], F32, kind="ExternalInput"),
        W1a=nc.dram_tensor("W1a", [10, C, 4096], F32, kind="ExternalInput"),
        W2a=nc.dram_tensor("W2a", [10, 4096, C], F32, kind="ExternalInput"),
        b1a=nc.dram_tensor("b1a", [128, 10, 32], F32, kind="ExternalInput"),
        b2a=nc.dram_tensor("b2a", [10, C], F32, kind="ExternalInput"),
        cent=nc.dram_tensor("cent", [C, 8], F32, kind="ExternalInput"),
        gsc=nc.dram_tensor("gsc", [128, 8], F32, kind="ExternalInput"),
        grc=nc.dram_tensor("grc", [128, 8], F32, kind="ExternalInput"),
        ident=nc.dram_tensor("ident", [128, 128], BF16, kind="ExternalInput"),
        x2_o=nc.dram_tensor("x2_o", [T, C], F32, kind="ExternalOutput"),
    )
    with ExitStack() as ES:
        stgs = ES.enter_context(nc.semaphore("stgs"))
        build_l0ffn_body(nc, ES, io, "", stgs, 0)
    return nc


def build_l0ffn_body(nc, ES, io, px, stg, kst):
    x1_d, W1_d, W2_d, b1_d, b2_d = (io["x1"], io["W1a"], io["W2a"],
                                    io["b1a"], io["b2a"])
    cent_d, gs_d, gr_d, id_d = io["cent"], io["gsc"], io["grc"], io["ident"]
    x2_o = io["x2_o"]

    def S(n):
        return ES.enter_context(nc.semaphore(px + n))

    with ExitStack() as ctx:
        E = ctx.enter_context
        accum = E(nc.sbuf_tensor(px + "accum", [128, 16, C], F32))
        xnT = E(nc.sbuf_tensor(px + "xnT", [128, 8, T], BF16))
        hTq = E(nc.sbuf_tensor(px + "hTq", [128, 8, T], BF16))
        w1q = E(nc.sbuf_tensor(px + "w1q", [128, 2, 8192], BF16))
        w2q = E(nc.sbuf_tensor(px + "w2q", [128, 2, 8192], BF16))
        b1t = E(nc.sbuf_tensor(px + "b1t", [128, 10, 32], F32))
        b2row = E(nc.sbuf_tensor(px + "b2row", [1, 1, C], BF16))
        onesr = E(nc.sbuf_tensor(px + "onesr", [1, 128], BF16))
        cents = E(nc.sbuf_tensor(px + "cents", [128, 8, 8], BF16))
        gsr = E(nc.sbuf_tensor(px + "gsc_s", [128, 8], F32))
        grr = E(nc.sbuf_tensor(px + "grc_s", [128, 8], F32))
        identf = E(nc.sbuf_tensor(px + "identf", [128, 128], F32))
        scrA = E(nc.sbuf_tensor(px + "scrA", [128, C], F32))
        scb = E(nc.sbuf_tensor(px + "scb", [8, 512], F32))
        sco = E(nc.sbuf_tensor(px + "sco", [128, 16, 8], F32))
        gates = E(nc.sbuf_tensor(px + "gates", [128, 16, 10], F32))
        m1t = E(nc.sbuf_tensor(px + "m1t", [128, 16], F32))
        m2t = E(nc.sbuf_tensor(px + "m2t", [128, 16], F32))
        rsum = E(nc.sbuf_tensor(px + "rsum", [128, 16], F32))
        selt = E(nc.sbuf_tensor(px + "selt", [128, 16, 8], F32))
        ssq = E(nc.sbuf_tensor(px + "ssqf", [128, 16], F32))
        rs = E(nc.sbuf_tensor(px + "rsf", [128, 16], F32))
        rs2 = E(nc.sbuf_tensor(px + "rs2f", [128, 16], F32))
        sums = E(nc.sbuf_tensor(px + "sums", [128, 16], F32))
        mean2 = E(nc.sbuf_tensor(px + "mean2", [128, 16], F32))
        var2 = E(nc.sbuf_tensor(px + "var2", [128, 16], F32))
        rstd2 = E(nc.sbuf_tensor(px + "rstd2", [128, 16], F32))
        PS = E(nc.psum_tensor(px + "PS", [128, 4096], F32))
        ldg = S("ldg")
        ww0 = S("ww0")
        ww1 = S("ww1")
        wwp = [ww0, ww1]
        w20 = S("w20")
        w21 = S("w21")
        w2p = [w20, w21]
        bb0 = S("bb0")
        bb1 = S("bb1")
        bbp = [bb0, bb1]
        ld = S("ld")
        vv = S("vv")
        ss = S("ss")
        wt = S("wt")
        wvv = S("wvv")
        wvs = S("wvs")
        sto = S("sto")
        block = E(nc.Block())
        PSB = PS.ap().bitcast(BF16)
        nvst = [0]
        nsst = [0]
        W = Waves()
        EPS_RMS = 1.1920929e-07
        QEND = {}
        BUSE = [0, 0]

        def vedge(v):
            nvst[0] += 1
            v.wait_ge(vv, nvst[0])

        for tt in range(16):
            def cf(v, tt=tt):
                v.tensor_tensor(scrA[:, :], accum[:, tt, :], accum[:, tt, :],
                                ALU.mult).then_inc(vv, 1)
                vedge(v)
                return v.reduce_sum(ssq[:, tt:tt + 1],
                                    scrA.ap().rearrange("p (one q) -> p one q", one=1),
                                    mybir.AxisListType.X)
            W.wave('v', None, cf)

        def cf1(v):
            return v.tensor_scalar(rs[:, :], ssq[:, :], 1.0 / C, EPS_RMS, ALU.mult, ALU.add)
        W.wave('v', None, cf1)

        def cf2(s):
            s.activation(rs[:, :], rs[:, :], AF.Ln).then_inc(ss, 1)
            nsst[0] += 1
            s.wait_ge(ss, nsst[0])
            return s.activation(rs[:, :], rs[:, :], AF.Exp, scale=-0.5)
        W.wave('s', None, cf2)

        def cf3(v):
            return v.tensor_scalar(rs2[:, :], rs[:, :], 0.5, None, ALU.mult)
        W.wave('v', None, cf3)

        for tt in range(16):
            def tf(t, tt=tt):
                if tt == 0:
                    t.wait_ge(ldg, 16 * 5)
                    t.wait_ge(ld, 16)
                for n in range(8):
                    ins = t.transpose(PS[:, 3072 + 128 * n:3072 + 128 * (n + 1)],
                                      accum[:, tt, 128 * n:128 * (n + 1)], identf[:, :])
                return ins

            def cf(v, tt=tt):
                return v.tensor_copy(hTq[:, :, 128 * tt:128 * (tt + 1)],
                                     PS[:, 3072:4096].rearrange("p (n q) -> p n q", n=8))
            W.wave('v', tf, cf)

        for tc in range(4):
            def tf(t, tc=tc):
                for cc in range(8):
                    ins = t.matmul(PS[0:8, 2560:2560 + 512], cents[:, cc, :],
                                   hTq[:, cc, 512 * tc:512 * (tc + 1)],
                                   start=(cc == 0), stop=(cc == 7))
                return ins

            def cf(s, tc=tc):
                return s.activation(scb[0:8, :], PS[0:8, 2560:2560 + 512], AF.Sigmoid)
            W.wave('s', tf, cf)

            def tf2(t, tc=tc):
                for n in range(4):
                    ins = t.transpose(PS[:, 2048 + 8 * n:2048 + 8 * (n + 1)],
                                      scb[0:8, 128 * n:128 * (n + 1)],
                                      identf[0:8, 0:8])
                return ins

            def cf2(v, tc=tc):
                return v.tensor_copy(sco[:, 4 * tc:4 * (tc + 1), :],
                                     PS[:, 2048:2048 + 32].rearrange("p (n q) -> p n q", n=4))
            W.wave('v', tf2, cf2)

        def cf4(v):
            v.memset(gates[:, :, 0:2], 1.0)
            v.reduce_max(m1t[:, :], sco.ap(), mybir.AxisListType.X)
            v.reduce_sum(rsum[:, :], sco.ap(), mybir.AxisListType.X).then_inc(vv, 1)
            vedge(v)
            v.tensor_tensor(selt.ap(), sco.ap(),
                            m1t.ap().rearrange("p (h one) -> p h one", one=1)
                            .broadcast_to([128, 16, 8]), ALU.is_ge).then_inc(vv, 1)
            vedge(v)
            v.scalar_tensor_tensor(selt.ap(), selt.ap(), -1e30, sco.ap(),
                                   ALU.mult, ALU.add).then_inc(vv, 1)
            vedge(v)
            v.reduce_max(m2t[:, :], selt.ap(), mybir.AxisListType.X).then_inc(vv, 1)
            vedge(v)
            v.tensor_tensor(selt.ap(), sco.ap(),
                            m2t.ap().rearrange("p (h one) -> p h one", one=1)
                            .broadcast_to([128, 16, 8]), ALU.is_ge)
            v.reciprocal(rsum[:, :], rsum[:, :]).then_inc(vv, 1)
            vedge(v)
            v.tensor_tensor(selt.ap(), selt.ap(), sco.ap(), ALU.mult).then_inc(vv, 1)
            vedge(v)
            v.tensor_tensor(gates[:, :, 2:10], selt.ap(),
                            rsum.ap().rearrange("p (h one) -> p h one", one=1)
                            .broadcast_to([128, 16, 8]), ALU.mult).then_inc(vv, 1)
            vedge(v)
            return v.tensor_scalar(accum.ap(), accum.ap(), 2.0, None, ALU.mult)
        W.wave('v', None, cf4)

        GQ = [0]

        def xnT_waves(rs_t):
            for tt in range(16):
                def cfa(v, tt=tt, rs_t=rs_t):
                    return v.tensor_scalar(scrA[:, :], accum[:, tt, :], rs_t[:, tt:tt + 1],
                                           None, ALU.mult)
                W.wave('v', None, cfa)

                def tfb(t, tt=tt):
                    for n in range(8):
                        ins = t.transpose(PS[:, 3072 + 128 * n:3072 + 128 * (n + 1)],
                                          scrA[:, 128 * n:128 * (n + 1)], identf[:, :])
                    return ins

                def cfb(v, tt=tt):
                    return v.tensor_copy(xnT[:, :, 128 * tt:128 * (tt + 1)],
                                         PS[:, 3072:4096].rearrange("p (n q) -> p n q", n=8))
                W.wave('v', tfb, cfb)

        def expert_waves(e):
            gain = gsr if e < 2 else grr
            for q in range(4):
                gq = GQ[0]
                GQ[0] += 1
                sl = gq % 2

                def cfw(v, gq=gq, e=e, gain=gain, sl=sl):
                    v.wait_ge(wwp[sl], 16 * (gq // 2 + 1))
                    if gq % 4 == 0:
                        v.wait_ge(bbp[0], 16 * (e + 1))
                    return v.tensor_tensor(
                        w1q[:, sl, :].rearrange("p (cg h) -> p cg h", cg=8),
                        w1q[:, sl, :].rearrange("p (cg h) -> p cg h", cg=8),
                        gain.ap().rearrange("p (cg one) -> p cg one", one=1)
                        .broadcast_to([128, 8, 1024]), ALU.mult)
                W.wave('v', None, cfw)
                for ht in range(8):
                    for th in range(2):
                        gu = gq * 16 + ht * 2 + th
                        pt = gu % 4
                        unit_i = ht * 2 + th

                        def tf(t, sl=sl, ht=ht, th=th, pt=pt, gq=gq, e=e):
                            for cg in range(8):
                                for half in range(2):
                                    ins = t.matmul(
                                        PS[:, 1024 * pt + 512 * half:1024 * pt + 512 * (half + 1)],
                                        w1q[:, sl, 1024 * cg + 128 * ht:1024 * cg + 128 * (ht + 1)],
                                        xnT[:, cg, 1024 * th + 512 * half:1024 * th + 512 * (half + 1)],
                                        start=(cg == 0), stop=(cg == 7))
                            return ins

                        def cf(s, e=e, q=q, ht=ht, th=th, pt=pt):
                            return s.activation(hTq[:, ht, 1024 * th:1024 * (th + 1)],
                                                PS[:, 1024 * pt:1024 * (pt + 1)],
                                                AF.Gelu, bias=b1t[:, e, 8 * q + ht:8 * q + ht + 1])
                        W.wave('s', tf, cf, lag=(unit_i if unit_i < 3 else 3))
                for tt in range(16):
                    gt = gq * 16 + tt
                    p2 = gt % 2

                    def tf(t, sl=sl, tt=tt, p2=p2, e=e, q=q, gq=gq):
                        if tt == 0:
                            t.wait_ge(w2p[sl], 16 * (gq // 2 + 1))
                        for hc in range(8):
                            for half in range(2):
                                ins = t.matmul(
                                    PS[:, 1024 * p2 + 512 * half:1024 * p2 + 512 * (half + 1)],
                                    hTq[:, hc, 128 * tt:128 * (tt + 1)],
                                    w2q[:, sl, 1024 * hc + 512 * half:1024 * hc + 512 * (half + 1)],
                                    start=(hc == 0), stop=(q != 0 and hc == 7))
                        if q == 0:
                            for half in range(2):
                                ins = t.matmul(
                                    PS[:, 1024 * p2 + 512 * half:1024 * p2 + 512 * (half + 1)],
                                    onesr[0:1, 0:128],
                                    b2row[0:1, 0, 512 * half:512 * (half + 1)],
                                    start=False, stop=True)
                        return ins

                    def cf(v, tt=tt, p2=p2, e=e):
                        return v.scalar_tensor_tensor(
                            accum[:, tt, :], PS[:, 1024 * p2:1024 * (p2 + 1)],
                            gates[:, tt, e:e + 1], accum[:, tt, :], ALU.mult, ALU.add)
                    W.wave('v', tf, cf, lag=(0 if tt == 0 else 1))
                QEND[gq] = len(W.items)

        import os as _os
        _nexp = int(_os.environ.get("L0FFN_NEXP", "10"))
        xnT_waves(rs2)
        for e in range(_nexp):
            expert_waves(e)

        _noln = _os.environ.get("L0FFN_NOLN") == "1"
        for tt in range(16):
            def cf(v, tt=tt):
                v.tensor_tensor(scrA[:, :], accum[:, tt, :], accum[:, tt, :],
                                ALU.mult).then_inc(vv, 1)
                vedge(v)
                v.reduce_sum(sums[:, tt:tt + 1],
                             accum[:, tt, :].rearrange("p (one q) -> p one q", one=1),
                             mybir.AxisListType.X)
                return v.reduce_sum(ssq[:, tt:tt + 1],
                                    scrA.ap().rearrange("p (one q) -> p one q", one=1),
                                    mybir.AxisListType.X)
            W.wave('v', None, cf)

        def cfc(v):
            v.tensor_scalar(mean2[:, :], sums[:, :], 1.0 / C, None, ALU.mult)
            v.tensor_scalar(var2[:, :], ssq[:, :], 1.0 / C, None, ALU.mult).then_inc(vv, 1)
            vedge(v)
            v.tensor_tensor(scrA[:, 0:16], mean2[:, :], mean2[:, :], ALU.mult).then_inc(vv, 1)
            vedge(v)
            v.tensor_tensor(var2[:, :], var2[:, :], scrA[:, 0:16], ALU.subtract).then_inc(vv, 1)
            vedge(v)
            return v.tensor_scalar(var2[:, :], var2[:, :], 1e-5, None, ALU.add)
        W.wave('v', None, cfc)

        def cfd(s):
            s.activation(rstd2[:, :], var2[:, :], AF.Ln).then_inc(ss, 1)
            nsst[0] += 1
            s.wait_ge(ss, nsst[0])
            return s.activation(rstd2[:, :], rstd2[:, :], AF.Exp, scale=-0.5)
        W.wave('s', None, cfd)

        def cfe(v):
            for tt in range(16):
                if _noln:
                    ins = v.memset(mean2[:, tt:tt + 1], 0.0)
                else:
                    ins = v.tensor_scalar(accum[:, tt, :], accum[:, tt, :], mean2[:, tt:tt + 1],
                                          rstd2[:, tt:tt + 1], ALU.subtract, ALU.mult)
            return ins
        W.wave('v', None, cfe)

        @block.gpsimd
        def _(g):
            g.wait_ge(stg, kst)
            g.dma_start(out=accum.ap(), in_=x1_d.ap().rearrange("(tt p) c -> p tt c", p=128)
                        ).then_inc(ldg, 16)
            g.dma_start(out=cents.ap(), in_=cent_d.ap().rearrange("(cc p) e -> p cc e", p=128)
                        ).then_inc(ldg, 16)
            g.dma_start(out=identf.ap(), in_=id_d.ap()).then_inc(ldg, 16)
            g.dma_start(out=gsr.ap(), in_=gs_d.ap()).then_inc(ldg, 16)
            g.dma_start(out=grr.ap(), in_=gr_d.ap()).then_inc(ldg, 16)
            import os as _os
            for e in range(int(_os.environ.get("L0FFN_NEXP", "10"))):
                for q in range(4):
                    gq = 4 * e + q
                    sl = gq % 2
                    if gq >= 2:
                        g.wait_ge(wt, QEND[gq - 2])
                    g.dma_start(out=w1q[:, sl, :].rearrange("p (cg h) -> p cg h", cg=8),
                                in_=W1_d.ap()[e].rearrange("(cg p) h -> p cg h", p=128)
                                [:, :, 1024 * q:1024 * (q + 1)]).then_inc(wwp[sl], 16)
                    g.dma_start(out=w2q[:, sl, :].rearrange("p (hc c) -> p hc c", hc=8),
                                in_=W2_d.ap()[e].rearrange("(q hc p) c -> q p hc c", p=128, hc=8)[q]
                                ).then_inc(w2p[sl], 16)
                    if q == 0:
                        g.dma_start(out=b2row[0:1, 0, :],
                                    in_=b2_d.ap()[e:e + 1, :]).then_inc(bbp[0], 16)

        @block.sync
        def _(s):
            s.wait_ge(stg, kst)
            s.dma_start(out=b1t.ap(), in_=b1_d.ap()).then_inc(ld, 16)
            nvec = sum(1 for it in W.items if it[1] == 'v')
            s.wait_ge(wvv, nvec)
            s.dma_start(out=x2_o.ap().rearrange("(tt p) c -> p tt c", p=128),
                        in_=accum.ap()).then_inc(sto, 16)
            s.wait_ge(sto, 16)
            s.sem_inc(stg, 1)

        @block.vector
        def _(v):
            v.wait_ge(stg, kst)
            v.wait_ge(ldg, 16 * 5)
            v.wait_ge(ld, 16)
            v.memset(onesr[0:1, :], 1.0)
            for k, (tfn, ceng, cfn, lag) in enumerate(W.items):
                if ceng != 'v':
                    continue
                v.wait_ge(wt, k + 1)
                cfn(v).then_inc(wvv, 1)

        @block.scalar
        def _(s):
            s.wait_ge(stg, kst)
            s.wait_ge(ldg, 16 * 5)
            s.wait_ge(ld, 16)
            for k, (tfn, ceng, cfn, lag) in enumerate(W.items):
                if ceng != 's':
                    continue
                s.wait_ge(wt, k + 1)
                cfn(s).then_inc(wvs, 1)

        @block.tensor
        def _(t):
            t.wait_ge(stg, kst)
            engs = [it[1] for it in W.items]
            for k, (tfn, ceng, cfn, lag) in enumerate(W.items):
                lim = max(0, k - lag)
                nvec = sum(1 for e in engs[:lim] if e == 'v')
                nsca = lim - nvec
                if nvec:
                    t.wait_ge(wvv, nvec)
                if nsca:
                    t.wait_ge(wvs, nsca)
                if tfn is not None:
                    tfn(t).then_inc(wt, 1)
                else:
                    t.sem_inc(wt, 1)


def build_l1attn():
    nc = bass.Bass("TRN2", target_bir_lowering=False)
    io = dict(
        x_in=nc.dram_tensor("x_in", [T, C], F32, kind="ExternalInput"),
        Wq=nc.dram_tensor("Wq", [C, C], F32, kind="ExternalInput"),
        Wk=nc.dram_tensor("Wk", [C, C], F32, kind="ExternalInput"),
        Wv=nc.dram_tensor("Wv", [C, C], F32, kind="ExternalInput"),
        Wo=nc.dram_tensor("Wo", [C, C], F32, kind="ExternalInput"),
        betas=nc.dram_tensor("betas", [64, 8], F32, kind="ExternalInput"),
        mem0z=nc.dram_tensor("mem0z", [64, 8 * 65], F32, kind="ExternalInput"),
        maskT=nc.dram_tensor("maskT", [128, 4 * 512], F32, kind="ExternalInput"),
        ident=nc.dram_tensor("ident", [128, 128], BF16, kind="ExternalInput"),
        x3_o=nc.dram_tensor("x3_o", [T, C], F32, kind="ExternalOutput"),
        qbuf=nc.dram_tensor("qbuf", [512 * 8, 64], BF16, kind="Internal"),
        kbuf=nc.dram_tensor("kbuf", [512 * 8, 64], BF16, kind="Internal"),
        vbuf=nc.dram_tensor("vbuf", [512 * 8, 64], BF16, kind="Internal"),
        abuf=nc.dram_tensor("abuf", [512 * 8, 64], BF16, kind="Internal"),
    )
    with ExitStack() as ES:
        stgs = ES.enter_context(nc.semaphore("stgs"))
        build_l1attn_body(nc, ES, io, "", stgs, 0)
    return nc


def build_l1attn_body(nc, ES, io, px, stg, kst):
    x_d, Wq_d, Wk_d, Wv_d, Wo_d = (io["x_in"], io["Wq"], io["Wk"],
                                   io["Wv"], io["Wo"])
    bet_d, m0_d, mt_d, id_d = io["betas"], io["mem0z"], io["maskT"], io["ident"]
    x3_o = io["x3_o"]
    qb_d, kb_d, vb_d, ab_d = io["qbuf"], io["kbuf"], io["vbuf"], io["abuf"]

    def S(n):
        return ES.enter_context(nc.semaphore(px + n))

    with ExitStack() as ctx:
        E = ctx.enter_context
        xseg = E(nc.sbuf_tensor(px + "xseg", [128, 2, C], F32))
        xTs = E(nc.sbuf_tensor(px + "xTsg", [128, 8, 256], BF16))
        qsg = E(nc.sbuf_tensor(px + "qsg", [128, 2, C], BF16))
        ksg = E(nc.sbuf_tensor(px + "ksg", [128, 2, C], BF16))
        vsg = E(nc.sbuf_tensor(px + "vsg", [128, 2, C], BF16))
        Qall = E(nc.sbuf_tensor(px + "Qall", [128, 32, 64], BF16))
        Kall = E(nc.sbuf_tensor(px + "Kall", [128, 32, 64], BF16))
        V65 = E(nc.sbuf_tensor(px + "V65", [128, 32, 65], BF16))
        skl = E(nc.sbuf_tensor(px + "skl", [128, 32, 64], BF16))
        QT = E(nc.sbuf_tensor(px + "QT", [64, 8, 512], BF16))
        KT = E(nc.sbuf_tensor(px + "KT", [64, 8, 512], BF16))
        sqT = E(nc.sbuf_tensor(px + "sqT", [64, 8, 512], BF16))
        scrq = E(nc.sbuf_tensor(px + "scrq", [64, 8, 512], BF16))
        PhT = E(nc.sbuf_tensor(px + "PhT", [128, 4, 512], BF16))
        maskT = E(nc.sbuf_tensor(px + "maskT_s", [128, 4, 512], BF16))
        tdot = E(nc.sbuf_tensor(px + "tdot", [64, 512], BF16))
        tmem = E(nc.sbuf_tensor(px + "tmem", [64, 512], BF16))
        am = E(nc.sbuf_tensor(px + "amx", [64, 512], BF16))
        recs = E(nc.sbuf_tensor(px + "recs", [128, 1024], BF16))
        atth = E(nc.sbuf_tensor(px + "atth", [64, 512], BF16))
        attI = E(nc.sbuf_tensor(px + "attI", [128, 32, 64], BF16))
        atts = E(nc.sbuf_tensor(px + "atts", [128, 2, C], BF16))
        attT = E(nc.sbuf_tensor(px + "attTs", [128, 8, 256], BF16))
        wq = E(nc.sbuf_tensor(px + "wqs", [128, 8, C], BF16))
        wk = E(nc.sbuf_tensor(px + "wks", [128, 8, C], BF16))
        wv = E(nc.sbuf_tensor(px + "wvs", [128, 8, C], BF16))
        wo = E(nc.sbuf_tensor(px + "wos", [128, 8, C], BF16))
        Mst = E(nc.sbuf_tensor(px + "Mst", [64, 8, 65], F32))
        Msb = E(nc.sbuf_tensor(px + "Msb", [64, 8, 65], BF16))
        gcol = E(nc.sbuf_tensor(px + "gcol", [64, 8], F32))
        g1col = E(nc.sbuf_tensor(px + "g1col", [64, 8], F32))
        onesb = E(nc.sbuf_tensor(px + "onesb", [128, 64], BF16))
        idt = E(nc.sbuf_tensor(px + "idt", [128, 128], BF16))
        identf = E(nc.sbuf_tensor(px + "identf", [128, 128], F32))
        sums = E(nc.sbuf_tensor(px + "sums", [128, 2], F32))
        ssq = E(nc.sbuf_tensor(px + "ssq", [128, 2], F32))
        mean2 = E(nc.sbuf_tensor(px + "mean2", [128, 2], F32))
        var2 = E(nc.sbuf_tensor(px + "var2", [128, 2], F32))
        rstd2 = E(nc.sbuf_tensor(px + "rstd2", [128, 2], F32))
        scrl = E(nc.sbuf_tensor(px + "scrl", [128, C], F32))
        PS = E(nc.psum_tensor(px + "PS", [128, 4096], F32))
        ldg = S("ldg")
        dq = S("dq")
        ld = S("ld")
        vv = S("vv")
        ss = S("ss")
        wt = S("wt")
        wvv = S("wvv")
        wvs = S("wvs")
        sto = S("sto")
        block = E(nc.Block())
        PSB = PS.ap().bitcast(BF16)
        nvst = [0]
        ssc = [0]
        W = Waves()

        def vedge(v):
            nvst[0] += 1
            v.wait_ge(vv, nvst[0])

        def nv_upto(k):
            return sum(1 for it in W.items[:k] if it[1] == 'v')

        def ns_upto(k):
            return sum(1 for it in W.items[:k] if it[1] == 's')

        gp = []
        ndq = [0]

        def gdma(fn, dq_bar=None, wt_wait=None, wvv_wait=None, wvs_wait=None):
            gp.append((dq_bar, wt_wait, wvv_wait, wvs_wait, fn))
            ndq[0] += 16
            return ndq[0]

        xloads = []

        def cfg(s):
            s.wait_ge(ld, 48)
            return s.activation(gcol[:, :], gcol[:, :], AF.Sigmoid)
        W.wave('s', None, cfg)

        def cfg2(v):
            v.memset(onesb[:, :], 1.0)
            v.memset(V65.ap()[:, :, 64:65], 1.0)
            return v.tensor_scalar(g1col[:, :], gcol[:, :], -1.0, 1.0, ALU.mult, ALU.add)
        W.wave('v', None, cfg2)

        seg_end_wave = [0]
        seg_ends = []
        for seg in range(8):
            myx = len(xloads)
            xloads.append((16 * (3 + myx + 1), nv_upto(seg_end_wave[0]) if seg >= 1 else None))

            def tf_xt(t, seg=seg, myx=myx):
                if seg == 0:
                    t.wait_ge(ldg, 16 * 7)
                t.wait_ge(ld, 16 * (3 + myx + 1))
                for u in range(2):
                    for n in range(8):
                        ins = t.transpose(PS[:, 1024 * u + 128 * n:1024 * u + 128 * (n + 1)],
                                          xseg[:, u, 128 * n:128 * (n + 1)], identf[:, :])
                return ins

            def cf_xt(v, seg=seg):
                return v.tensor_copy(
                    xTs.ap().rearrange("p cc (u q) -> p cc u q", u=2),
                    PS[:, 0:2048].rearrange("p (u n q) -> p n u q", u=2, n=8))
            W.wave('v', tf_xt, cf_xt)

            proj_wvs_start = None
            for pi, (wsb, outb) in enumerate([(wq, qsg), (wk, ksg), (wv, vsg)]):
                for u in range(2):
                    pp = (2 * pi + u) % 2

                    def tf(t, wsb=wsb, u=u, pp=pp):
                        for cin in range(8):
                            for half in range(2):
                                ins = t.matmul(
                                    PS[:, 2048 * pp + 512 * half:2048 * pp + 512 * (half + 1)],
                                    xTs[:, cin, 128 * u:128 * (u + 1)],
                                    wsb[:, cin, 512 * half:512 * (half + 1)],
                                    start=(cin == 0), stop=(cin == 7))
                        return ins

                    mydq = ndq[0] if seg >= 1 else None

                    def cf(s, outb=outb, u=u, pp=pp, mydq=mydq):
                        if mydq is not None and outb is qsg and u == 0:
                            s.wait_ge(dq, mydq)
                        return s.activation(outb[:, u, :], PS[:, 2048 * pp:2048 * pp + 1024],
                                            AF.Copy)
                    W.wave('s', tf, cf, lag=(0 if (pi == 0 and u == 0) else 1))
            proj_end = len(W.items)

            dq_before = ndq[0]
            for (srcb, dstd) in [(qsg, qb_d), (ksg, kb_d), (vsg, vb_d)]:
                def f(g, srcb=srcb, dstd=dstd):
                    return g.dma_start(
                        out=dstd.ap().rearrange("(u p cb) d -> p u (cb d)", p=128, cb=16),
                        in_=srcb.ap())
                gdma(f, wvs_wait=ns_upto(proj_end))
            outs_done = ndq[0]
            prev_tensor_end = seg_end_wave[0]
            for (srcd, dstb, wid) in [(qb_d, Qall, 64), (kb_d, Kall, 64), (vb_d, V65, 65)]:
                def f(g, srcd=srcd, dstb=dstb, wid=wid):
                    if wid == 65:
                        return g.dma_start(out=dstb.ap()[:, :, 0:64],
                                           in_=srcd.ap().rearrange("(m p) d -> p m d", p=128))
                    return g.dma_start(out=dstb.ap(),
                                       in_=srcd.ap().rearrange("(m p) d -> p m d", p=128))
                gdma(f, dq_bar=outs_done, wt_wait=(prev_tensor_end if seg >= 1 else None))
            dq_need = ndq[0]

            def cf_p1(v, dq_need=dq_need):
                v.wait_ge(dq, dq_need)
                v.tensor_scalar_min(skl.ap(), Kall.ap(), 0.0).then_inc(vv, 1)
                vedge(v)
                return v.tensor_copy(Msb.ap(), Mst.ap())
            W.wave('v', None, cf_p1)

            def cf_p2(s):
                return s.activation(skl.ap().rearrange("p m d -> p (m d)"),
                                    skl.ap().rearrange("p m d -> p (m d)"), AF.Exp)
            W.wave('s', None, cf_p2)

            def cf_p3(v):
                return v.scalar_tensor_tensor(skl.ap(), Kall.ap(), 0.0, skl.ap(),
                                              ALU.max, ALU.add)
            W.wave('v', None, cf_p3)

            for hl in range(8):
                def tf_t(t, hl=hl):
                    for n in range(4):
                        t.transpose(PSB[0:64, 128 * n:128 * (n + 1)],
                                    Qall[:, 4 * hl + n, :], idt[:, :])
                    for n in range(4):
                        ins = t.transpose(PSB[0:64, 1024 + 128 * n:1024 + 128 * (n + 1)],
                                          Kall[:, 4 * hl + n, :], idt[:, :])
                    return ins

                def cf_t(v, hl=hl):
                    v.tensor_copy(QT[:, hl, :], PSB[0:64, 0:512])
                    return v.tensor_copy(KT[:, hl, :], PSB[0:64, 1024:1536])
                W.wave('v', tf_t, cf_t)

            def cf_q1(v):
                return v.tensor_scalar_min(scrq.ap(), QT.ap(), 0.0)
            W.wave('v', None, cf_q1)

            def cf_q2(s):
                return s.activation(scrq.ap().rearrange("p m d -> p (m d)"),
                                    scrq.ap().rearrange("p m d -> p (m d)"), AF.Exp)
            W.wave('s', None, cf_q2)

            def cf_q3(v):
                return v.scalar_tensor_tensor(sqT.ap(), QT.ap(), 0.0, scrq.ap(),
                                              ALU.max, ALU.add)
            W.wave('v', None, cf_q3)

            for hl in range(8):
                def tf_sc(t, hl=hl):
                    for j in range(4):
                        ins = t.matmul(PS[:, 512 * j:512 * (j + 1)],
                                       KT[:, hl, 128 * j:128 * (j + 1)],
                                       QT[:, hl, :], start=True, stop=True)
                    return ins

                def cf_sc(s, hl=hl):
                    for j in range(4):
                        ins = s.activation(PhT[:, j, :], PS[:, 512 * j:512 * (j + 1)],
                                           AF.Exp, scale=0.125)
                    return ins
                W.wave('s', tf_sc, cf_sc)

                def cf_mask(v, hl=hl):
                    return v.tensor_tensor(PhT.ap(), PhT.ap(), maskT.ap(), ALU.mult)
                W.wave('v', None, cf_mask)

                def tf_pv(t, hl=hl):
                    for j in range(4):
                        t.matmul(PS[0:65, 2048:2048 + 512], V65[:, 4 * hl + j, :],
                                 PhT[:, j, :], start=(j == 0), stop=(j == 3))
                    t.matmul(PS[0:65, 2560:2560 + 512], Msb[:, hl, :], sqT[:, hl, :],
                             start=True, stop=True)
                    for j in range(4):
                        ins = t.matmul(PS[0:64, 3072:3072 + 65], skl[:, 4 * hl + j, :],
                                       V65[:, 4 * hl + j, :], start=(j == 0), stop=(j == 3))
                    return ins

                def cf_pv(v, hl=hl):
                    v.tensor_copy(tdot[:, :], PS[0:64, 2048:2048 + 512])
                    v.tensor_copy(tmem[:, :], PS[0:64, 2560:2560 + 512])
                    with nc.allow_low_precision(reason="bf16 recip rows for bcast"):
                        v.reciprocal(recs[64:65, 0:512], PS[64:65, 2048:2048 + 512])
                        v.reciprocal(recs[64:65, 512:1024], PS[64:65, 2560:2560 + 512]
                                     ).then_inc(vv, 1)
                    vedge(v)
                    return v.tensor_tensor(Mst[:, hl, :], Mst[:, hl, :],
                                           PS[0:64, 3072:3072 + 65], ALU.add)
                W.wave('v', tf_pv, cf_pv)

                def tf_bc(t, hl=hl):
                    t.matmul(PS[0:64, 0:512], onesb[64:65, 0:64], recs[64:65, 0:512],
                             start=True, stop=True)
                    ins = t.matmul(PS[0:64, 512:1024], onesb[64:65, 0:64],
                                   recs[64:65, 512:1024], start=True, stop=True)
                    return ins

                def cf_bc(v, hl=hl):
                    v.tensor_tensor(am[:, :], tmem[:, :], PS[0:64, 512:1024],
                                    ALU.mult).then_inc(vv, 1)
                    vedge(v)
                    v.tensor_scalar(am[:, :], am[:, :], gcol[:, hl:hl + 1], None,
                                    ALU.mult)
                    v.tensor_tensor(atth[:, :], tdot[:, :], PS[0:64, 0:512],
                                    ALU.mult).then_inc(vv, 1)
                    vedge(v)
                    return v.scalar_tensor_tensor(atth[:, :], atth[:, :],
                                                  g1col[:, hl:hl + 1], am[:, :],
                                                  ALU.mult, ALU.add)
                W.wave('v', tf_bc, cf_bc)

                def tf_at(t, hl=hl):
                    for n in range(4):
                        ins = t.transpose(PSB[:, 2048 + 64 * n:2048 + 64 * (n + 1)],
                                          atth[0:64, 128 * n:128 * (n + 1)], idt[0:64, 0:64])
                    return ins

                def cf_at(v, hl=hl):
                    return v.tensor_copy(attI[:, 4 * hl:4 * (hl + 1), :],
                                         PSB[:, 2048:2304].rearrange("p (n q) -> p n q", n=4))
                W.wave('v', tf_at, cf_at)
            heads_end = len(W.items)

            def f(g):
                return g.dma_start(out=ab_d.ap().rearrange("(m p) d -> p m d", p=128),
                                   in_=attI.ap())
            gdma(f, wvv_wait=nv_upto(heads_end))
            att_out_done = ndq[0]

            def f2(g):
                return g.dma_start(
                    out=atts.ap(),
                    in_=ab_d.ap().rearrange("(u p cb) d -> p u (cb d)", p=128, cb=16))
            gdma(f2, dq_bar=att_out_done)
            dq_att = ndq[0]

            def tf_at2(t, seg=seg, dq_att=dq_att):
                t.wait_ge(dq, dq_att)
                for u in range(2):
                    for n in range(8):
                        ins = t.transpose(PSB[:, 1024 * u + 128 * n:1024 * u + 128 * (n + 1)],
                                          atts[:, u, 128 * n:128 * (n + 1)], idt[:, :])
                return ins

            def cf_at2(v, seg=seg):
                return v.tensor_copy(
                    attT.ap().rearrange("p cc (u q) -> p cc u q", u=2),
                    PSB[:, 0:2048].rearrange("p (u n q) -> p n u q", u=2, n=8))
            W.wave('v', tf_at2, cf_at2)

            for u in range(2):
                wob = 2048 if u == 0 else 1024

                def tf_wo(t, u=u, wob=wob):
                    for cin in range(8):
                        for half in range(2):
                            ins = t.matmul(
                                PS[:, wob + 512 * half:wob + 512 * (half + 1)],
                                attT[:, cin, 128 * u:128 * (u + 1)],
                                wo[:, cin, 512 * half:512 * (half + 1)],
                                start=(cin == 0), stop=(cin == 7))
                    return ins

                def cf_wo(v, u=u, wob=wob):
                    v.tensor_tensor(xseg[:, u, :], PS[:, wob:wob + 1024],
                                    xseg[:, u, :], ALU.add).then_inc(vv, 1)
                    vedge(v)
                    return v.reduce_sum(sums[:, u:u + 1],
                                        xseg[:, u, :].rearrange("p (one q) -> p one q", one=1),
                                        mybir.AxisListType.X)
                W.wave('v', tf_wo, cf_wo)

            def cf_c(v, seg=seg):
                v.tensor_scalar(mean2[:, :], sums[:, :], 1.0 / C, None, ALU.mult).then_inc(vv, 1)
                vedge(v)
                for u in range(2):
                    v.tensor_scalar(xseg[:, u, :], xseg[:, u, :], mean2[:, u:u + 1],
                                    None, ALU.subtract).then_inc(vv, 1)
                    vedge(v)
                    v.tensor_tensor(scrl[:, :], xseg[:, u, :], xseg[:, u, :],
                                    ALU.mult).then_inc(vv, 1)
                    vedge(v)
                    v.reduce_sum(ssq[:, u:u + 1],
                                 scrl.ap().rearrange("p (one q) -> p one q", one=1),
                                 mybir.AxisListType.X).then_inc(vv, 1)
                    vedge(v)
                v.tensor_scalar(var2[:, :], ssq[:, :], 1.0 / C, None, ALU.mult).then_inc(vv, 1)
                vedge(v)
                v.tensor_scalar(var2[:, :], var2[:, :], 1e-5, None, ALU.add).then_inc(vv, 1)
                vedge(v)
                return v.tensor_scalar(var2[:, :], var2[:, :], 2.0 ** -40, None, ALU.mult)
            W.wave('v', None, cf_c)

            def cf_d(s, seg=seg):
                return s.activation(rstd2[:, :], var2[:, :], AF.Ln)
            W.wave('s', None, cf_d)

            def cf_d2(v, seg=seg):
                return v.tensor_scalar(rstd2[:, :], rstd2[:, :], -0.5,
                                       -13.862943611198906, ALU.mult, ALU.add)
            W.wave('v', None, cf_d2)

            def cf_d3(s, seg=seg):
                return s.activation(rstd2[:, :], rstd2[:, :], AF.Exp)
            W.wave('s', None, cf_d3)

            def cf_e(v, seg=seg):
                for u in range(2):
                    ins = v.tensor_scalar(xseg[:, u, :], xseg[:, u, :],
                                          rstd2[:, u:u + 1], None, ALU.mult)
                return ins
            W.wave('v', None, cf_e)
            seg_end_wave[0] = len(W.items)
            seg_ends.append((nv_upto(len(W.items)), len(W.items)))

        NW = len(W.items)

        @block.gpsimd
        def _(g):
            g.wait_ge(stg, kst)
            g.dma_start(out=wq.ap(), in_=Wq_d.ap().rearrange("(cc p) c -> p cc c", p=128)
                        ).then_inc(ldg, 16)
            g.dma_start(out=wk.ap(), in_=Wk_d.ap().rearrange("(cc p) c -> p cc c", p=128)
                        ).then_inc(ldg, 16)
            g.dma_start(out=wv.ap(), in_=Wv_d.ap().rearrange("(cc p) c -> p cc c", p=128)
                        ).then_inc(ldg, 16)
            g.dma_start(out=wo.ap(), in_=Wo_d.ap().rearrange("(cc p) c -> p cc c", p=128)
                        ).then_inc(ldg, 16)
            g.dma_start(out=idt.ap(), in_=id_d.ap()).then_inc(ldg, 16)
            g.dma_start(out=identf.ap(), in_=id_d.ap()).then_inc(ldg, 16)
            g.dma_start(out=maskT.ap(), in_=mt_d.ap().rearrange("p (j q) -> p j q", j=4)
                        ).then_inc(ldg, 16)
            for (dq_bar, wt_wait, wvv_wait, wvs_wait, fn) in gp:
                if dq_bar is not None:
                    g.wait_ge(dq, dq_bar)
                if wt_wait is not None:
                    g.wait_ge(wt, wt_wait)
                if wvv_wait is not None:
                    g.wait_ge(wvv, wvv_wait)
                if wvs_wait is not None:
                    g.wait_ge(wvs, wvs_wait)
                fn(g).then_inc(dq, 16)

        @block.sync
        def _(s):
            s.wait_ge(stg, kst)
            s.dma_start(out=gcol.ap(), in_=bet_d.ap()).then_inc(ld, 16)
            s.dma_start(out=Mst.ap().rearrange("p a b -> p (a b)"), in_=m0_d.ap()
                        ).then_inc(ld, 16)
            s.sem_inc(ld, 16)
            for seg in range(8):
                if seg >= 1:
                    s.wait_ge(wvv, seg_ends[seg - 1][0])
                    s.wait_ge(sto, 16 * seg)
                s.dma_start(out=xseg.ap(),
                            in_=x_d.ap()[256 * seg:256 * (seg + 1), :]
                            .rearrange("(u p) c -> p u c", p=128)).then_inc(ld, 16)
                s.wait_ge(wvv, seg_ends[seg][0])
                s.dma_start(out=x3_o.ap()[256 * seg:256 * (seg + 1), :]
                            .rearrange("(u p) c -> p u c", p=128),
                            in_=xseg.ap()).then_inc(sto, 16)
            s.wait_ge(sto, 16 * 8)
            s.sem_inc(stg, 1)

        @block.vector
        def _(v):
            v.wait_ge(stg, kst)
            v.wait_ge(ldg, 16 * 7)
            v.wait_ge(ld, 48)
            for k, (tfn, ceng, cfn, lag) in enumerate(W.items):
                if ceng != 'v':
                    continue
                v.wait_ge(wt, k + 1)
                cfn(v).then_inc(wvv, 1)

        @block.scalar
        def _(s):
            s.wait_ge(stg, kst)
            s.wait_ge(ldg, 16 * 7)
            for k, (tfn, ceng, cfn, lag) in enumerate(W.items):
                if ceng != 's':
                    continue
                s.wait_ge(wt, k + 1)
                cfn(s).then_inc(wvs, 1)

        @block.tensor
        def _(t):
            t.wait_ge(stg, kst)
            engs = [it[1] for it in W.items]
            for k, (tfn, ceng, cfn, lag) in enumerate(W.items):
                lim = max(0, k - lag)
                nvec = sum(1 for e in engs[:lim] if e == 'v')
                nsca = lim - nvec
                if nvec:
                    t.wait_ge(wvv, nvec)
                if nsca:
                    t.wait_ge(wvs, nsca)
                if tfn is not None:
                    tfn(t).then_inc(wt, 1)
                else:
                    t.sem_inc(wt, 1)


def build_l1ffn():
    nc = bass.Bass("TRN2", target_bir_lowering=False)
    io = dict(
        x3=nc.dram_tensor("x3", [T, C], F32, kind="ExternalInput"),
        w1T=nc.dram_tensor("w1T", [8, C, 4096], F32, kind="ExternalInput"),
        w3T=nc.dram_tensor("w3T", [8, C, 4096], F32, kind="ExternalInput"),
        w2T=nc.dram_tensor("w2T", [8, 4096, C], F32, kind="ExternalInput"),
        gwT=nc.dram_tensor("gwT", [C, 8], F32, kind="ExternalInput"),
        ident=nc.dram_tensor("ident", [128, 128], BF16, kind="ExternalInput"),
        out_o=nc.dram_tensor("out_o", [T, C], F32, kind="ExternalOutput"),
    )
    with ExitStack() as ES:
        stgs = ES.enter_context(nc.semaphore("stgs"))
        build_l1ffn_body(nc, ES, io, "", stgs, 0)
    return nc


def build_l1ffn_body(nc, ES, io, px, stg, kst):
    x3_d, W1_d, W3_d, W2_d = io["x3"], io["w1T"], io["w3T"], io["w2T"]
    gw_d, id_d, out_o = io["gwT"], io["ident"], io["out_o"]

    def S(n):
        return ES.enter_context(nc.semaphore(px + n))

    with ExitStack() as ctx:
        E = ctx.enter_context
        accum = E(nc.sbuf_tensor(px + "accum", [128, 16, C], F32))
        xT = E(nc.sbuf_tensor(px + "xT", [128, 8, T], BF16))
        h1T = E(nc.sbuf_tensor(px + "h1T", [128, 8, T], BF16))
        w1qb = E(nc.sbuf_tensor(px + "w1qb", [128, 1, 8192], BF16))
        w3qb = E(nc.sbuf_tensor(px + "w3qb", [128, 1, 8192], BF16))
        w2qb = E(nc.sbuf_tensor(px + "w2qb", [128, 1, 8192], BF16))
        gwt = E(nc.sbuf_tensor(px + "gwt", [128, 8, 8], BF16))
        identf = E(nc.sbuf_tensor(px + "identf", [128, 128], F32))
        scrA = E(nc.sbuf_tensor(px + "scrA", [128, C], F32))
        scb = E(nc.sbuf_tensor(px + "scb", [8, 512], F32))
        sco = E(nc.sbuf_tensor(px + "sco", [128, 16, 8], F32))
        gates = E(nc.sbuf_tensor(px + "gates", [128, 16, 8], F32))
        m1t = E(nc.sbuf_tensor(px + "m1t", [128, 16], F32))
        m2t = E(nc.sbuf_tensor(px + "m2t", [128, 16], F32))
        rsum = E(nc.sbuf_tensor(px + "rsum", [128, 16], F32))
        ssq = E(nc.sbuf_tensor(px + "ssqf", [128, 16], F32))
        sums = E(nc.sbuf_tensor(px + "sums", [128, 16], F32))
        mean2 = E(nc.sbuf_tensor(px + "mean2", [128, 16], F32))
        var2 = E(nc.sbuf_tensor(px + "var2", [128, 16], F32))
        rstd2 = E(nc.sbuf_tensor(px + "rstd2", [128, 16], F32))
        PS = E(nc.psum_tensor(px + "PS", [128, 4096], F32))
        ldg = S("ldg")
        ww0 = S("ww0")
        ww3 = S("ww3")
        ww2 = S("ww2")
        vv = S("vv")
        ss = S("ss")
        wt = S("wt")
        wvv = S("wvv")
        wvs = S("wvs")
        sto = S("sto")
        block = E(nc.Block())
        nvst = [0]
        ssc = [0]
        W = Waves()
        QEND = {}
        MM1END = {}
        MM3END = {}

        def vedge(v):
            nvst[0] += 1
            v.wait_ge(vv, nvst[0])

        for tt in range(16):
            def cfa(v, tt=tt):
                return v.tensor_copy(scrA[:, :], accum[:, tt, :])
            W.wave('v', None, cfa)

            def tfb(t, tt=tt):
                if tt == 0:
                    t.wait_ge(ldg, 16 * 3)
                for n in range(8):
                    ins = t.transpose(PS[:, 3072 + 128 * n:3072 + 128 * (n + 1)],
                                      scrA[:, 128 * n:128 * (n + 1)], identf[:, :])
                return ins

            def cfb(v, tt=tt):
                return v.tensor_copy(xT[:, :, 128 * tt:128 * (tt + 1)],
                                     PS[:, 3072:4096].rearrange("p (n q) -> p n q", n=8))
            W.wave('v', tfb, cfb)

        for tc in range(4):
            def tf(t, tc=tc):
                for cc in range(8):
                    ins = t.matmul(PS[0:8, 2560:2560 + 512], gwt[:, cc, :],
                                   xT[:, cc, 512 * tc:512 * (tc + 1)],
                                   start=(cc == 0), stop=(cc == 7))
                return ins

            def cf(s, tc=tc):
                return s.activation(scb[0:8, :], PS[0:8, 2560:2560 + 512], AF.Exp)
            W.wave('s', tf, cf)

            def tf2(t, tc=tc):
                for n in range(4):
                    ins = t.transpose(PS[:, 2048 + 8 * n:2048 + 8 * (n + 1)],
                                      scb[0:8, 128 * n:128 * (n + 1)],
                                      identf[0:8, 0:8])
                return ins

            def cf2(v, tc=tc):
                return v.tensor_copy(sco[:, 4 * tc:4 * (tc + 1), :],
                                     PS[:, 2048:2048 + 32].rearrange("p (n q) -> p n q", n=4))
            W.wave('v', tf2, cf2)

        def cf4(v):
            v.reduce_max(m1t[:, :], sco.ap(), mybir.AxisListType.X).then_inc(vv, 1)
            vedge(v)
            selt_v = scrA[:, 0:128].rearrange("p (h e) -> p h e", h=16)
            v.tensor_tensor(selt_v, sco.ap(),
                            m1t.ap().rearrange("p (h one) -> p h one", one=1)
                            .broadcast_to([128, 16, 8]), ALU.is_ge).then_inc(vv, 1)
            vedge(v)
            v.scalar_tensor_tensor(selt_v, selt_v, -1e30, sco.ap(),
                                   ALU.mult, ALU.add).then_inc(vv, 1)
            vedge(v)
            v.reduce_max(m2t[:, :], selt_v, mybir.AxisListType.X).then_inc(vv, 1)
            vedge(v)
            v.tensor_tensor(selt_v, sco.ap(),
                            m2t.ap().rearrange("p (h one) -> p h one", one=1)
                            .broadcast_to([128, 16, 8]), ALU.is_ge).then_inc(vv, 1)
            vedge(v)
            v.tensor_tensor(selt_v, selt_v, sco.ap(), ALU.mult).then_inc(vv, 1)
            vedge(v)
            v.reduce_sum(rsum[:, :], selt_v, mybir.AxisListType.X).then_inc(vv, 1)
            vedge(v)
            v.reciprocal(rsum[:, :], rsum[:, :]).then_inc(vv, 1)
            vedge(v)
            return v.tensor_tensor(gates.ap(), selt_v,
                                   rsum.ap().rearrange("p (h one) -> p h one", one=1)
                                   .broadcast_to([128, 16, 8]), ALU.mult)
        W.wave('v', None, cf4)

        GQ = [0]
        for e in range(8):
            for q in range(4):
                gq = GQ[0]
                GQ[0] += 1
                sl = gq % 2
                gu = 0
                for phase in range(2):
                    for ht in range(8):
                        for th in range(2):
                            unit_i = phase * 16 + ht * 2 + th
                            pt = (gq * 32 + unit_i) % 4

                            def tf(t, ht=ht, th=th, pt=pt, gq=gq, phase=phase):
                                if phase == 0 and ht == 0 and th == 0:
                                    t.wait_ge(ww0, 16 * (gq + 1))
                                if phase == 1 and ht == 0 and th == 0:
                                    t.wait_ge(ww3, 16 * (gq + 1))
                                wsb = w1qb if phase == 0 else w3qb
                                for cg in range(8):
                                    for half in range(2):
                                        ins = t.matmul(
                                            PS[:, 1024 * pt + 512 * half:1024 * pt + 512 * (half + 1)],
                                            wsb[:, 0, 1024 * cg + 128 * ht:1024 * cg + 128 * (ht + 1)],
                                            xT[:, cg, 1024 * th + 512 * half:1024 * th + 512 * (half + 1)],
                                            start=(cg == 0), stop=(cg == 7))
                                return ins

                            if phase == 0:
                                def cf(s, ht=ht, th=th, pt=pt):
                                    return s.activation(h1T[:, ht, 1024 * th:1024 * (th + 1)],
                                                        PS[:, 1024 * pt:1024 * (pt + 1)],
                                                        AF.Silu)
                                W.wave('s', tf, cf, lag=(unit_i if unit_i < 3 else 3))
                            else:
                                def cf(v, ht=ht, th=th, pt=pt):
                                    return v.tensor_tensor(
                                        h1T[:, ht, 1024 * th:1024 * (th + 1)],
                                        h1T[:, ht, 1024 * th:1024 * (th + 1)],
                                        PS[:, 1024 * pt:1024 * (pt + 1)], ALU.mult)
                                W.wave('v', tf, cf, lag=3)
                    if phase == 0:
                        MM1END[gq] = len(W.items)
                    else:
                        MM3END[gq] = len(W.items)
                for tt in range(16):
                    gt = gq * 16 + tt
                    p2 = gt % 2

                    def tf(t, tt=tt, p2=p2, gq=gq):
                        if tt == 0:
                            t.wait_ge(ww2, 16 * (gq + 1))
                        for hc in range(8):
                            for half in range(2):
                                ins = t.matmul(
                                    PS[:, 1024 * p2 + 512 * half:1024 * p2 + 512 * (half + 1)],
                                    h1T[:, hc, 128 * tt:128 * (tt + 1)],
                                    w2qb[:, 0, 1024 * hc + 512 * half:1024 * hc + 512 * (half + 1)],
                                    start=(hc == 0), stop=(hc == 7))
                        return ins

                    def cf(v, tt=tt, p2=p2, e=e):
                        return v.scalar_tensor_tensor(
                            accum[:, tt, :], PS[:, 1024 * p2:1024 * (p2 + 1)],
                            gates[:, tt, e:e + 1], accum[:, tt, :], ALU.mult, ALU.add)
                    W.wave('v', tf, cf, lag=(0 if tt == 0 else 1))
                QEND[gq] = len(W.items)

        for tt in range(16):
            def cf(v, tt=tt):
                v.tensor_tensor(scrA[:, :], accum[:, tt, :], accum[:, tt, :],
                                ALU.mult).then_inc(vv, 1)
                vedge(v)
                v.reduce_sum(sums[:, tt:tt + 1],
                             accum[:, tt, :].rearrange("p (one q) -> p one q", one=1),
                             mybir.AxisListType.X)
                return v.reduce_sum(ssq[:, tt:tt + 1],
                                    scrA.ap().rearrange("p (one q) -> p one q", one=1),
                                    mybir.AxisListType.X)
            W.wave('v', None, cf)

        def cfc(v):
            v.tensor_scalar(mean2[:, :], sums[:, :], 1.0 / C, None, ALU.mult)
            v.tensor_scalar(var2[:, :], ssq[:, :], 1.0 / C, None, ALU.mult).then_inc(vv, 1)
            vedge(v)
            v.tensor_tensor(scrA[:, 0:16], mean2[:, :], mean2[:, :], ALU.mult).then_inc(vv, 1)
            vedge(v)
            v.tensor_tensor(var2[:, :], var2[:, :], scrA[:, 0:16], ALU.subtract).then_inc(vv, 1)
            vedge(v)
            return v.tensor_scalar(var2[:, :], var2[:, :], 1e-5, None, ALU.add)
        W.wave('v', None, cfc)

        def cfd(s):
            s.activation(rstd2[:, :], var2[:, :], AF.Ln).then_inc(ss, 1)
            ssc[0] += 1
            s.wait_ge(ss, ssc[0])
            return s.activation(rstd2[:, :], rstd2[:, :], AF.Exp, scale=-0.5)
        W.wave('s', None, cfd)

        def cfe(v):
            for tt in range(16):
                ins = v.tensor_scalar(accum[:, tt, :], accum[:, tt, :], mean2[:, tt:tt + 1],
                                      rstd2[:, tt:tt + 1], ALU.subtract, ALU.mult)
            return ins
        W.wave('v', None, cfe)

        @block.gpsimd
        def _(g):
            g.wait_ge(stg, kst)
            g.dma_start(out=accum.ap(), in_=x3_d.ap().rearrange("(tt p) c -> p tt c", p=128)
                        ).then_inc(ldg, 16)
            g.dma_start(out=gwt.ap(), in_=gw_d.ap().rearrange("(cc p) e -> p cc e", p=128)
                        ).then_inc(ldg, 16)
            g.dma_start(out=identf.ap(), in_=id_d.ap()).then_inc(ldg, 16)
            for e in range(8):
                for q in range(4):
                    gq = 4 * e + q
                    if gq >= 1:
                        g.wait_ge(wt, MM1END[gq - 1])
                    g.dma_start(out=w1qb[:, 0, :].rearrange("p (cg h) -> p cg h", cg=8),
                                in_=W1_d.ap()[e].rearrange("(cg p) h -> p cg h", p=128)
                                [:, :, 1024 * q:1024 * (q + 1)]).then_inc(ww0, 16)
                    if gq >= 1:
                        g.wait_ge(wt, MM3END[gq - 1])
                    g.dma_start(out=w3qb[:, 0, :].rearrange("p (cg h) -> p cg h", cg=8),
                                in_=W3_d.ap()[e].rearrange("(cg p) h -> p cg h", p=128)
                                [:, :, 1024 * q:1024 * (q + 1)]).then_inc(ww3, 16)
                    if gq >= 1:
                        g.wait_ge(wt, QEND[gq - 1])
                    g.dma_start(out=w2qb[:, 0, :].rearrange("p (hc c) -> p hc c", hc=8),
                                in_=W2_d.ap()[e].rearrange("(q hc p) c -> q p hc c", p=128, hc=8)[q]
                                ).then_inc(ww2, 16)

        @block.sync
        def _(s):
            s.wait_ge(stg, kst)
            nvec = sum(1 for it in W.items if it[1] == 'v')
            s.wait_ge(wvv, nvec)
            s.dma_start(out=out_o.ap().rearrange("(tt p) c -> p tt c", p=128),
                        in_=accum.ap()).then_inc(sto, 16)
            s.wait_ge(sto, 16)
            s.sem_inc(stg, 1)

        @block.vector
        def _(v):
            v.wait_ge(stg, kst)
            v.wait_ge(ldg, 16 * 3)
            for k, (tfn, ceng, cfn, lag) in enumerate(W.items):
                if ceng != 'v':
                    continue
                v.wait_ge(wt, k + 1)
                cfn(v).then_inc(wvv, 1)

        @block.scalar
        def _(s):
            s.wait_ge(stg, kst)
            s.wait_ge(ldg, 16 * 3)
            for k, (tfn, ceng, cfn, lag) in enumerate(W.items):
                if ceng != 's':
                    continue
                s.wait_ge(wt, k + 1)
                cfn(s).then_inc(wvs, 1)

        @block.tensor
        def _(t):
            t.wait_ge(stg, kst)
            engs = [it[1] for it in W.items]
            for k, (tfn, ceng, cfn, lag) in enumerate(W.items):
                lim = max(0, k - lag)
                nvec = sum(1 for e2 in engs[:lim] if e2 == 'v')
                nsca = lim - nvec
                if nvec:
                    t.wait_ge(wvv, nvec)
                if nsca:
                    t.wait_ge(wvs, nsca)
                if tfn is not None:
                    tfn(t).then_inc(wt, 1)
                else:
                    t.sem_inc(wt, 1)


def build_xexch_body(nc, ES, io, px, stg, kst):
    x2_d, agx_d, xin_d = io["x2"], io["agx"], io["xin5"]
    me_d, mo_d = io["me"], io["mo"]

    def S(n):
        return ES.enter_context(nc.semaphore(px + n))

    with ExitStack() as ctx:
        E = ctx.enter_context
        bufA = E(nc.sbuf_tensor(px + "bufA", [128, 2, 2, C], F32))
        bufB = E(nc.sbuf_tensor(px + "bufB", [128, 2, 2, C], F32))
        xo = E(nc.sbuf_tensor(px + "xo", [128, 2, 2, C], F32))
        mes = E(nc.sbuf_tensor(px + "mes", [128, 1], F32))
        mos = E(nc.sbuf_tensor(px + "mos", [128, 1], F32))
        ld = S("ld")
        cc = S("cc")
        l0 = S("l0")
        l1 = S("l1")
        so0 = S("so0")
        so1 = S("so1")
        vb = S("vb")
        lds = [l0, l1]
        sos = [so0, so1]
        block = E(nc.Block())

        @block.gpsimd
        def _(g):
            g.wait_ge(stg, kst)
            g.collective_compute(
                "AllGather", ALU.bypass, replica_groups=PAIRS4x2,
                ins=[x2_d.ap()], outs=[agx_d.ap()]).then_inc(cc, 1)
            g.wait_ge(cc, 1)
            ag = agx_d.ap()
            for s8 in range(8):
                sl = s8 % 2
                if s8 >= 2:
                    g.wait_ge(vb, 2 * (s8 - 1))
                g.dma_start(out=bufA[:, sl, :, :],
                            in_=ag[512 * s8:512 * s8 + 256, :]
                            .rearrange("(u p) c -> p u c", p=128)).then_inc(lds[sl], 16)
                g.dma_start(out=bufB[:, sl, :, :],
                            in_=ag[512 * s8 + 256:512 * s8 + 512, :]
                            .rearrange("(u p) c -> p u c", p=128)).then_inc(lds[sl], 16)

        @block.sync
        def _(s):
            s.wait_ge(stg, kst)
            s.dma_start(out=mes.ap(), in_=me_d.ap()).then_inc(ld, 16)
            s.dma_start(out=mos.ap(), in_=mo_d.ap()).then_inc(ld, 16)
            for s8 in range(8):
                sl = s8 % 2
                s.wait_ge(vb, 2 * s8 + 2)
                s.dma_start(out=xin_d.ap()[256 * s8:256 * (s8 + 1), :]
                            .rearrange("(u p) c -> p u c", p=128),
                            in_=xo[:, sl, :, :]).then_inc(sos[sl], 16)
            s.wait_ge(sos[0], 16 * 4)
            s.wait_ge(sos[1], 16 * 4)
            s.sem_inc(stg, 1)

        @block.vector
        def _(v):
            v.wait_ge(stg, kst)
            v.wait_ge(ld, 32)
            for s8 in range(8):
                sl = s8 % 2
                v.wait_ge(lds[sl], 32 * (s8 // 2 + 1))
                if s8 >= 2:
                    v.wait_ge(sos[sl], 16 * (s8 // 2))
                xof = xo[:, sl, :, :].rearrange("p u c -> p (u c)")
                v.tensor_scalar(xof, bufA[:, sl, :, :].rearrange("p u c -> p (u c)"),
                                mes[:, 0:1], None, ALU.mult).then_inc(vb, 1)
                v.wait_ge(vb, 2 * s8 + 1)
                v.scalar_tensor_tensor(xof, bufB[:, sl, :, :].rearrange("p u c -> p (u c)"),
                                       mos[:, 0:1], xof,
                                       ALU.mult, ALU.add).then_inc(vb, 1)


def build_fused():
    nc = bass.Bass("TRN2", target_bir_lowering=False)
    EI, EO, IN = "ExternalInput", "ExternalOutput", "Internal"
    D = nc.dram_tensor
    t_ident = D("ident", [128, 128], BF16, kind=EI)
    io_front = dict(
        xT=D("xT", [C, T + 1], F32, kind=EI),
        maas=D("maas", [128, 7 * 8], F32, kind=EI),
        maa_w1=D("maa_w1", [C, 160], F32, kind=EI),
        maa_w2=D("maa_w2", [5, 32, C], F32, kind=EI),
        Wk=D("Wk0", [C, C], F32, kind=EI),
        Wv=D("Wv0", [C, C], F32, kind=EI),
        Wr=D("Wr0", [C, C], F32, kind=EI),
        gate_w1=D("gate_w1", [C, 64], F32, kind=EI),
        gate_w2=D("gate_w2", [64, C], F32, kind=EI),
        decay_w1=D("decay_w1", [C, 64], F32, kind=EI),
        decay_w2=D("decay_w2", [64, C], F32, kind=EI),
        rT_o=D("rTf", [C, T], BF16, kind=IN),
        kT_o=D("kTf", [C, T], BF16, kind=IN),
        v_o=D("vf", [T, C], BF16, kind=IN),
        wrawT_o=D("wTf", [C, T], F32, kind=IN),
        g_o=D("gf", [T, C], BF16, kind=IN),
    )
    io_wkv = dict(
        rTf=io_front["rT_o"], kTf=io_front["kT_o"], wTf=io_front["wrawT_o"],
        vf=io_front["v_o"],
        ubc2=D("ubc2", [128, 512], F32, kind=EI),
        maskst=D("maskst", [128, 64], F32, kind=EI),
        eyem=D("eyem", [128, 64], F32, kind=EI),
        ident=t_ident,
        usin=D("usin", [128, 1], F32, kind=EI),
        y2=D("y2b", [T, C], BF16, kind=IN),
        sfb=D("sfb", [128, 512], F32, kind=IN),
        agS=D("agS", [256, 512], F32, kind=IN),
    )
    io_back = dict(
        y_in=io_wkv["y2"], g_in=io_front["g_o"],
        x_in=D("xr", [T, C], F32, kind=EI),
        lngr=D("lngr", [128, C], F32, kind=EI),
        lnbr=D("lnbr", [128, C], F32, kind=EI),
        Wo=D("Wo0", [C, C], F32, kind=EI),
        ident=t_ident,
        x1_o=D("x1b", [T, C], F32, kind=IN),
    )
    io_ffn0 = dict(
        x1=io_back["x1_o"],
        W1a=D("W1a", [10, C, 4096], F32, kind=EI),
        W2a=D("W2a", [10, 4096, C], F32, kind=EI),
        b1a=D("b1a", [128, 10, 32], F32, kind=EI),
        b2a=D("b2a", [10, C], F32, kind=EI),
        cent=D("cent", [C, 8], F32, kind=EI),
        gsc=D("gsc", [128, 8], F32, kind=EI),
        grc=D("grc", [128, 8], F32, kind=EI),
        ident=t_ident,
        x2_o=D("x2b", [T, C], F32, kind=IN),
    )
    io_exch = dict(
        x2=io_ffn0["x2_o"],
        agx=D("agx", [2 * T, C], F32, kind=IN),
        xin5=D("xin5", [T, C], F32, kind=IN),
        me=D("me", [128, 1], F32, kind=EI),
        mo=D("mo", [128, 1], F32, kind=EI),
    )
    io_attn = dict(
        x_in=io_exch["xin5"],
        Wq=D("Wq1", [C, C], F32, kind=EI),
        Wk=D("Wk1", [C, C], F32, kind=EI),
        Wv=D("Wv1", [C, C], F32, kind=EI),
        Wo=D("Wo1", [C, C], F32, kind=EI),
        betas=D("betas", [64, 8], F32, kind=EI),
        mem0z=D("mem0z", [64, 8 * 65], F32, kind=EI),
        maskT=D("maskT", [128, 4 * 512], F32, kind=EI),
        ident=t_ident,
        x3_o=D("x3b", [T, C], F32, kind=IN),
        qbuf=D("qbuf", [512 * 8, 64], BF16, kind=IN),
        kbuf=D("kbuf", [512 * 8, 64], BF16, kind=IN),
        vbuf=D("vbuf", [512 * 8, 64], BF16, kind=IN),
        abuf=D("abuf", [512 * 8, 64], BF16, kind=IN),
    )
    io_ffn1 = dict(
        x3=io_attn["x3_o"],
        w1T=D("w1T", [8, C, 4096], F32, kind=EI),
        w3T=D("w3T", [8, C, 4096], F32, kind=EI),
        w2T=D("w2T", [8, 4096, C], F32, kind=EI),
        gwT=D("gwT", [C, 8], F32, kind=EI),
        ident=t_ident,
        out_o=D("out_o", [T, C], F32, kind=EO),
    )
    with ExitStack() as ES:
        stg = ES.enter_context(nc.semaphore("stg"))
        with nc.named_scope("front"):
            build_l0front_body(nc, ES, io_front, "f_", stg, 0)
        with nc.named_scope("wkv"):
            build_wkv2_body(nc, ES, io_wkv, "w_", stg, 1)
        with nc.named_scope("back"):
            build_l0back_body(nc, ES, io_back, "b_", stg, 2)
        with nc.named_scope("ffn0"):
            build_l0ffn_body(nc, ES, io_ffn0, "m_", stg, 3)
        with nc.named_scope("exch"):
            build_xexch_body(nc, ES, io_exch, "e_", stg, 4)
        with nc.named_scope("attn"):
            build_l1attn_body(nc, ES, io_attn, "a_", stg, 5)
        with nc.named_scope("ffn1"):
            build_l1ffn_body(nc, ES, io_ffn1, "n_", stg, 6)
    return nc


def prep_fused(inputs):
    f32 = np.float32
    x = np.asarray(inputs["L0_attn_x"], f32)
    maas = np.stack([_cmaj(np.asarray(inputs[k], f32)) for k in
                     ["L0_attn_maa_x", "L0_attn_maa_w", "L0_attn_maa_k",
                      "L0_attn_maa_v", "L0_attn_maa_r", "L0_attn_maa_g",
                      "L0_attn_time_decay"]], axis=1)
    maas = np.ascontiguousarray(maas.reshape(128, 56), f32)
    u = np.asarray(inputs["L0_attn_u"], f32)
    ubc2 = np.zeros((128, 512), f32)
    for g in range(2):
        for hl in range(8):
            j, hp = hl // 2, hl % 2
            ubc2[64 * hp:64 * hp + 64, 256 * g + 64 * j:256 * g + 64 * (j + 1)] = \
                u[8 * g + hl][:, None]
    maskst = np.zeros((128, 64), f32)
    eyem = np.zeros((128, 64), f32)
    for p in range(128):
        for tcol in range(64):
            if (p % 64) < tcol:
                maskst[p, tcol] = 1.0
            if (p % 64) == tcol:
                eyem[p, tcol] = 1.0
    lngr = np.ascontiguousarray(np.broadcast_to(
        np.asarray(inputs["L0_attn_ln_g"], f32), (128, C)))
    lnbr = np.ascontiguousarray(np.broadcast_to(
        np.asarray(inputs["L0_attn_ln_b"], f32), (128, C)))
    W1a = np.ascontiguousarray(np.concatenate(
        [np.asarray(inputs["L0_ffn_W1_s"], f32), np.asarray(inputs["L0_ffn_W1_r"], f32)], 0))
    W2a = np.ascontiguousarray(np.concatenate(
        [np.asarray(inputs["L0_ffn_W2_s"], f32), np.asarray(inputs["L0_ffn_W2_r"], f32)], 0))
    b1v = np.concatenate([np.asarray(inputs["L0_ffn_b1_s"], f32),
                          np.asarray(inputs["L0_ffn_b1_r"], f32)], 0)
    b1t = np.ascontiguousarray(b1v.reshape(10, 32, 128).transpose(2, 0, 1))
    b2a = np.ascontiguousarray(np.concatenate(
        [np.asarray(inputs["L0_ffn_b2_s"], f32), np.asarray(inputs["L0_ffn_b2_r"], f32)], 0))
    maskT = np.zeros((128, 4, 512), f32)
    for j in range(4):
        for p in range(128):
            maskT[p, j, 128 * j + p:] = 1.0
    maskT = np.ascontiguousarray(maskT.reshape(128, 2048))
    betas = np.asarray(inputs["L1_attn_betas"], f32)[0, :, 0, :]
    z0 = np.asarray(inputs["L1_attn_z0"], f32)[0, :, 0, :]
    m0 = np.asarray(inputs["L1_attn_memory0"], f32)[0]
    m0z = np.concatenate([m0, z0[:, :, None]], axis=2)
    shared = dict(
        ident=_IDENT, maas=maas,
        maa_w1=np.asarray(inputs["L0_attn_maa_w1"], f32),
        maa_w2=np.asarray(inputs["L0_attn_maa_w2"], f32),
        Wk0=np.asarray(inputs["L0_attn_Wk"], f32),
        Wv0=np.asarray(inputs["L0_attn_Wv"], f32),
        Wr0=np.asarray(inputs["L0_attn_Wr"], f32),
        gate_w1=np.asarray(inputs["L0_attn_gate_w1"], f32),
        gate_w2=np.asarray(inputs["L0_attn_gate_w2"], f32),
        decay_w1=np.asarray(inputs["L0_attn_decay_w1"], f32),
        decay_w2=np.asarray(inputs["L0_attn_decay_w2"], f32),
        ubc2=ubc2, maskst=maskst, eyem=eyem,
        lngr=lngr, lnbr=lnbr,
        Wo0=np.asarray(inputs["L0_attn_Wo"], f32),
        W1a=W1a, W2a=W2a, b1a=b1t, b2a=b2a,
        cent=np.asarray(inputs["L0_ffn_centroids"], f32),
        gsc=np.ascontiguousarray(
            np.asarray(inputs["L0_ffn_g_shared"], f32).reshape(8, 128).T),
        grc=np.ascontiguousarray(
            np.asarray(inputs["L0_ffn_g_routed"], f32).reshape(8, 128).T),
        Wq1=np.asarray(inputs["L1_attn_Wq"], f32),
        Wk1=np.asarray(inputs["L1_attn_Wk"], f32),
        Wv1=np.asarray(inputs["L1_attn_Wv"], f32),
        Wo1=np.asarray(inputs["L1_attn_Wo"], f32),
        maskT=maskT,
        w1T=np.ascontiguousarray(np.asarray(inputs["L1_ffn_w1"], f32).transpose(0, 2, 1)),
        w3T=np.ascontiguousarray(np.asarray(inputs["L1_ffn_w3"], f32).transpose(0, 2, 1)),
        w2T=np.ascontiguousarray(np.asarray(inputs["L1_ffn_w2"], f32).transpose(0, 2, 1)),
        gwT=np.ascontiguousarray(np.asarray(inputs["L1_ffn_gate_w"], f32).T),
    )
    maps = []
    for c in range(8):
        b, hf = c // 2, c % 2
        xb = np.asarray(x[b], f32)
        if hf == 0:
            xT = np.concatenate([np.zeros((C, 1), f32), xb[0:2048].T], axis=1)
        else:
            xT = xb[2047:4096].T
        h0 = 8 * hf
        maps.append(dict(
            shared,
            xT=np.ascontiguousarray(xT),
            xr=np.ascontiguousarray(xb[2048 * hf:2048 * (hf + 1)]),
            usin=np.full((128, 1), float(hf), f32),
            me=np.full((128, 1), 1.0 - hf, f32),
            mo=np.full((128, 1), float(hf), f32),
            betas=np.ascontiguousarray(betas[h0:h0 + 8].T),
            mem0z=np.ascontiguousarray(
                m0z[h0:h0 + 8].transpose(1, 0, 2).reshape(64, 8 * 65)),
        ))
    return maps


def kernel_fused(**inputs):
    nc = build_fused()
    maps = prep_fused(inputs)
    res = run_bass_kernel_spmd(nc, maps, core_ids=list(range(8))).results
    out = np.empty((4, 4096, C), np.float32)
    srows = [_striped_rows(0), _striped_rows(1)]
    for c in range(8):
        b, hg = c // 2, c % 2
        out[b][srows[hg]] = np.asarray(res[c]["out_o"])
    return out


_IDENT = np.eye(128, dtype=BF)


def _striped_rows(hg):
    idx = []
    for s in range(8):
        idx.extend(range(512 * s + 256 * hg, 512 * s + 256 * (hg + 1)))
    return np.array(idx)


def kernel(**inputs):
    return kernel_fused(**inputs)


def kernel_multi(**inputs):
    f32 = np.float32
    x = np.asarray(inputs["L0_attn_x"], f32)

    nc1 = build_l0front()
    maps1 = prep_l0front(inputs)
    r1 = run_bass_kernel_spmd(nc1, maps1, core_ids=list(range(8))).results

    nc2 = build_wkv()
    maps2 = prep_wkv(r1, np.asarray(inputs["L0_attn_u"], f32))
    r2 = run_bass_kernel_spmd(nc2, maps2, core_ids=list(range(8))).results

    nc3 = build_l0back()
    lngr = np.ascontiguousarray(np.broadcast_to(
        np.asarray(inputs["L0_attn_ln_g"], f32), (128, C)))
    lnbr = np.ascontiguousarray(np.broadcast_to(
        np.asarray(inputs["L0_attn_ln_b"], f32), (128, C)))
    Wo0 = np.asarray(inputs["L0_attn_Wo"], f32)
    maps3 = []
    for c in range(8):
        b, hf = c // 2, c % 2
        yb = np.concatenate([np.asarray(r2[2 * b]["y"]),
                             np.asarray(r2[2 * b + 1]["y"])], axis=1)
        maps3.append(dict(y_in=np.ascontiguousarray(yb[2048 * hf:2048 * (hf + 1)]),
                          g_in=np.asarray(r1[c]["g_o"]),
                          x_in=np.ascontiguousarray(x[b, 2048 * hf:2048 * (hf + 1)]),
                          lngr=lngr, lnbr=lnbr, Wo=Wo0, ident=_IDENT))
    r3 = run_bass_kernel_spmd(nc3, maps3, core_ids=list(range(8))).results

    nc4 = build_l0ffn()
    W1a = np.ascontiguousarray(np.concatenate(
        [np.asarray(inputs["L0_ffn_W1_s"], f32), np.asarray(inputs["L0_ffn_W1_r"], f32)], 0))
    W2a = np.ascontiguousarray(np.concatenate(
        [np.asarray(inputs["L0_ffn_W2_s"], f32), np.asarray(inputs["L0_ffn_W2_r"], f32)], 0))
    b1v = np.concatenate([np.asarray(inputs["L0_ffn_b1_s"], f32),
                          np.asarray(inputs["L0_ffn_b1_r"], f32)], 0)
    b1t = np.ascontiguousarray(b1v.reshape(10, 32, 128).transpose(2, 0, 1))
    b2a = np.ascontiguousarray(np.concatenate(
        [np.asarray(inputs["L0_ffn_b2_s"], f32), np.asarray(inputs["L0_ffn_b2_r"], f32)], 0))
    shared4 = dict(W1a=W1a, W2a=W2a, b1a=b1t, b2a=b2a,
                   cent=np.asarray(inputs["L0_ffn_centroids"], f32),
                   gsc=np.ascontiguousarray(
                       np.asarray(inputs["L0_ffn_g_shared"], f32).reshape(8, 128).T),
                   grc=np.ascontiguousarray(
                       np.asarray(inputs["L0_ffn_g_routed"], f32).reshape(8, 128).T),
                   ident=_IDENT)
    maps4 = [dict(shared4, x1=np.asarray(r3[c]["x1_o"])) for c in range(8)]
    r4 = run_bass_kernel_spmd(nc4, maps4, core_ids=list(range(8))).results

    nc5 = build_l1attn()
    maskT = np.zeros((128, 4, 512), f32)
    for j in range(4):
        for p in range(128):
            maskT[p, j, 128 * j + p:] = 1.0
    maskT = np.ascontiguousarray(maskT.reshape(128, 2048))
    betas = np.asarray(inputs["L1_attn_betas"], f32)[0, :, 0, :]
    z0 = np.asarray(inputs["L1_attn_z0"], f32)[0, :, 0, :]
    m0 = np.asarray(inputs["L1_attn_memory0"], f32)[0]
    m0z = np.concatenate([m0, z0[:, :, None]], axis=2)
    shared5 = dict(Wq=np.asarray(inputs["L1_attn_Wq"], f32),
                   Wk=np.asarray(inputs["L1_attn_Wk"], f32),
                   Wv=np.asarray(inputs["L1_attn_Wv"], f32),
                   Wo=np.asarray(inputs["L1_attn_Wo"], f32),
                   maskT=maskT, ident=_IDENT)
    maps5 = []
    srows = [_striped_rows(0), _striped_rows(1)]
    for c in range(8):
        b, hg = c // 2, c % 2
        x2b = np.concatenate([np.asarray(r4[2 * b]["x2_o"]),
                              np.asarray(r4[2 * b + 1]["x2_o"])], axis=0)
        h0 = 8 * hg
        maps5.append(dict(shared5,
                          x_in=np.ascontiguousarray(x2b[srows[hg]]),
                          betas=np.ascontiguousarray(betas[h0:h0 + 8].T),
                          mem0z=np.ascontiguousarray(
                              m0z[h0:h0 + 8].transpose(1, 0, 2).reshape(64, 8 * 65))))
    r5 = run_bass_kernel_spmd(nc5, maps5, core_ids=list(range(8))).results

    nc6 = build_l1ffn()
    shared6 = dict(
        w1T=np.ascontiguousarray(np.asarray(inputs["L1_ffn_w1"], f32).transpose(0, 2, 1)),
        w3T=np.ascontiguousarray(np.asarray(inputs["L1_ffn_w3"], f32).transpose(0, 2, 1)),
        w2T=np.ascontiguousarray(np.asarray(inputs["L1_ffn_w2"], f32).transpose(0, 2, 1)),
        gwT=np.ascontiguousarray(np.asarray(inputs["L1_ffn_gate_w"], f32).T),
        ident=_IDENT)
    maps6 = []
    for c in range(8):
        b, hf = c // 2, c % 2
        x3b = np.empty((4096, C), f32)
        x3b[srows[0]] = np.asarray(r5[2 * b]["x3_o"])
        x3b[srows[1]] = np.asarray(r5[2 * b + 1]["x3_o"])
        maps6.append(dict(shared6, x3=np.ascontiguousarray(x3b[2048 * hf:2048 * (hf + 1)])))
    r6 = run_bass_kernel_spmd(nc6, maps6, core_ids=list(range(8))).results

    out = np.empty((4, 4096, C), f32)
    for c in range(8):
        b, hf = c // 2, c % 2
        out[b, 2048 * hf:2048 * (hf + 1)] = np.asarray(r6[c]["out_o"])
    return out
